# version 20 (final)
# baseline (speedup 1.0000x reference)
import numpy as np
from contextlib import ExitStack

import concourse.bass as bass
import concourse.mybir as mybir
from concourse.bass_utils import run_bass_kernel_spmd

F32 = mybir.dt.float32
BF16 = mybir.dt.bfloat16
I32 = mybir.dt.int32
AF = mybir.ActivationFunctionType
ALU = mybir.AluOpType

D = 1024
TB = 2048
NB = 2
TOK = NB * TB
CH = 128
GRP = 512
NG = TB // GRP
NCT = D // 128
EPS = 1e-5
QSC = float(128 ** -0.5)


class SemCtr:
    def __init__(self, stack, nc, name):
        self.h = stack.enter_context(nc.semaphore(name))
        self.n = 0

    def inc(self, instr, k=1):
        instr.then_inc(self.h, k)
        self.n += k
        return self.n

    def wait(self, eng, val=None):
        eng.wait_ge(self.h, self.n if val is None else val)

    def fence(self, eng, instr):
        self.inc(instr)
        eng.wait_ge(self.h, self.n)
        return instr


class _NamePrefix:

    def __init__(self, nc, pfx):
        self._nc = nc
        self._pfx = pfx

    def sbuf_tensor(self, name, *a, **k):
        return self._nc.sbuf_tensor(self._pfx + name, *a, **k)

    def psum_tensor(self, name, *a, **k):
        return self._nc.psum_tensor(self._pfx + name, *a, **k)

    def __getattr__(self, x):
        return getattr(self._nc, x)


def mm_chain(t, out_ap, pairs, sem=None, start=True, stop=True):
    last = None
    n = len(pairs)
    for i, (l, r) in enumerate(pairs):
        last = t.matmul(out_ap, l, r,
                        start=(start and i == 0), stop=(stop and i == n - 1))
    if sem is not None:
        sem.inc(last)
    return last



GLA_IN = [("xT", [D, TOK]), ("wq", [D, 512]), ("wk", [D, 512]), ("wv", [D, D]),
          ("gw1", [D, 16]), ("gw2", [16, 512]), ("gb2", [1, 512]), ("wg", [D, D]),
          ("gnw", [2, 128, 1]), ("wo", [D, D])]


def make_sems(nc, st):
    return {n: SemCtr(st, nc, n)
            for n in ["gs", "gc", "ws", "wh", "ys", "ts", "vs", "ss", "cc"]}


def build_gla():
    nc = bass.Bass("TRN2", target_bir_lowering=False, detect_race_conditions=False)
    io = {n: nc.dram_tensor(n, sh, F32, kind="ExternalInput").ap() for n, sh in GLA_IN}
    io["x1T"] = nc.dram_tensor("x1T", [D, TOK], F32, kind="ExternalOutput").ap()
    with ExitStack() as st0:
        _stage_gla(nc, make_sems(nc, st0), io)
    return nc


def _stage_gla(nc, S, io):
    xT, wq, wk, wv, gw1, gw2, gb2, wg, gnw, wo, x1T = (
        io["xT"], io["wq"], io["wk"], io["wv"], io["gw1"], io["gw2"], io["gb2"],
        io["wg"], io["gnw"], io["wo"], io["x1T"])
    gs, gc, ws, ys, ts, vs, ss = (S["gs"], S["gc"], S["ws"], S["ys"], S["ts"],
                                  S["vs"], S["ss"])

    nc = _NamePrefix(nc, "ga_")
    with ExitStack() as st:
        e = st.enter_context
        sWq = [e(nc.sbuf_tensor(f"sWq{i}", [128, 512], BF16)) for i in range(NCT)]
        sWk = [e(nc.sbuf_tensor(f"sWk{i}", [128, 512], BF16)) for i in range(NCT)]
        sWv = [e(nc.sbuf_tensor(f"sWv{i}", [128, D], BF16)) for i in range(NCT)]
        sWg = [e(nc.sbuf_tensor(f"sWg{i}", [128, D], BF16)) for i in range(NCT)]
        sWo = [e(nc.sbuf_tensor(f"sWo{i}", [128, D], BF16)) for i in range(NCT)]
        sGw1 = [e(nc.sbuf_tensor(f"sGw1{i}", [128, 16], BF16)) for i in range(NCT)]
        sGw2 = e(nc.sbuf_tensor("sGw2", [16, 512], BF16))
        siota = e(nc.sbuf_tensor("siota", [128, 512], F32))
        TRI_INCL = e(nc.sbuf_tensor("TRI_INCL", [128, 128], BF16))
        TRI_GT = e(nc.sbuf_tensor("TRI_GT", [128, 128], BF16))
        MASK4 = e(nc.sbuf_tensor("MASK4", [128, 512], BF16))
        ONES16 = e(nc.sbuf_tensor("ONES16", [128, 1], BF16))
        ONES256 = e(nc.sbuf_tensor("ONES256", [128, 1], BF16))
        ONES1024 = e(nc.sbuf_tensor("ONES1024", [128, 1], BF16))
        ONESROWF = e(nc.sbuf_tensor("ONESROWF", [1, 128], F32))
        sEPS = e(nc.sbuf_tensor("sEPS", [128, 1], F32))
        GNW = e(nc.sbuf_tensor("GNW", [128, 2], F32))
        sgb2row = e(nc.sbuf_tensor("sgb2row", [1, 512], F32))
        GB2T = e(nc.sbuf_tensor("GB2T", [128, 512], F32))
        sX = [e(nc.sbuf_tensor(f"sX{i}", [128, TB], BF16)) for i in range(NCT)]
        sS = e(nc.sbuf_tensor("sS", [128, 1024], F32))
        sS_bf = e(nc.sbuf_tensor("sS_bf", [128, 1024], BF16))
        sq = e(nc.sbuf_tensor("sq", [128, 4 * GRP], BF16))
        sk = e(nc.sbuf_tensor("sk", [128, 4 * GRP], BF16))
        ssg = e(nc.sbuf_tensor("ssg", [128, 8 * GRP], BF16))
        sof = e(nc.sbuf_tensor("sof", [128, 8 * GRP], BF16))
        sz1 = e(nc.sbuf_tensor("sz1", [16, GRP], BF16))
        szb = e(nc.sbuf_tensor("szb", [128, 512], F32))
        stA = e(nc.sbuf_tensor("stA", [128, 512], F32))
        stB = e(nc.sbuf_tensor("stB", [128, 512], F32))
        stC = e(nc.sbuf_tensor("stC", [128, 512], BF16))
        sgk_bf = e(nc.sbuf_tensor("sgk_bf", [128, 512], BF16))
        sexpA = e(nc.sbuf_tensor("sexpA", [128, 512], BF16))
        sexpNegA = e(nc.sbuf_tensor("sexpNegA", [128, 512], BF16))
        sexpT = e(nc.sbuf_tensor("sexpT", [128, 512], BF16))
        sk2 = e(nc.sbuf_tensor("sk2", [128, 512], BF16))
        sv_tm = e(nc.sbuf_tensor("sv_tm", [128, 1024], BF16))
        sQp = e(nc.sbuf_tensor("sQp", [128, 512], BF16))
        sKp = e(nc.sbuf_tensor("sKp", [128, 512], BF16))
        sPm = e(nc.sbuf_tensor("sPm", [128, 512], BF16))
        so2 = e(nc.sbuf_tensor("so2", [128, 1024], BF16))
        sAL = e(nc.sbuf_tensor("sAL", [128, 4], F32))
        srstd = e(nc.sbuf_tensor("srstd", [1, 512], F32))
        sln1 = e(nc.sbuf_tensor("sln1", [1, 512], F32))
        sogate = e(nc.sbuf_tensor("sogate", [128, 1024], BF16))
        sxf = [e(nc.sbuf_tensor(f"sxf{i}", [128, 512], F32)) for i in range(2)]
        sy = e(nc.sbuf_tensor("sy", [128, 8 * 512], F32))
        sy_bf = e(nc.sbuf_tensor("sy_bf", [128, 512], BF16))
        sy2 = e(nc.sbuf_tensor("sy2", [128, 512], BF16))
        s_m = e(nc.sbuf_tensor("s_m", [1, 512], F32))
        s_t1 = e(nc.sbuf_tensor("s_t1", [1, 512], F32))
        s_var = e(nc.sbuf_tensor("s_var", [1, 512], F32))
        s_r = e(nc.sbuf_tensor("s_r", [1, 512], F32))
        s_bt = e(nc.sbuf_tensor("s_bt", [1, 512], F32))
        sxo = e(nc.sbuf_tensor("sxo", [128, 512], F32))
        sxout = [e(nc.sbuf_tensor(f"sxout{i}", [128, 512], F32)) for i in range(2)]
        pb = [e(nc.psum_tensor(f"pb{i}", [128, 512], F32)) for i in range(8)]

        block = e(nc.Block())

        @block.gpsimd
        def _(g):
            for i in range(NCT):
                gs.inc(g.dma_start(out=sWq[i][:], in_=wq[i * 128:(i + 1) * 128, :]), 16)
                gs.inc(g.dma_start(out=sWk[i][:], in_=wk[i * 128:(i + 1) * 128, :]), 16)
                gs.inc(g.dma_start(out=sWv[i][:], in_=wv[i * 128:(i + 1) * 128, :]), 16)
                gs.inc(g.dma_start(out=sWg[i][:], in_=wg[i * 128:(i + 1) * 128, :]), 16)
                gs.inc(g.dma_start(out=sWo[i][:], in_=wo[i * 128:(i + 1) * 128, :]), 16)
                gs.inc(g.dma_start(out=sGw1[i][:], in_=gw1[i * 128:(i + 1) * 128, :]), 16)
            gs.inc(g.dma_start(out=sGw2[:], in_=gw2), 16)
            gc.inc(g.iota(siota[:], [[0, 4], [1, 128]], base=0, channel_multiplier=-1,
                          allow_small_or_imprecise_dtypes=True), 1)

        @block.sync
        def _(sy_):
            ys.inc(sy_.dma_start(out=sgb2row[:], in_=gb2), 16)
            ys.inc(sy_.dma_start(out=GNW[:, 0:1], in_=gnw[0]), 16)
            ys.inc(sy_.dma_start(out=GNW[:, 1:2], in_=gnw[1]), 16)

        @block.vector
        def _(v):
            gs.wait(v)
            gc.wait(v)
            ys.wait(v)
            v.tensor_scalar(TRI_INCL[:], siota[:, 0:128], 0.0, 1.0 / 16, ALU.is_ge, ALU.mult)
            v.tensor_scalar(TRI_GT[:], siota[:, 0:128], 0.0, 1.0 / 16, ALU.is_lt, ALU.mult)
            v.tensor_scalar(MASK4[:], siota[:], 0.0, None, ALU.is_ge)
            v.memset(ONES16[:], 1.0 / 16)
            v.memset(ONES256[:], 1.0 / 256)
            v.memset(ONES1024[:], 1.0 / 1024)
            v.memset(sEPS[:], EPS)
            vs.inc(v.memset(ONESROWF[:], 1.0))

        @block.tensor
        def _(t):
            vs.wait(t)
            ts.inc(t.matmul(pb[0][:], ONESROWF[:], sgb2row[:], start=True, stop=True))

        @block.vector
        def _(v):
            ts.wait(v)
            vs.inc(v.tensor_copy(GB2T[:], pb[0][:]))

        init_vs = vs.n

        for b in range(NB):
            @block.gpsimd
            def _(g, b=b):
                if b > 0:
                    ts.wait(g)
                for i in range(NCT):
                    gs.inc(g.dma_start(
                        out=sX[i][:], in_=xT[i * 128:(i + 1) * 128, b * TB:(b + 1) * TB]), 16)
                gc.inc(g.memset(sS[:], 0.0), 1)

            xb_gs = gs.n
            xb_gc = gc.n

            for grp in range(NG):
                t0 = grp * GRP

                @block.tensor
                def _(t, t0=t0, b=b, grp=grp):
                    t.wait_ge(gs.h, xb_gs)
                    t.wait_ge(gc.h, xb_gc)
                    if b == 0 and grp == 0:
                        t.wait_ge(vs.h, init_vs)
                    else:
                        vs.wait(t)
                    mm_chain(t, pb[6][0:16, :],
                             [(sGw1[ci][:], sX[ci][:, t0:t0 + GRP]) for ci in range(NCT)],
                             ts)

                @block.vector
                def _(v):
                    ts.wait(v)
                    vs.inc(v.tensor_copy(sz1[:], pb[6][0:16, :]))

                sqk = {}
                for jt in range(4):
                    @block.tensor
                    def _(t, jt=jt, t0=t0):
                        pv = sqk.get(jt - 2)
                        if pv:
                            t.wait_ge(ss.h, pv)
                        mm_chain(t, pb[jt % 2][:],
                                 [(sWq[ci][:, jt * 128:(jt + 1) * 128],
                                   sX[ci][:, t0:t0 + GRP]) for ci in range(NCT)], ts)
                        mm_chain(t, pb[2 + jt % 2][:],
                                 [(sWk[ci][:, jt * 128:(jt + 1) * 128],
                                   sX[ci][:, t0:t0 + GRP]) for ci in range(NCT)], ts)

                    @block.scalar
                    def _(s, jt=jt):
                        ts.wait(s)
                        s.activation(sq[:, jt * GRP:(jt + 1) * GRP], pb[jt % 2][:],
                                     AF.Copy, scale=QSC)
                        ss.inc(s.copy(sk[:, jt * GRP:(jt + 1) * GRP], pb[2 + jt % 2][:]))
                    sqk[jt] = ss.n

                sgv = {}
                for jt in range(8):
                    @block.tensor
                    def _(t, jt=jt, t0=t0):
                        pv = sgv.get(jt - 2)
                        if pv:
                            t.wait_ge(vs.h, pv)
                        mm_chain(t, pb[4 + jt % 2][:],
                                 [(sWg[ci][:, jt * 128:(jt + 1) * 128],
                                   sX[ci][:, t0:t0 + GRP]) for ci in range(NCT)], ts)

                    @block.scalar
                    def _(s, jt=jt):
                        ts.wait(s)
                        vs.wait(s)
                        ss.inc(s.activation(stC[:], pb[4 + jt % 2][:], AF.Tanh, scale=0.5))

                    @block.vector
                    def _(v, jt=jt):
                        ss.wait(v)
                        v.tensor_scalar(stB[:], stC[:], 0.5, 0.5, ALU.mult, ALU.add)
                        vs.inc(v.tensor_tensor(ssg[:, jt * GRP:(jt + 1) * GRP],
                                               stB[:], pb[4 + jt % 2][:], ALU.mult))
                    sgv[jt] = vs.n

                for c in range(4):
                    tc = t0 + c * CH

                    @block.tensor
                    def _(t, c=c, tc=tc):
                        vs.wait(t)
                        ts.inc(t.matmul(pb[0][:], sz1[:, c * CH:(c + 1) * CH], sGw2[:],
                                        start=True, stop=True))

                    @block.vector
                    def _(v):
                        ts.wait(v)
                        vs.inc(v.tensor_tensor(szb[:], pb[0][:], GB2T[:], ALU.add))

                    @block.scalar
                    def _(s):
                        vs.wait(s)
                        s.activation(stA[:], szb[:], AF.Abs)
                        s.activation(stB[:], stA[:], AF.Exp, scale=-1.0)
                        ss.inc(s.activation(stA[:], stB[:], AF.Ln, bias=1.0))

                    @block.vector
                    def _(v):
                        ss.wait(v)
                        v.tensor_scalar_min(stB[:], szb[:], 0.0)
                        v.tensor_tensor(stB[:], stB[:], stA[:], ALU.subtract)
                        vs.inc(v.tensor_copy(sgk_bf[:], stB[:]))

                    @block.tensor
                    def _(t):
                        vs.wait(t)
                        for h in range(4):
                            t.matmul(pb[1][:, h * 128:(h + 1) * 128],
                                     sgk_bf[:, h * 128:(h + 1) * 128], TRI_INCL[:],
                                     start=True, stop=True)
                            t.matmul(pb[2][:, h:h + 1],
                                     sgk_bf[:, h * 128:(h + 1) * 128], ONES16[:],
                                     start=True, stop=True)
                        ts.inc(t.matmul(pb[0][:], TRI_GT[:], sgk_bf[:],
                                        start=True, stop=True))

                    @block.scalar
                    def _(s):
                        ts.wait(s)
                        s.activation(sexpA[:], pb[1][:], AF.Exp)
                        s.activation(sexpNegA[:], pb[1][:], AF.Exp, scale=-1.0)
                        s.activation(sexpT[:], pb[0][:], AF.Exp)
                        ss.inc(s.activation(sAL[:], pb[2][:, 0:4], AF.Exp))

                    @block.tensor
                    def _(t, tc=tc):
                        ss.wait(t)
                        mm_chain(t, pb[0][:],
                                 [(sX[ci][:, tc:tc + CH], sWk[ci][:]) for ci in range(NCT)])
                        mm_chain(t, pb[2][:],
                                 [(sX[ci][:, tc:tc + CH], sWv[ci][:, 0:512])
                                  for ci in range(NCT)])
                        mm_chain(t, pb[3][:],
                                 [(sX[ci][:, tc:tc + CH], sWv[ci][:, 512:1024])
                                  for ci in range(NCT)], ts)

                    @block.vector
                    def _(v, c=c):
                        ts.wait(v)
                        v.tensor_tensor(sk2[:], pb[0][:], sexpT[:], ALU.mult)
                        v.tensor_copy(sv_tm[:, 0:512], pb[2][:])
                        v.tensor_copy(sv_tm[:, 512:1024], pb[3][:])
                        v.tensor_copy(sS_bf[:], sS[:])
                        for h in range(4):
                            sl = slice(h * 128, (h + 1) * 128)
                            v.tensor_tensor(sQp[:, sl], sq[:, h * GRP + c * CH:
                                                           h * GRP + (c + 1) * CH],
                                            sexpA[:, sl], ALU.mult)
                            v.tensor_tensor(sKp[:, sl], sk[:, h * GRP + c * CH:
                                                           h * GRP + (c + 1) * CH],
                                            sexpNegA[:, sl], ALU.mult)
                        vs.inc(v.engine_nop())

                    @block.tensor
                    def _(t):
                        vs.wait(t)
                        for h in range(4):
                            sl = slice(h * 128, (h + 1) * 128)
                            last = t.matmul(pb[1][:, sl], sKp[:, sl], sQp[:, sl],
                                            start=True, stop=True)
                        ts.inc(last)

                    @block.vector
                    def _(v):
                        ts.wait(v)
                        vs.inc(v.tensor_tensor(sPm[:], pb[1][:], MASK4[:], ALU.mult))

                    @block.tensor
                    def _(t):
                        vs.wait(t)
                        for h in range(4):
                            for vt in range(2):
                                slot = pb[4 + h // 2][:, (2 * (h % 2) + vt) * 128:
                                                      (2 * (h % 2) + vt + 1) * 128]
                                vsl = slice(h * 256 + vt * 128, h * 256 + (vt + 1) * 128)
                                t.matmul(slot, sS_bf[:, vsl],
                                         sQp[:, h * 128:(h + 1) * 128],
                                         start=True, stop=False)
                                last = t.matmul(slot, sv_tm[:, vsl],
                                                sPm[:, h * 128:(h + 1) * 128],
                                                start=False, stop=True)
                        for h in range(4):
                            slot = pb[6 + h // 2][:, (h % 2) * 256:(h % 2 + 1) * 256]
                            last = t.matmul(slot, sk2[:, h * 128:(h + 1) * 128],
                                            sv_tm[:, h * 256:(h + 1) * 256],
                                            start=True, stop=True)
                        ts.inc(last)

                    @block.vector
                    def _(v):
                        ts.wait(v)
                        for h in range(4):
                            slot = pb[6 + h // 2][:, (h % 2) * 256:(h % 2 + 1) * 256]
                            last = v.scalar_tensor_tensor(
                                sS[:, h * 256:(h + 1) * 256],
                                sS[:, h * 256:(h + 1) * 256], sAL[:, h:h + 1], slot,
                                ALU.mult, ALU.add)
                        vs.inc(last)

                    @block.scalar
                    def _(s):
                        ts.wait(s)
                        s.square(so2[:, 0:512], pb[4][:])
                        ss.inc(s.square(so2[:, 512:1024], pb[5][:]))

                    @block.tensor
                    def _(t):
                        ss.wait(t)
                        for h in range(4):
                            for vt in range(2):
                                last = t.matmul(
                                    pb[1][0:1, h * 128:(h + 1) * 128], ONES256[:],
                                    so2[:, (2 * h + vt) * 128:(2 * h + vt + 1) * 128],
                                    start=(vt == 0), stop=(vt == 1))
                        ts.inc(last)

                    @block.scalar
                    def _(s):
                        ts.wait(s)
                        s.activation(sln1[:], pb[1][0:1, 0:512], AF.Ln, bias=sEPS[0:1, 0:1])
                        ss.inc(s.activation(srstd[:], sln1[:], AF.Exp, scale=-0.5))

                    @block.tensor
                    def _(t):
                        ss.wait(t)
                        ts.inc(t.matmul(pb[0][:], ONESROWF[:], srstd[:],
                                        start=True, stop=True))

                    @block.vector
                    def _(v, c=c):
                        ts.wait(v)
                        for h in range(4):
                            for vt in range(2):
                                slot = pb[4 + h // 2][:, (2 * (h % 2) + vt) * 128:
                                                      (2 * (h % 2) + vt + 1) * 128]
                                osl = slice((2 * h + vt) * 128, (2 * h + vt + 1) * 128)
                                v.scalar_tensor_tensor(
                                    sogate[:, osl], slot, GNW[:, vt:vt + 1],
                                    ssg[:, (2 * h + vt) * GRP + c * CH:
                                         (2 * h + vt) * GRP + (c + 1) * CH],
                                    ALU.mult, ALU.mult)
                                last = v.tensor_tensor(
                                    sof[:, (2 * h + vt) * GRP + c * CH:
                                        (2 * h + vt) * GRP + (c + 1) * CH],
                                    sogate[:, osl], pb[0][:, h * 128:(h + 1) * 128],
                                    ALU.mult)
                        vs.inc(last)

                for ct in range(NCT):
                    @block.sync
                    def _(sy_, ct=ct, b=b, t0=t0, grp=grp):
                        if not (b == 0 and grp == 0 and ct < 2):
                            vs.wait(sy_)
                        ys.inc(sy_.dma_start(
                            out=sxf[ct % 2][:],
                            in_=xT[ct * 128:(ct + 1) * 128,
                                   b * TB + t0:b * TB + t0 + GRP]), 16)

                    @block.tensor
                    def _(t, ct=ct):
                        if ct == 0:
                            vs.wait(t)
                        if ct >= 2:
                            vs.wait(t)
                        mm_chain(t, pb[2 + ct % 2][:],
                                 [(sWo[vt][:, ct * 128:(ct + 1) * 128],
                                   sof[:, vt * GRP:(vt + 1) * GRP]) for vt in range(NCT)],
                                 ts)

                    @block.vector
                    def _(v, ct=ct):
                        ts.wait(v)
                        ys.wait(v)
                        vs.inc(v.tensor_tensor(sy[:, ct * GRP:(ct + 1) * GRP],
                                               pb[2 + ct % 2][:], sxf[ct % 2][:], ALU.add))

                    @block.vector
                    def _(v, ct=ct):
                        vs.inc(v.tensor_copy(sy_bf[:], sy[:, ct * GRP:(ct + 1) * GRP]))

                    @block.scalar
                    def _(s, ct=ct):
                        vs.wait(s)
                        ss.inc(s.square(sy2[:], sy_bf[:]))

                    @block.tensor
                    def _(t, ct=ct):
                        ss.wait(t)
                        t.matmul(pb[0][0:1, :], ONES1024[:], sy_bf[:],
                                 start=(ct == 0), stop=(ct == NCT - 1))
                        ts.inc(t.matmul(pb[1][0:1, :], ONES1024[:], sy2[:],
                                        start=(ct == 0), stop=(ct == NCT - 1)))

                @block.vector
                def _(v):
                    ts.wait(v)
                    vs.fence(v, v.tensor_copy(s_m[:], pb[0][0:1, :]))
                    vs.fence(v, v.tensor_tensor(s_t1[:], s_m[:], s_m[:], ALU.mult))
                    vs.inc(v.tensor_tensor(s_var[:], pb[1][0:1, :], s_t1[:], ALU.subtract))

                @block.scalar
                def _(s):
                    vs.wait(s)
                    s.activation(sln1[:], s_var[:], AF.Ln, bias=sEPS[0:1, 0:1])
                    ss.inc(s.activation(s_r[:], sln1[:], AF.Exp, scale=-0.5))

                @block.vector
                def _(v):
                    ss.wait(v)
                    vs.inc(v.scalar_tensor_tensor(s_bt[:], s_m[:], -1.0, s_r[:],
                                                  ALU.mult, ALU.mult))

                @block.tensor
                def _(t):
                    vs.wait(t)
                    t.matmul(pb[4][:], ONESROWF[:], s_r[:], start=True, stop=True)
                    ts.inc(t.matmul(pb[5][:], ONESROWF[:], s_bt[:], start=True, stop=True))

                for ct in range(NCT):
                    @block.vector
                    def _(v, ct=ct, b=b, grp=grp):
                        if ct == 0:
                            ts.wait(v)
                        if not (b == 0 and grp == 0 and ct < 2):
                            ys.wait(v)
                        v.tensor_tensor(sxo[:], sy[:, ct * GRP:(ct + 1) * GRP],
                                        pb[4][:], ALU.mult)
                        vs.inc(v.tensor_tensor(sxout[ct % 2][:], sxo[:], pb[5][:],
                                               ALU.add))

                    @block.sync
                    def _(sy_, ct=ct, b=b, t0=t0):
                        vs.wait(sy_)
                        ys.inc(sy_.dma_start(
                            out=x1T[ct * 128:(ct + 1) * 128,
                                    b * TB + t0:b * TB + t0 + GRP],
                            in_=sxout[ct % 2][:]), 16)

        @block.sync
        def _(sy_):
            ys.wait(sy_)


def _prep_gla_inputs(x_shard, ins):
    xT = np.ascontiguousarray(
        np.concatenate([x_shard[0].T, x_shard[1].T], axis=1)).astype(np.float32)
    return {
        "xT": xT,
        "wq": ins["L0_attn_Wq"], "wk": ins["L0_attn_Wk"], "wv": ins["L0_attn_Wv"],
        "gw1": ins["L0_attn_gk_w1"], "gw2": ins["L0_attn_gk_w2"],
        "gb2": ins["L0_attn_gk_b2"].reshape(1, 512),
        "wg": ins["L0_attn_Wg"],
        "gnw": ins["L0_attn_g_norm_w"].reshape(2, 128, 1),
        "wo": ins["L0_attn_Wo"],
    }





FFN0_IN = [("f0xT", [D, TOK]), ("wsi", [D, 2048]), ("wso", [D, D]),
           ("w1s", [D, 2048]), ("w2", [8, D, D]), ("wgt", [D, 8]), ("gbias", [1, 8])]


def build_ffn0(debug=False):
    nc = bass.Bass("TRN2", target_bir_lowering=False, detect_race_conditions=False)
    io = {n: nc.dram_tensor(n, sh, F32, kind="ExternalInput").ap() for n, sh in FFN0_IN}
    io["x2T"] = nc.dram_tensor("x2T", [D, TOK], F32, kind="ExternalOutput").ap()
    with ExitStack() as st0:
        _stage_ffn0(nc, make_sems(nc, st0), io, debug=debug)
    return nc


def _stage_ffn0(nc, S, io, debug=False):
    xT, wsi, wso, w1s, w2, wgt, gbias, x2T = (
        io["f0xT"], io["wsi"], io["wso"], io["w1s"], io["w2"], io["wgt"],
        io["gbias"], io["x2T"])
    gs, gc, ws, ys, ts, vs, ss = (S["gs"], S["gc"], S["ws"], S["ys"], S["ts"],
                                  S["vs"], S["ss"])
    if debug:
        d_ssh = nc.dram_tensor("d_ssh", [128, 8 * GRP], BF16, kind="ExternalOutput").ap()
        d_acc1 = nc.dram_tensor("d_acc1", [128, 8 * GRP], F32, kind="ExternalOutput").ap()
        d_hid = nc.dram_tensor("d_hid", [128, 8 * GRP], BF16, kind="ExternalOutput").ap()
        d_wdT = nc.dram_tensor("d_wdT", [8, GRP], F32, kind="ExternalOutput").ap()
        d_accF = nc.dram_tensor("d_accF", [128, 8 * GRP], F32, kind="ExternalOutput").ap()

    NGRP = TOK // GRP

    nc = _NamePrefix(nc, "f0_")
    with ExitStack() as st:
        e = st.enter_context
        sWsi = [e(nc.sbuf_tensor(f"sWsi{i}", [128, 2048], BF16)) for i in range(NCT)]
        sWso = [e(nc.sbuf_tensor(f"sWso{i}", [128, D], BF16)) for i in range(NCT)]
        sW1 = [e(nc.sbuf_tensor(f"sW1{i}", [128, 2048], BF16)) for i in range(NCT)]
        sWgt = [e(nc.sbuf_tensor(f"sWgt{i}", [128, 8], BF16)) for i in range(NCT)]
        sW2b = [e(nc.sbuf_tensor(f"sW2b{i}", [128, 8192], BF16)) for i in range(2)]
        sXg = [[e(nc.sbuf_tensor(f"sXg{p}_{i}", [128, GRP], BF16)) for i in range(NCT)]
               for p in range(2)]
        siota = e(nc.sbuf_tensor("siota", [128, 128], F32))
        siota8 = e(nc.sbuf_tensor("siota8", [8, 1024], F32))
        SEL8 = e(nc.sbuf_tensor("SEL8", [8, 1024], F32))
        IDENTF = e(nc.sbuf_tensor("IDENTF", [128, 128], F32))
        ONES1024 = e(nc.sbuf_tensor("ONES1024", [128, 1], BF16))
        ONESROWF = e(nc.sbuf_tensor("ONESROWF", [1, 128], F32))
        sEPS = e(nc.sbuf_tensor("sEPS", [128, 1], F32))
        sgbrow = e(nc.sbuf_tensor("sgbrow", [1, 8], F32))
        GBT = e(nc.sbuf_tensor("GBT", [128, 8], F32))
        s_sh = e(nc.sbuf_tensor("s_sh", [128, 8 * GRP], BF16))
        s_hid = s_sh
        s_hs = e(nc.sbuf_tensor("s_hs", [128, 8 * GRP], BF16))
        sacc = e(nc.sbuf_tensor("sacc", [128, 8 * GRP], F32))
        stC = e(nc.sbuf_tensor("stC", [128, 512], BF16))
        stB = e(nc.sbuf_tensor("stB", [128, 512], BF16))
        stS = e(nc.sbuf_tensor("stS", [128, 512], BF16))
        szb = e(nc.sbuf_tensor("szb", [128, 8], F32))
        sE = e(nc.sbuf_tensor("sE", [128, 8], F32))
        sm1 = e(nc.sbuf_tensor("sm1", [128, 1], F32))
        sE2 = e(nc.sbuf_tensor("sE2", [128, 8], F32))
        sm2 = e(nc.sbuf_tensor("sm2", [128, 1], F32))
        skeep = e(nc.sbuf_tensor("skeep", [128, 8], F32))
        swm = e(nc.sbuf_tensor("swm", [128, 8], F32))
        sden = e(nc.sbuf_tensor("sden", [128, 1], F32))
        srd = e(nc.sbuf_tensor("srd", [128, 1], F32))
        swd = e(nc.sbuf_tensor("swd", [128, 8], F32))
        swdT = e(nc.sbuf_tensor("swdT", [8, GRP], F32))
        smb = e(nc.sbuf_tensor("smb", [128, GRP], BF16))
        sxf = [e(nc.sbuf_tensor(f"sxf{i}", [128, 512], F32)) for i in range(2)]
        sy = sacc
        sy_bf = e(nc.sbuf_tensor("sy_bf", [128, 512], BF16))
        sy2 = e(nc.sbuf_tensor("sy2", [128, 512], BF16))
        s_m = e(nc.sbuf_tensor("s_m", [1, 512], F32))
        s_t1 = e(nc.sbuf_tensor("s_t1", [1, 512], F32))
        s_var = e(nc.sbuf_tensor("s_var", [1, 512], F32))
        s_r = e(nc.sbuf_tensor("s_r", [1, 512], F32))
        s_bt = e(nc.sbuf_tensor("s_bt", [1, 512], F32))
        sln1 = e(nc.sbuf_tensor("sln1", [1, 512], F32))
        sxo = e(nc.sbuf_tensor("sxo", [128, 512], F32))
        sxout = [e(nc.sbuf_tensor(f"sxout{i}", [128, 512], F32)) for i in range(2)]
        pb = [e(nc.psum_tensor(f"pb{i}", [128, 512], F32)) for i in range(8)]

        block = e(nc.Block())

        @block.gpsimd
        def _(g):
            for i in range(NCT):
                gs.inc(g.dma_start(out=sWsi[i][:], in_=wsi[i * 128:(i + 1) * 128, :]), 16)
                gs.inc(g.dma_start(out=sWso[i][:], in_=wso[i * 128:(i + 1) * 128, :]), 16)
                gs.inc(g.dma_start(out=sW1[i][:], in_=w1s[i * 128:(i + 1) * 128, :]), 16)
                gs.inc(g.dma_start(out=sWgt[i][:], in_=wgt[i * 128:(i + 1) * 128, :]), 16)
            gc.inc(g.iota(siota[:], [[1, 128]], base=0, channel_multiplier=-1,
                          allow_small_or_imprecise_dtypes=True), 1)
            gc.inc(g.iota(siota8[:], [[1, 8], [0, 128]], base=0, channel_multiplier=-1,
                          allow_small_or_imprecise_dtypes=True), 1)
            ys.wait(g)
            vs.wait(g)
            ss.wait(g)
            for i in range(NCT):
                gs.inc(g.dma_start(out=sXg[0][i][:], in_=xT[i * 128:(i + 1) * 128, 0:GRP]), 16)

        @block.sync
        def _(sy_):
            ys.wait(sy_)
            vs.wait(sy_)
            ys.inc(sy_.dma_start(out=sgbrow[:], in_=gbias), 16)

        @block.vector
        def _(v):
            gs.wait(v)
            gc.wait(v)
            ys.wait(v)
            v.tensor_scalar(IDENTF[:], siota[:], 0.0, None, ALU.is_equal)
            v.tensor_scalar(SEL8[:], siota8[:], 0.0, None, ALU.is_equal)
            v.memset(ONES1024[:], 1.0 / 1024)
            v.memset(sEPS[:], EPS)
            vs.inc(v.memset(ONESROWF[:], 1.0))

        @block.tensor
        def _(t):
            vs.wait(t)
            ts.inc(t.matmul(pb[0][:, 0:8], ONESROWF[:], sgbrow[:], start=True, stop=True))

        @block.vector
        def _(v):
            ts.wait(v)
            vs.inc(v.tensor_copy(GBT[:], pb[0][:, 0:8]))

        @block.gpsimd
        def _(g):
            ws.inc(g.dma_start(out=sW2b[0][:].rearrange("p (a f) -> p a f", a=8),
                               in_=w2[0].rearrange("(a p) f -> p a f", p=128)), 16)

        for grp in range(NGRP):
            t0 = grp * GRP
            cur = grp % 2

            if grp + 1 < NGRP:
                @block.gpsimd
                def _(g, grp=grp):
                    if grp >= 1:
                        ts.wait(g)
                    for i in range(NCT):
                        gs.inc(g.dma_start(
                            out=sXg[(grp + 1) % 2][i][:],
                            in_=xT[i * 128:(i + 1) * 128,
                                   (grp + 1) * GRP:(grp + 2) * GRP]), 16)

            xg = sXg[cur]
            xg_gs = gs.n if grp + 1 >= NGRP else gs.n

            vsw = {}
            for which, wmat, sdst in ((0, sWsi, s_sh),):
                for jt in range(8):
                    @block.tensor
                    def _(t, jt=jt, wmat=wmat, grp=grp, which=which):
                        if grp == 0 and which == 0 and jt == 0:
                            gs.wait(t)
                            vs.wait(t)
                        elif which == 0 and jt == 0 and grp > 0:
                            vs.wait(t)
                        k = which * 8 + jt
                        pv = vsw.get(k - 2)
                        if pv:
                            t.wait_ge(vs.h, pv)
                        mm_chain(t, pb[jt % 2][:],
                                 [(wmat[ci][:, jt * 128:(jt + 1) * 128], xg[ci][:])
                                  for ci in range(NCT)])
                        mm_chain(t, pb[2 + jt % 2][:],
                                 [(wmat[ci][:, 1024 + jt * 128:1024 + (jt + 1) * 128],
                                   xg[ci][:]) for ci in range(NCT)], ts)

                    @block.scalar
                    def _(s, jt=jt):
                        ts.wait(s)
                        vs.wait(s)
                        ss.inc(s.activation(stC[:], pb[jt % 2][:], AF.Tanh, scale=0.5))

                    @block.vector
                    def _(v, jt=jt, sdst=sdst):
                        ss.wait(v)
                        v.tensor_scalar(stB[:], stC[:], 0.5, 0.5, ALU.mult, ALU.add)
                        v.tensor_tensor(stS[:], stB[:], pb[jt % 2][:], ALU.mult)
                        vs.inc(v.tensor_tensor(sdst[:, jt * GRP:(jt + 1) * GRP],
                                               stS[:], pb[2 + jt % 2][:], ALU.mult))
                    vsw[which * 8 + jt] = vs.n

            if debug and grp == 0:
                @block.sync
                def _(sy_):
                    vs.wait(sy_)
                    ys.inc(sy_.dma_start(out=d_ssh, in_=s_sh[:]), 16)

            vso = {}
            for ct in range(NCT):
                @block.tensor
                def _(t, ct=ct):
                    if ct == 0:
                        vs.wait(t)
                    pv = vso.get(ct - 2)
                    if pv:
                        t.wait_ge(vs.h, pv)
                    mm_chain(t, pb[4 + ct % 2][:],
                             [(sWso[jt][:, ct * 128:(ct + 1) * 128],
                               s_sh[:, jt * GRP:(jt + 1) * GRP]) for jt in range(NCT)],
                             ts)

                @block.vector
                def _(v, ct=ct):
                    ts.wait(v)
                    vs.inc(v.tensor_copy(sacc[:, ct * GRP:(ct + 1) * GRP],
                                         pb[4 + ct % 2][:]))
                vso[ct] = vs.n

            if debug and grp == 0:
                @block.sync
                def _(sy_):
                    vs.wait(sy_)
                    ys.inc(sy_.dma_start(out=d_acc1, in_=sacc[:]), 16)

                @block.tensor
                def _(t):
                    ys.wait(t)

            for which, wmat, sdst in ((1, sW1, s_hid),):
                for jt in range(8):
                    @block.tensor
                    def _(t, jt=jt, wmat=wmat):
                        k = which * 8 + jt
                        pv = vsw.get(k - 2)
                        if pv:
                            t.wait_ge(vs.h, pv)
                        if jt == 0:
                            vs.wait(t)
                        mm_chain(t, pb[jt % 2][:],
                                 [(wmat[ci][:, jt * 128:(jt + 1) * 128], xg[ci][:])
                                  for ci in range(NCT)])
                        mm_chain(t, pb[2 + jt % 2][:],
                                 [(wmat[ci][:, 1024 + jt * 128:1024 + (jt + 1) * 128],
                                   xg[ci][:]) for ci in range(NCT)], ts)

                    @block.scalar
                    def _(s, jt=jt):
                        ts.wait(s)
                        vs.wait(s)
                        ss.inc(s.activation(stC[:], pb[jt % 2][:], AF.Tanh, scale=0.5))

                    @block.vector
                    def _(v, jt=jt, sdst=sdst):
                        ss.wait(v)
                        v.tensor_scalar(stB[:], stC[:], 0.5, 0.5, ALU.mult, ALU.add)
                        v.tensor_tensor(stS[:], stB[:], pb[jt % 2][:], ALU.mult)
                        vs.inc(v.tensor_tensor(sdst[:, jt * GRP:(jt + 1) * GRP],
                                               stS[:], pb[2 + jt % 2][:], ALU.mult))
                    vsw[which * 8 + jt] = vs.n

            if debug and grp == 0:
                @block.sync
                def _(sy_):
                    vs.wait(sy_)
                    ys.inc(sy_.dma_start(out=d_hid, in_=s_hid[:]), 16)

            for c in range(4):
                @block.tensor
                def _(t, c=c):
                    vs.wait(t)
                    mm_chain(t, pb[6][:, 0:8],
                             [(xg[ci][:, c * 128:(c + 1) * 128], sWgt[ci][:])
                              for ci in range(NCT)], ts)

                @block.vector
                def _(v):
                    ts.wait(v)
                    vs.inc(v.tensor_tensor(szb[:], pb[6][:, 0:8], GBT[:], ALU.add))

                @block.scalar
                def _(s):
                    vs.wait(s)
                    ss.inc(s.activation(sE[:], szb[:], AF.Exp))

                @block.vector
                def _(v):
                    ss.wait(v)
                    vs.fence(v, v.reduce_max(sm1[:], sE[:], axis=mybir.AxisListType.X))
                    vs.fence(v, v.tensor_scalar(sE2[:], sE[:], sm1[:, 0:1], None, ALU.is_equal))
                    vs.fence(v, v.scalar_tensor_tensor(sE2[:], sE2[:], -1e30, sE[:], ALU.mult, ALU.add))
                    vs.fence(v, v.reduce_max(sm2[:], sE2[:], axis=mybir.AxisListType.X))
                    vs.fence(v, v.tensor_scalar(skeep[:], sE[:], sm2[:, 0:1], None, ALU.is_ge))
                    vs.fence(v, v.tensor_tensor(swm[:], sE[:], skeep[:], ALU.mult))
                    vs.fence(v, v.reduce_sum(sden[:], sE[:], axis=mybir.AxisListType.X))
                    vs.fence(v, v.reciprocal(srd[:], sden[:]))
                    vs.inc(v.tensor_scalar(swd[:], swm[:], srd[:, 0:1], None, ALU.mult))

                @block.tensor
                def _(t, c=c):
                    vs.wait(t)
                    ts.inc(t.transpose(pb[7][0:8, 0:128], swd[:], IDENTF[:]))

                @block.vector
                def _(v, c=c):
                    ts.wait(v)
                    vs.inc(v.tensor_copy(swdT[:, c * 128:(c + 1) * 128], pb[7][0:8, 0:128]))

            if debug and grp == 0:
                @block.sync
                def _(sy_):
                    vs.wait(sy_)
                    ys.inc(sy_.dma_start(out=d_wdT, in_=swdT[:]), 16)

            vex = {}
            for ex in range(8):
                nxt_g, nxt_e = (grp, ex + 1) if ex + 1 < 8 else (grp + 1, 0)
                if nxt_g < NGRP:
                    @block.gpsimd
                    def _(g, nxt_g=nxt_g, nxt_e=nxt_e, ex=ex):
                        ts.wait(g)
                        ws.inc(g.dma_start(
                            out=sW2b[(ex + 1) % 2][:].rearrange("p (a f) -> p a f", a=8),
                            in_=w2[nxt_e].rearrange("(a p) f -> p a f", p=128)), 16)

                @block.tensor
                def _(t, ex=ex):
                    vs.wait(t)
                    ts.inc(t.matmul(pb[6][:], SEL8[:, ex * 128:(ex + 1) * 128],
                                    swdT[:], start=True, stop=True))

                @block.vector
                def _(v):
                    ts.wait(v)
                    vs.inc(v.tensor_copy(smb[:], pb[6][:]))

                @block.vector
                def _(v):
                    for jt in range(NCT):
                        last = v.tensor_tensor(s_hs[:, jt * GRP:(jt + 1) * GRP],
                                               s_hid[:, jt * GRP:(jt + 1) * GRP],
                                               smb[:], ALU.mult)
                    vs.inc(last)

                for ct in range(NCT):
                    @block.tensor
                    def _(t, ct=ct, ex=ex):
                        if ct == 0:
                            vs.wait(t)
                            ws.wait(t)
                        pv = vex.get((ex, ct - 2), vex.get((ex - 1, NCT - 2 + ct)))
                        if pv:
                            t.wait_ge(vs.h, pv)
                        mm_chain(t, pb[4 + ct % 2][:],
                                 [(sW2b[ex % 2][:, jt * 1024 + ct * 128:
                                                jt * 1024 + (ct + 1) * 128],
                                   s_hs[:, jt * GRP:(jt + 1) * GRP])
                                  for jt in range(NCT)], ts)

                    @block.vector
                    def _(v, ct=ct):
                        ts.wait(v)
                        vs.inc(v.tensor_tensor(sacc[:, ct * GRP:(ct + 1) * GRP],
                                               sacc[:, ct * GRP:(ct + 1) * GRP],
                                               pb[4 + ct % 2][:], ALU.add))
                    vex[(ex, ct)] = vs.n

            if debug and grp == 0:
                @block.sync
                def _(sy_):
                    vs.wait(sy_)
                    ys.inc(sy_.dma_start(out=d_accF, in_=sacc[:]), 16)

                @block.vector
                def _(v):
                    ys.wait(v)

            for ct in range(NCT):
                @block.sync
                def _(sy_, ct=ct, t0=t0, grp=grp):
                    if not (grp == 0 and ct < 2):
                        vs.wait(sy_)
                    ys.inc(sy_.dma_start(
                        out=sxf[ct % 2][:],
                        in_=xT[ct * 128:(ct + 1) * 128, t0:t0 + GRP]), 16)

                @block.vector
                def _(v, ct=ct):
                    ys.wait(v)
                    v.tensor_tensor(sy[:, ct * GRP:(ct + 1) * GRP],
                                    sy[:, ct * GRP:(ct + 1) * GRP],
                                    sxf[ct % 2][:], ALU.add)
                    vs.inc(v.tensor_copy(sy_bf[:], sy[:, ct * GRP:(ct + 1) * GRP]))

                @block.scalar
                def _(s, ct=ct):
                    vs.wait(s)
                    ss.inc(s.square(sy2[:], sy_bf[:]))

                @block.tensor
                def _(t, ct=ct):
                    ss.wait(t)
                    t.matmul(pb[0][0:1, :], ONES1024[:], sy_bf[:],
                             start=(ct == 0), stop=(ct == NCT - 1))
                    ts.inc(t.matmul(pb[1][0:1, :], ONES1024[:], sy2[:],
                                    start=(ct == 0), stop=(ct == NCT - 1)))

            @block.vector
            def _(v):
                ts.wait(v)
                vs.fence(v, v.tensor_copy(s_m[:], pb[0][0:1, :]))
                vs.fence(v, v.tensor_tensor(s_t1[:], s_m[:], s_m[:], ALU.mult))
                vs.inc(v.tensor_tensor(s_var[:], pb[1][0:1, :], s_t1[:], ALU.subtract))

            @block.scalar
            def _(s):
                vs.wait(s)
                s.activation(sln1[:], s_var[:], AF.Ln, bias=sEPS[0:1, 0:1])
                ss.inc(s.activation(s_r[:], sln1[:], AF.Exp, scale=-0.5))

            @block.vector
            def _(v):
                ss.wait(v)
                vs.inc(v.scalar_tensor_tensor(s_bt[:], s_m[:], -1.0, s_r[:],
                                              ALU.mult, ALU.mult))

            @block.tensor
            def _(t):
                vs.wait(t)
                t.matmul(pb[2][:], ONESROWF[:], s_r[:], start=True, stop=True)
                ts.inc(t.matmul(pb[3][:], ONESROWF[:], s_bt[:], start=True, stop=True))

            for ct in range(NCT):
                @block.vector
                def _(v, ct=ct, grp=grp):
                    if ct == 0:
                        ts.wait(v)
                    if not (grp == 0 and ct < 2):
                        ys.wait(v)
                    v.tensor_tensor(sxo[:], sy[:, ct * GRP:(ct + 1) * GRP],
                                    pb[2][:], ALU.mult)
                    vs.inc(v.tensor_tensor(sxout[ct % 2][:], sxo[:], pb[3][:], ALU.add))

                @block.sync
                def _(sy_, ct=ct, t0=t0):
                    vs.wait(sy_)
                    ys.inc(sy_.dma_start(
                        out=x2T[ct * 128:(ct + 1) * 128, t0:t0 + GRP],
                        in_=sxout[ct % 2][:]), 16)

        @block.sync
        def _(sy_):
            ys.wait(sy_)


def _prep_ffn0_inputs(x1T_core, ins):
    return {
        "f0xT": x1T_core,
        "wsi": ins["L0_ffn_w_shared_in"], "wso": ins["L0_ffn_w_shared_out"],
        "w1s": ins["L0_ffn_w1_shared"], "w2": ins["L0_ffn_w2"],
        "wgt": ins["L0_ffn_w_gate"], "gbias": ins["L0_ffn_gate_bias"].reshape(1, 8),
    }





MHA_IN = [("mxT", [D, TOK]), ("qkvw", [D, 3072]), ("qkvb", [24, 128, 1]),
          ("qkvbrow", [1, 1024]), ("outw", [D, D]), ("outb", [8, 128, 1]),
          ("projw", [8, D]), ("projb", [1, 8]), ("ebias", [1, 8])]


def build_mha(phase=9):
    nc = bass.Bass("TRN2", target_bir_lowering=False, detect_race_conditions=False)
    io = {n: nc.dram_tensor(n, sh, F32, kind="ExternalInput").ap() for n, sh in MHA_IN}
    io["x3T"] = nc.dram_tensor("x3T", [D, TOK], F32, kind="ExternalOutput").ap()
    io["m0"] = nc.dram_tensor("m0", [TOK, 8], F32, kind="ExternalOutput").ap()
    io["m0t"] = nc.dram_tensor("m0t", [8, TOK], F32, kind="ExternalOutput").ap()
    io["x3bfT"] = nc.dram_tensor("x3bfT", [D, TOK], BF16, kind="ExternalOutput").ap()
    with ExitStack() as st0:
        _stage_mha(nc, make_sems(nc, st0), io, phase=phase)
    return nc


def _stage_mha(nc, S, io, phase=9):
    xT, qkvw, qkvb, qkvbrow, outw, outb, projw, projb, ebias = (
        io["mxT"], io["qkvw"], io["qkvb"], io["qkvbrow"], io["outw"], io["outb"],
        io["projw"], io["projb"], io["ebias"])
    x3T, m0, m0t = io["x3T"], io["m0"], io["m0t"]
    x3bfT = io["x3bfT"]
    gs, gc, ws, ys, ts, vs, ss = (S["gs"], S["gc"], S["ws"], S["ys"], S["ts"],
                                  S["vs"], S["ss"])
    wh = S["wh"]
    aoDram = nc.dram_tensor("aoDram", [D, TB], BF16, kind="Internal").ap()
    fused_pre = "x3tmLoc" in io
    _tp_ld, _tp_st = {}, {}

    NKT = TB // 128

    nc = _NamePrefix(nc, "mh_")
    _e_sTp4, _e_sZt, _e_sZf = [], [], []
    with ExitStack() as st:
        e = st.enter_context
        sWqkv = [e(nc.sbuf_tensor(f"sWqkv{i}", [128, 3072], BF16)) for i in range(NCT)]
        sWout = [e(nc.sbuf_tensor(f"sWout{i}", [128, D], BF16)) for i in range(NCT)]
        sPWn = e(nc.sbuf_tensor("sPWn", [8, D], F32))
        sProjW = [e(nc.sbuf_tensor(f"sProjW{i}", [128, 8], BF16)) for i in range(NCT)]
        QB = e(nc.sbuf_tensor("QB", [128, 24], F32))
        QBS = e(nc.sbuf_tensor("QBS", [128, 8], F32))
        OB = e(nc.sbuf_tensor("OB", [128, 8], F32))
        sprow = e(nc.sbuf_tensor("sprow", [1, 8], F32))
        sqbrow = e(nc.sbuf_tensor("sqbrow", [1, 1024], BF16))
        ONESROWB = e(nc.sbuf_tensor("ONESROWB", [1, 128], BF16))
        serow = e(nc.sbuf_tensor("serow", [1, 8], F32))
        PBT = e(nc.sbuf_tensor("PBT", [128, 8], F32))
        EBT = e(nc.sbuf_tensor("EBT", [128, 8], F32))
        VBT = e(nc.sbuf_tensor("VBT", [128, 1024], BF16))
        siota = e(nc.sbuf_tensor("siota", [128, 128], F32))
        IDENTF = e(nc.sbuf_tensor("IDENTF", [128, 128], F32))
        ONES1024 = e(nc.sbuf_tensor("ONES1024", [128, 1], BF16))
        ONESB = e(nc.sbuf_tensor("ONESB", [128, 1], BF16))
        ONESF = e(nc.sbuf_tensor("ONESF", [128, 1], F32))
        ONESROWF = e(nc.sbuf_tensor("ONESROWF", [1, 128], F32))
        sEPS = e(nc.sbuf_tensor("sEPS", [128, 1], F32))
        sX = [e(nc.sbuf_tensor(f"sX{i}", [128, TB], BF16)) for i in range(NCT)]
        sV = [e(nc.sbuf_tensor(f"sV{i}", [128, 16 * 65], BF16)) for i in range(NKT)]
        sqT = e(nc.sbuf_tensor("sqT", [128, TB], BF16))
        skT = e(nc.sbuf_tensor("skT", [128, TB], BF16))
        sk2s = e(nc.sbuf_tensor("sk2s", [128, 512], BF16))
        sAoW = e(nc.sbuf_tensor("sAoW", [128, TB], BF16))
        sAob = [e(nc.sbuf_tensor(f"sAob{i}", [128, 512], BF16)) for i in range(NCT)]
        sE0 = e(nc.sbuf_tensor("sE0", [128, 1024], BF16))
        sE1 = e(nc.sbuf_tensor("sE1", [128, 1024], BF16))
        sEx = [None, None]
        snc = e(nc.sbuf_tensor("snc", [128, 1], F32))
        sb1 = e(nc.sbuf_tensor("sb1", [1, 8], F32))
        sb2 = e(nc.sbuf_tensor("sb2", [1, 8], F32))
        sqmax = e(nc.sbuf_tensor("sqmax", [128, 4], F32))
        srd0 = e(nc.sbuf_tensor("srd0", [1, 512], F32))
        srd1 = e(nc.sbuf_tensor("srd1", [1, 512], F32))
        srdb = e(nc.sbuf_tensor("srdb", [64, 512], BF16))
        szb = e(nc.sbuf_tensor("szb", [128, 8], F32))
        sm1 = e(nc.sbuf_tensor("sm1", [128, 1], F32))
        sE2r = e(nc.sbuf_tensor("sE2r", [128, 8], F32))
        sm2 = e(nc.sbuf_tensor("sm2", [128, 1], F32))
        skeep = e(nc.sbuf_tensor("skeep", [128, 8], F32))
        swm = e(nc.sbuf_tensor("swm", [128, 8], F32))
        swsum = e(nc.sbuf_tensor("swsum", [128, 1], F32))
        slnw = e(nc.sbuf_tensor("slnw", [128, 8], F32))
        slns = e(nc.sbuf_tensor("slns", [128, 1], F32))
        swn = e(nc.sbuf_tensor("swn", [128, 8], F32))
        swnT = e(nc.sbuf_tensor("swnT", [8, 128], F32))
        sxf = [e(nc.sbuf_tensor(f"sxf{i}", [128, 512], F32)) for i in range(2)]
        syb = e(nc.sbuf_tensor("syb", [128, 8 * 512], BF16))
        sy2 = e(nc.sbuf_tensor("sy2", [128, 512], BF16))
        s_m = e(nc.sbuf_tensor("s_m", [1, 512], F32))
        s_t1 = e(nc.sbuf_tensor("s_t1", [1, 512], F32))
        s_var = e(nc.sbuf_tensor("s_var", [1, 512], F32))
        s_r = e(nc.sbuf_tensor("s_r", [1, 512], F32))
        s_bt = e(nc.sbuf_tensor("s_bt", [1, 512], F32))
        sln1 = e(nc.sbuf_tensor("sln1", [1, 512], F32))
        sxo = e(nc.sbuf_tensor("sxo", [128, 512], F32))
        sxout = [e(nc.sbuf_tensor(f"sxout{i}", [128, 512], F32)) for i in range(2)]
        pU = [e(nc.psum_tensor(f"pU{i}", [128, 1024], F32)) for i in range(2)]
        pC = e(nc.psum_tensor("pC", [128, 512], F32))
        pD = e(nc.psum_tensor("pD", [128, 512], F32))
        pE = e(nc.psum_tensor("pE", [128, 512], F32))
        pF = e(nc.psum_tensor("pF", [128, 512], F32))
        sEx[0], sEx[1] = sE0, sE1

        if fused_pre:
            sTp4 = [e(nc.sbuf_tensor(f"sTp4_{i}", [128, 1024], BF16,
                                     is_transpose_dest=True)) for i in range(3)]
            sZf = e(nc.sbuf_tensor("sZf", [128, 2 * (LCAP + 128) // 128], F32))
        block = e(nc.Block())

        @block.gpsimd
        def _(g):
            ys.wait(g)
            vs.wait(g)
            ss.wait(g)
            for i in range(NCT):
                gs.inc(g.dma_start(out=sWqkv[i][:], in_=qkvw[i * 128:(i + 1) * 128, :]), 16)
                gs.inc(g.dma_start(out=sWout[i][:], in_=outw[i * 128:(i + 1) * 128, :]), 16)
            gc.inc(g.iota(siota[:], [[1, 128]], base=0, channel_multiplier=-1,
                          allow_small_or_imprecise_dtypes=True), 1)
            for kt in range(NKT):
                gc.inc(g.memset(sV[kt][:], 1.0), 1)

        if fused_pre:
            @block.gpsimd
            def _(g):
                gs.inc(g.dma_start(out=io["sh1bf"], in_=io["sh1w"]), 16)
                gs.inc(g.dma_start(out=io["sh2bf"], in_=io["sh2w"]), 16)
                gs.inc(g.dma_start(out=io["myfc1bf"], in_=io["myfc1"]), 16)
                gs.inc(g.dma_start(out=io["myfc2bf"], in_=io["myfc2"]), 16)
                gc.inc(g.memset(sTp4[0][:], 0.0), 1)
                gc.inc(g.memset(sZf[:], 0.0), 1)
                g.wait_ge(gc.h, gc.n)
                for a in range(GTOK // 128):
                    gs.inc(g.dma_start(
                        out=io["yAll"][a * 128:(a + 1) * 128, :],
                        in_=sTp4[0][:]), 16)
                gs.inc(g.dma_start(
                    out=io["elist"][0:LCAP + 128, :].rearrange(
                        "(p q) w -> p (q w)", p=128),
                    in_=sZf[:]), 16)
            _zf_done = [gs.n]

        @block.sync
        def _(sy_):
            ys.wait(sy_)
            vs.wait(sy_)
            for j in range(24):
                ys.inc(sy_.dma_start(out=QB[:, j:j + 1], in_=qkvb[j]), 16)
            for j in range(8):
                ys.inc(sy_.dma_start(out=OB[:, j:j + 1], in_=outb[j]), 16)
            ys.inc(sy_.dma_start(out=sPWn[:], in_=projw), 16)
            ys.inc(sy_.dma_start(out=sprow[:], in_=projb), 16)
            ys.inc(sy_.dma_start(out=serow[:], in_=ebias), 16)

        @block.gpsimd
        def _(g):
            gs.inc(g.dma_start(out=sqbrow[:], in_=qkvbrow), 16)


        @block.vector
        def _(v):
            gs.wait(v)
            gc.wait(v)
            ys.wait(v)
            v.tensor_scalar(IDENTF[:], siota[:], 0.0, None, ALU.is_equal)
            v.memset(ONES1024[:], 1.0 / 1024)
            v.memset(ONESB[:], 1.0)
            v.memset(ONESF[:], 1.0)
            v.memset(sEPS[:], EPS)
            v.memset(ONESROWF[:], 1.0)
            v.memset(ONESROWB[:], 1.0)
            vs.inc(v.tensor_scalar(QBS[:], QB[:, 0:8], 0.125, None, ALU.mult))

        @block.tensor
        def _(t):
            vs.wait(t)
            t.matmul(pC[:, 0:8], ONESROWF[:], sprow[:], start=True, stop=True)
            t.matmul(pD[:, 0:8], ONESROWF[:], serow[:], start=True, stop=True)
            t.matmul(pE[:], ONESROWB[:], sqbrow[0:1, 0:512], start=True, stop=True)
            t.matmul(pF[:], ONESROWB[:], sqbrow[0:1, 512:1024], start=True, stop=True)
            for i in range(NCT):
                last = t.transpose(pC[:, 16 + i * 8:24 + i * 8],
                                   sPWn[0:8, i * 128:(i + 1) * 128], IDENTF[0:8, 0:8])
            ts.inc(last)

        @block.vector
        def _(v):
            ts.wait(v)
            v.tensor_copy(PBT[:], pC[:, 0:8])
            v.tensor_copy(EBT[:], pD[:, 0:8])
            v.tensor_copy(VBT[:, 0:512], pE[:])
            v.tensor_copy(VBT[:, 512:1024], pF[:])
            for i in range(NCT):
                last = v.tensor_copy(sProjW[i][:], pC[:, 16 + i * 8:24 + i * 8])
            vs.inc(last)

        init_vs = vs.n

        for b in range(NB):
            @block.gpsimd
            def _(g, b=b):
                if b > 0:
                    ts.wait(g)
                for i in range(NCT):
                    gs.inc(g.dma_start(
                        out=sX[i][:], in_=xT[i * 128:(i + 1) * 128, b * TB:(b + 1) * TB]), 16)

            xb_gs = gs.n

            if phase < 1:
                break
            for kt in range(NKT):
                @block.tensor
                def _(t, kt=kt, b=b):
                    t.wait_ge(gs.h, xb_gs)
                    if b == 0 and kt == 0:
                        t.wait_ge(vs.h, init_vs)
                    else:
                        vs.wait(t)
                    tc = kt * 128
                    mm_chain(t, pE[:],
                             [(sX[ci][:, tc:tc + 128], sWqkv[ci][:, 2048:2560])
                              for ci in range(NCT)])
                    mm_chain(t, pF[:],
                             [(sX[ci][:, tc:tc + 128], sWqkv[ci][:, 2560:3072])
                              for ci in range(NCT)], ts)

                @block.vector
                def _(v, kt=kt):
                    ts.wait(v)
                    vs.fence(v, v.tensor_tensor(sxo[:], pE[:], VBT[:, 0:512], ALU.add))
                    v.tensor_copy(
                        sV[kt][:].rearrange("p (h w) -> p h w", w=65)[:, 0:8, 0:64],
                        sxo[:].rearrange("p (h w) -> p h w", w=64))
                    vs.fence(v, v.tensor_tensor(sxo[:], pF[:], VBT[:, 512:1024], ALU.add))
                    vs.inc(v.tensor_copy(
                        sV[kt][:].rearrange("p (h w) -> p h w", w=65)[:, 8:16, 0:64],
                        sxo[:].rearrange("p (h w) -> p h w", w=64)))

            if phase < 2:
                continue
            for hp in range(8):
                for tb in range(4):
                    @block.tensor
                    def _(t, hp=hp, tb=tb):
                        vs.wait(t)
                        ss.wait(t)
                        mm_chain(t, pE[:],
                                 [(sWqkv[ci][:, hp * 128:(hp + 1) * 128],
                                   sX[ci][:, tb * GRP:(tb + 1) * GRP])
                                  for ci in range(NCT)])
                        mm_chain(t, pF[:],
                                 [(sWqkv[ci][:, 1024 + hp * 128:1024 + (hp + 1) * 128],
                                   sX[ci][:, tb * GRP:(tb + 1) * GRP])
                                  for ci in range(NCT)], ts)

                    @block.vector
                    def _(v, hp=hp, tb=tb):
                        ts.wait(v)
                        sl = slice(tb * GRP, (tb + 1) * GRP)
                        v.tensor_scalar(sqT[:, sl], pE[:], 0.125, QBS[:, hp:hp + 1],
                                        ALU.mult, ALU.add)
                        vs.inc(v.tensor_scalar(skT[:, sl], pF[:], QB[:, 8 + hp:9 + hp],
                                               None, ALU.add))

                    @block.vector
                    def _(v, tb=tb):
                        sl = slice(tb * GRP, (tb + 1) * GRP)
                        vs.fence(v, v.tensor_tensor(sy2[:], sqT[:, sl], sqT[:, sl],
                                                    ALU.mult))
                        vs.fence(v, v.tensor_tensor(sk2s[:], skT[:, sl],
                                                    skT[:, sl], ALU.mult))
                        vs.fence(v, v.reduce_max(sqmax[:, 2:3], sy2[:],
                                                 axis=mybir.AxisListType.X))
                        if tb == 0:
                            vs.fence(v, v.tensor_copy(sqmax[:, 0:1], sqmax[:, 2:3]))
                        else:
                            vs.fence(v, v.tensor_tensor(sqmax[:, 0:1], sqmax[:, 0:1],
                                                        sqmax[:, 2:3], ALU.max))
                        vs.fence(v, v.reduce_max(sqmax[:, 3:4], sk2s[:],
                                                 axis=mybir.AxisListType.X))
                        if tb == 0:
                            vs.inc(v.tensor_copy(sqmax[:, 1:2], sqmax[:, 3:4]))
                        else:
                            vs.inc(v.tensor_tensor(sqmax[:, 1:2], sqmax[:, 1:2],
                                                   sqmax[:, 3:4], ALU.max))

                @block.tensor
                def _(t):
                    vs.wait(t)
                    t.matmul(pC[0:1, 24:25], ONESF[0:64, :], sqmax[0:64, 0:1],
                             start=True, stop=True)
                    t.matmul(pC[0:1, 25:26], ONESF[64:128, :], sqmax[64:128, 0:1],
                             start=True, stop=True)
                    t.matmul(pC[0:1, 26:27], ONESF[0:64, :], sqmax[0:64, 1:2],
                             start=True, stop=True)
                    ts.inc(t.matmul(pC[0:1, 27:28], ONESF[64:128, :], sqmax[64:128, 1:2],
                                    start=True, stop=True))

                @block.scalar
                def _(s):
                    ts.wait(s)
                    ss.inc(s.activation(sb1[0:1, 0:4], pC[0:1, 24:28], AF.Ln))

                @block.vector
                def _(v):
                    ss.wait(v)
                    vs.fence(v, v.tensor_tensor(sb2[0:1, 0:2], sb1[0:1, 0:2],
                                                sb1[0:1, 2:4], ALU.add))
                    vs.fence(v, v.reduce_max(sb2[0:1, 4:5], sb2[0:1, 0:2],
                                             axis=mybir.AxisListType.X))
                    vs.inc(v.engine_nop())

                @block.scalar
                def _(s):
                    vs.wait(s)
                    ss.inc(s.activation(sb2[0:1, 5:6], sb2[0:1, 4:5], AF.Exp, scale=0.5))

                @block.vector
                def _(v):
                    ss.wait(v)
                    vs.inc(v.tensor_scalar(sb2[0:1, 6:7], sb2[0:1, 5:6], -1.0, None,
                                           ALU.mult))

                @block.tensor
                def _(t):
                    vs.wait(t)
                    ts.inc(t.matmul(pC[:, 24:25], ONESROWF[:], sb2[0:1, 6:7],
                                    start=True, stop=True))

                @block.vector
                def _(v):
                    ts.wait(v)
                    vs.inc(v.tensor_copy(snc[:], pC[:, 24:25]))

                for qb in range(4 if phase >= 3 else 0):
                    qsl = slice(qb * GRP, (qb + 1) * GRP)

                    def _scores(t, kt, qsl=qsl):
                        t.matmul(pU[kt % 2][:, 0:512],
                                 skT[0:64, kt * 128:(kt + 1) * 128],
                                 sqT[0:64, qsl], start=True, stop=True)
                        return t.matmul(pU[kt % 2][:, 512:1024],
                                        skT[64:128, kt * 128:(kt + 1) * 128],
                                        sqT[64:128, qsl], start=True, stop=True)

                    sc_ready = {}

                    @block.tensor
                    def _(t, _scores=_scores):
                        vs.wait(t)
                        _scores(t, 0)
                        ts.inc(_scores(t, 1))
                    sc_ready[0] = sc_ready[1] = ts.n

                    for kt in range(16):
                        u = kt % 2

                        @block.scalar
                        def _(s, u=u, kt=kt, sc_ready=sc_ready):
                            s.wait_ge(ts.h, sc_ready[kt])
                            ss.inc(s.activation(sEx[u][:], pU[u][:], AF.Exp,
                                                bias=snc[:, 0:1]))

                        @block.tensor
                        def _(t, hp=hp, kt=kt, u=u, _scores=_scores):
                            ss.wait(t)
                            t.matmul(pC[0:65, :],
                                     sV[kt][:, (2 * hp) * 65:(2 * hp) * 65 + 65],
                                     sEx[u][:, 0:512],
                                     start=(kt == 0), stop=(kt == 15))
                            last = t.matmul(pD[0:65, :],
                                            sV[kt][:, (2 * hp + 1) * 65:(2 * hp + 1) * 65 + 65],
                                            sEx[u][:, 512:1024],
                                            start=(kt == 0), stop=(kt == 15))
                            if kt + 2 < 16:
                                last = _scores(t, kt + 2)
                            ts.inc(last)
                        sc_ready[kt + 2] = ts.n

                    @block.scalar
                    def _(s):
                        ts.wait(s)
                        s.activation(sln1[:], pC[64:65, :], AF.Ln)
                        s.activation(srd0[:], sln1[:], AF.Exp, scale=-1.0)
                        s.activation(sln1[:], pD[64:65, :], AF.Ln)
                        ss.inc(s.activation(srd1[:], sln1[:], AF.Exp, scale=-1.0))

                    @block.tensor
                    def _(t):
                        ss.wait(t)
                        t.matmul(pE[0:64, :], ONESROWF[0:1, 0:64], srd0[:],
                                 start=True, stop=True)
                        ts.inc(t.matmul(pF[0:64, :], ONESROWF[0:1, 0:64], srd1[:],
                                        start=True, stop=True))

                    @block.vector
                    def _(v, hp=hp, qsl=qsl, qb=qb, b=b):
                        ts.wait(v)
                        if qb == 0 and not (hp == 0 and b == 0):
                            ys.wait(v)
                        vs.fence(v, v.tensor_copy(srdb[:], pE[0:64, :]))
                        v.tensor_tensor(sAoW[0:64, qsl], pC[0:64, :], srdb[:],
                                        ALU.mult)
                        vs.fence(v, v.tensor_copy(srdb[:], pF[0:64, :]))
                        vs.inc(v.tensor_tensor(sAoW[64:128, qsl], pD[0:64, :],
                                               srdb[:], ALU.mult))

                @block.sync
                def _(sy_, hp=hp):
                    vs.wait(sy_)
                    ys.inc(sy_.dma_start(out=aoDram[hp * 128:(hp + 1) * 128, :],
                                         in_=sAoW[:]), 16)

            if phase < 4:
                continue
            for tb in range(4):
                t0 = tb * GRP

                @block.sync
                def _(sy_, tb=tb, b=b):
                    sy_.wait_ge(ys.h, ys.n)
                    if not (b == 0 and tb == 0):
                        ts.wait(sy_)
                    for vt in range(NCT):
                        ys.inc(sy_.dma_start(
                            out=sAob[vt][:],
                            in_=aoDram[vt * 128:(vt + 1) * 128,
                                       tb * GRP:(tb + 1) * GRP]), 16)

                for ct in range(NCT):
                    @block.sync
                    def _(sy_, ct=ct, b=b, t0=t0, tb=tb):
                        if not (b == 0 and tb == 0 and ct < 2):
                            vs.wait(sy_)
                        ys.inc(sy_.dma_start(
                            out=sxf[ct % 2][:],
                            in_=xT[ct * 128:(ct + 1) * 128,
                                   b * TB + t0:b * TB + t0 + GRP]), 16)

                    @block.tensor
                    def _(t, ct=ct, tb=tb):
                        vs.wait(t)
                        ys.wait(t)
                        mm_chain(t, [pE, pF][ct % 2][:],
                                 [(sWout[vt][:, ct * 128:(ct + 1) * 128],
                                   sAob[vt][:]) for vt in range(NCT)], ts)

                    @block.vector
                    def _(v, ct=ct):
                        ts.wait(v)
                        ys.wait(v)
                        vs.fence(v, v.tensor_scalar([pE, pF][ct % 2][:] if False else sxo[:],
                                                    [pE, pF][ct % 2][:],
                                                    OB[:, ct:ct + 1], None, ALU.add))
                        vs.inc(v.tensor_tensor(syb[:, ct * GRP:(ct + 1) * GRP], sxo[:],
                                                sxf[ct % 2][:], ALU.add))

                    @block.scalar
                    def _(s, ct=ct):
                        vs.wait(s)
                        ss.inc(s.square(sy2[:], syb[:, ct * GRP:(ct + 1) * GRP]))

                    @block.tensor
                    def _(t, ct=ct):
                        ss.wait(t)
                        t.matmul(pC[0:1, 0:512], ONES1024[:],
                                 syb[:, ct * GRP:(ct + 1) * GRP],
                                 start=(ct == 0), stop=(ct == NCT - 1))
                        ts.inc(t.matmul(pD[0:1, 0:512], ONES1024[:], sy2[:],
                                        start=(ct == 0), stop=(ct == NCT - 1)))

                @block.vector
                def _(v):
                    ts.wait(v)
                    vs.fence(v, v.tensor_copy(s_m[:], pC[0:1, 0:512]))
                    vs.fence(v, v.tensor_tensor(s_t1[:], s_m[:], s_m[:], ALU.mult))
                    vs.inc(v.tensor_tensor(s_var[:], pD[0:1, 0:512], s_t1[:],
                                           ALU.subtract))

                @block.scalar
                def _(s):
                    vs.wait(s)
                    s.activation(sln1[:], s_var[:], AF.Ln, bias=sEPS[0:1, 0:1])
                    ss.inc(s.activation(s_r[:], sln1[:], AF.Exp, scale=-0.5))

                @block.vector
                def _(v):
                    ss.wait(v)
                    vs.inc(v.scalar_tensor_tensor(s_bt[:], s_m[:], -1.0, s_r[:],
                                                  ALU.mult, ALU.mult))

                @block.tensor
                def _(t):
                    vs.wait(t)
                    t.matmul(pC[:], ONESROWF[:], s_r[:], start=True, stop=True)
                    ts.inc(t.matmul(pD[:], ONESROWF[:], s_bt[:], start=True, stop=True))

                for ct in range(NCT):
                    @block.vector
                    def _(v, ct=ct, b=b, tb=tb):
                        if ct == 0:
                            ts.wait(v)
                        if not (b == 0 and tb == 0 and ct < 2):
                            ys.wait(v)
                        v.tensor_tensor(sxo[:], syb[:, ct * GRP:(ct + 1) * GRP],
                                        pC[:], ALU.mult)
                        vs.fence(v, v.tensor_tensor(sxout[ct % 2][:], sxo[:], pD[:],
                                                    ALU.add))
                        stash = sqT if ct < 4 else skT
                        vs.inc(v.tensor_copy(stash[:, (ct % 4) * GRP:(ct % 4 + 1) * GRP],
                                             sxout[ct % 2][:]))

                    @block.sync
                    def _(sy_, ct=ct, b=b, t0=t0):
                        vs.wait(sy_)
                        ys.inc(sy_.dma_start(
                            out=x3T[ct * 128:(ct + 1) * 128,
                                    b * TB + t0:b * TB + t0 + GRP],
                            in_=sxout[ct % 2][:]), 16)
                        stash = sqT if ct < 4 else skT
                        ys.inc(sy_.dma_start(
                            out=x3bfT[ct * 128:(ct + 1) * 128,
                                      b * TB + t0:b * TB + t0 + GRP],
                            in_=stash[:, (ct % 4) * GRP:(ct % 4 + 1) * GRP]), 16)

                if fused_pre:
                    @block.sync
                    def _(sy_, b=b, tb=tb):
                        sy_.wait_ge(ys.h, ys.n)
                        if b == 0 and tb == 0:
                            sy_.wait_ge(gs.h, _zf_done[0])
                        cch = b * 4 + tb
                        for j in range(4):
                            i = 4 * cch + j
                            if i >= 3:
                                sy_.wait_ge(ys.h, _tp_st[i - 3])
                            wh.inc(sy_.dma_start(
                                out=sTp4[i % 3][:],
                                in_=x3bfT[0:1024, i * 128:(i + 1) * 128],
                                transpose=True), 16)
                            _tp_ld[i] = wh.n
                            if i >= 1:
                                sy_.wait_ge(wh.h, _tp_ld[i - 1])
                                ys.inc(sy_.dma_start(
                                    out=io["x3tmLoc"][(i - 1) * 128:i * 128, :],
                                    in_=sTp4[(i - 1) % 3][:]), 16)
                                _tp_st[i - 1] = ys.n

                for c in range(4 if phase >= 5 else 0):
                    @block.tensor
                    def _(t, c=c):
                        vs.wait(t)
                        mm_chain(t, pE[:, 0:8],
                                 [((sqT if ci < 4 else skT)[:, (ci % 4) * GRP + c * 128:
                                                            (ci % 4) * GRP + (c + 1) * 128],
                                   sProjW[ci][:]) for ci in range(NCT)], ts)

                    @block.vector
                    def _(v):
                        ts.wait(v)
                        vs.inc(v.tensor_tensor(szb[:], pE[:, 0:8], PBT[:], ALU.add))

                    @block.scalar
                    def _(s):
                        vs.wait(s)
                        ss.inc(s.activation(swm[:], szb[:], AF.Tanh, scale=0.5))

                    @block.vector
                    def _(v):
                        ss.wait(v)
                        vs.fence(v, v.tensor_scalar(swm[:], swm[:], 0.5, 0.5,
                                                    ALU.mult, ALU.add))
                        vs.fence(v, v.tensor_tensor(swm[:], swm[:], EBT[:], ALU.add))
                        vs.fence(v, v.tensor_scalar(swm[:], swm[:], 1e-12, None,
                                                    ALU.max))
                        vs.fence(v, v.reduce_max(sm1[:], swm[:],
                                                 axis=mybir.AxisListType.X))
                        vs.fence(v, v.tensor_scalar(sE2r[:], swm[:], sm1[:, 0:1], None,
                                                    ALU.is_equal))
                        vs.fence(v, v.scalar_tensor_tensor(sE2r[:], sE2r[:], -1e30,
                                                           swm[:], ALU.mult, ALU.add))
                        vs.fence(v, v.reduce_max(sm2[:], sE2r[:],
                                                 axis=mybir.AxisListType.X))
                        vs.fence(v, v.tensor_scalar(skeep[:], swm[:], sm2[:, 0:1], None,
                                                    ALU.is_ge))
                        vs.fence(v, v.tensor_tensor(swm[:], swm[:], skeep[:], ALU.mult))
                        vs.fence(v, v.reduce_sum(swsum[:], swm[:],
                                                 axis=mybir.AxisListType.X))
                        vs.inc(v.tensor_scalar(swm[:], swm[:], 1e-30, None, ALU.max))

                    @block.scalar
                    def _(s):
                        vs.wait(s)
                        s.activation(slnw[:], swm[:], AF.Ln)
                        ss.inc(s.activation(slns[:], swsum[:], AF.Ln))

                    @block.vector
                    def _(v):
                        ss.wait(v)
                        vs.inc(v.tensor_scalar(slnw[:], slnw[:], slns[:, 0:1], None,
                                               ALU.subtract))

                    @block.scalar
                    def _(s):
                        vs.wait(s)
                        s.wait_ge(ys.h, ys.n)
                        ss.inc(s.activation(swn[:], slnw[:], AF.Exp))

                    @block.tensor
                    def _(t):
                        ss.wait(t)
                        ts.inc(t.transpose(pF[0:8, 0:128], swn[:], IDENTF[:]))

                    @block.vector
                    def _(v):
                        ts.wait(v)
                        vs.inc(v.tensor_copy(swnT[:], pF[0:8, 0:128]))

                    @block.sync
                    def _(sy_, b=b, t0=t0, c=c):
                        vs.wait(sy_)
                        row0 = b * TB + t0 + c * 128
                        ys.inc(sy_.dma_start(out=m0[row0:row0 + 128, :], in_=swn[:]), 16)
                        ys.inc(sy_.dma_start(out=m0t[:, row0:row0 + 128], in_=swnT[:]), 16)

        if fused_pre:
            NT32 = TOK // 128
            @block.sync
            def _(sy_):
                sy_.wait_ge(wh.h, _tp_ld[NT32 - 1])
                ys.inc(sy_.dma_start(
                    out=io["x3tmLoc"][(NT32 - 1) * 128:NT32 * 128, :],
                    in_=sTp4[(NT32 - 1) % 3][:]), 16)
            io["x3tm_done"] = ys.n

        @block.sync
        def _(sy_):
            ys.wait(sy_)


def _qkv_perm():
    q = [192 * h + d for h in range(16) for d in range(64)]
    k = [192 * h + 64 + d for h in range(16) for d in range(64)]
    v = [192 * h + 128 + d for h in range(16) for d in range(64)]
    return np.array(q + k + v)


def _prep_mha_inputs(x2T_core, ins):
    perm = _qkv_perm()
    qkvw = np.ascontiguousarray(ins["L1_attn_qkv_w"][:, perm])
    qkvb = np.ascontiguousarray(ins["L1_attn_qkv_b"][perm])
    return {
        "mxT": x2T_core,
        "qkvw": qkvw,
        "qkvb": qkvb.reshape(24, 128, 1),
        "qkvbrow": np.ascontiguousarray(qkvb[2048:].reshape(1, 1024)),
        "outw": ins["L1_attn_out_w"],
        "outb": ins["L1_attn_out_b"].reshape(8, 128, 1),
        "projw": ins["L1_ffn_proj_w"],
        "projb": ins["L1_ffn_proj_b"].reshape(1, 8),
        "ebias": ins["L1_ffn_expert_bias"].reshape(1, 8),
    }





CAP = 10240.0

FFN1_IN = [("f1xT", [D, TOK]), ("fc1w", [8, 2048, D]), ("fc1b", [8, 16, 128, 1]),
           ("fc2w", [8, D, 2048]), ("fc2b", [8, D]), ("sh1w", [4096, D]),
           ("sh1b", [32, 128, 1]), ("sh2w", [D, 4096]), ("sh2b", [8, 128, 1])]


def build_ffn1():
    nc = bass.Bass("TRN2", target_bir_lowering=False, detect_race_conditions=False)
    io = {n: nc.dram_tensor(n, sh, F32, kind="ExternalInput").ap() for n, sh in FFN1_IN}
    io["m0all"] = nc.dram_tensor("m0all", [8, 8 * TOK], F32, kind="ExternalInput").ap()
    io["m0loc"] = nc.dram_tensor("m0loc", [TOK, 8], F32, kind="ExternalInput").ap()
    io["x4T"] = nc.dram_tensor("x4T", [D, TOK], F32, kind="ExternalOutput").ap()
    with ExitStack() as st0:
        _stage_ffn1(nc, make_sems(nc, st0), io, ranks_layout=False)
    return nc


def _stage_ffn1(nc, S, io, ranks_layout=False):
    xT, fc1w, fc1b, fc2w, fc2b, sh1w, sh1b, sh2w, sh2b = (
        io["f1xT"], io["fc1w"], io["fc1b"], io["fc2w"], io["fc2b"], io["sh1w"],
        io["sh1b"], io["sh2w"], io["sh2b"])
    m0all, m0loc, x4T = io["m0all"], io["m0loc"], io["x4T"]
    gs, gc, ws, ys, ts, vs, ss, cc = (S["gs"], S["gc"], S["wh"], S["ys"], S["ts"],
                                      S["vs"], S["ss"], S["cc"])
    cc_target = S["cc"].n

    fc1bf = nc.dram_tensor("fc1bf", [8, 2048, D], BF16, kind="Internal").ap()
    fc2bf = nc.dram_tensor("fc2bf", [8, D, 2048], BF16, kind="Internal").ap()
    sh1bf = nc.dram_tensor("sh1bf", [4096, D], BF16, kind="Internal").ap()
    sh2bf = nc.dram_tensor("sh2bf", [D, 4096], BF16, kind="Internal").ap()

    NGRP = TOK // GRP
    NIT = 37

    nc = _NamePrefix(nc, "f1_")
    with ExitStack() as st:
        e = st.enter_context
        sMe = e(nc.sbuf_tensor("sMe", [128, 8 * 256], F32))
        sjunk = e(nc.sbuf_tensor("sjunk", [128, 256], F32))
        scnt = e(nc.sbuf_tensor("scnt", [128, 8], F32))
        sth = e(nc.sbuf_tensor("sth", [1, 8], F32))
        sthk = e(nc.sbuf_tensor("sthk", [1, 8], F32))
        sgt = e(nc.sbuf_tensor("sgt", [1, 8], F32))
        sok = e(nc.sbuf_tensor("sok", [1, 8], F32))
        st1 = e(nc.sbuf_tensor("st1", [1, 8], F32))
        sthc = e(nc.sbuf_tensor("sthc", [128, 8], F32))
        smloc = e(nc.sbuf_tensor("smloc", [128, 8], F32))
        smk = e(nc.sbuf_tensor("smk", [128, 8], F32))
        sm0Tm = e(nc.sbuf_tensor("sm0Tm", [8, TOK], BF16))
        siota = e(nc.sbuf_tensor("siota", [8, 1024], F32))
        siota2 = e(nc.sbuf_tensor("siota2", [128, 128], F32))
        SEL8 = e(nc.sbuf_tensor("SEL8", [8, 1024], BF16))
        IDENTF = e(nc.sbuf_tensor("IDENTF", [128, 128], F32))
        ONESROWF = e(nc.sbuf_tensor("ONESROWF", [1, 128], F32))
        ONESF = e(nc.sbuf_tensor("ONESF", [128, 1], F32))
        ONES1024 = e(nc.sbuf_tensor("ONES1024", [128, 1], BF16))
        sEPS = e(nc.sbuf_tensor("sEPS", [128, 1], F32))
        sFc1b = e(nc.sbuf_tensor("sFc1b", [128, 128], F32))
        sSh1b = e(nc.sbuf_tensor("sSh1b", [128, 32], F32))
        sSh2b = e(nc.sbuf_tensor("sSh2b", [128, 8], F32))
        sFc2b = e(nc.sbuf_tensor("sFc2b", [8, D], BF16))
        sFc2bf32 = e(nc.sbuf_tensor("sFc2bf32", [8, D], F32))
        sXg = [[e(nc.sbuf_tensor(f"sXg{p}_{i}", [128, GRP], BF16)) for i in range(NCT)]
               for p in range(2)]
        sW1 = [[e(nc.sbuf_tensor(f"sW1_{p}_{i}", [128, 512], BF16, is_transpose_dest=True))
                for i in range(NCT)] for p in range(2)]
        sW2 = [[e(nc.sbuf_tensor(f"sW2_{p}_{i}", [128, 1024], BF16, is_transpose_dest=True))
                for i in range(4)] for p in range(2)]
        sHd = e(nc.sbuf_tensor("sHd", [128, 4 * 512], BF16))
        sHs = e(nc.sbuf_tensor("sHs", [128, 4 * 512], BF16))
        smb = e(nc.sbuf_tensor("smb", [128, GRP], BF16))
        sacc = e(nc.sbuf_tensor("sacc", [128, 8 * GRP], F32))
        sb2sb = e(nc.sbuf_tensor("sb2sb", [128, GRP], F32))
        sxf = [e(nc.sbuf_tensor(f"sxf{i}", [128, 512], F32)) for i in range(2)]
        sy_bf = e(nc.sbuf_tensor("sy_bf", [128, 512], BF16))
        sy2 = e(nc.sbuf_tensor("sy2", [128, 512], BF16))
        s_m = e(nc.sbuf_tensor("s_m", [1, 512], F32))
        s_t1 = e(nc.sbuf_tensor("s_t1", [1, 512], F32))
        s_var = e(nc.sbuf_tensor("s_var", [1, 512], F32))
        s_r = e(nc.sbuf_tensor("s_r", [1, 512], F32))
        s_bt = e(nc.sbuf_tensor("s_bt", [1, 512], F32))
        sln1 = e(nc.sbuf_tensor("sln1", [1, 512], F32))
        sxo = e(nc.sbuf_tensor("sxo", [128, 512], F32))
        sxout = [e(nc.sbuf_tensor(f"sxout{i}", [128, 512], F32)) for i in range(2)]
        pb = [e(nc.psum_tensor(f"pb{i}", [128, 512], F32)) for i in range(8)]

        block = e(nc.Block())

        @block.gpsimd
        def _(g):
            gs.inc(g.dma_start(out=fc1bf, in_=fc1w), 16)
            gs.inc(g.dma_start(out=fc2bf, in_=fc2w), 16)
            gs.inc(g.dma_start(out=sh1bf, in_=sh1w), 16)
            gs.inc(g.dma_start(out=sh2bf, in_=sh2w), 16)
            ys.wait(g)
            vs.wait(g)
            ss.wait(g)
            gc.inc(g.iota(siota[:], [[1, 8], [0, 128]], base=0, channel_multiplier=-1,
                          allow_small_or_imprecise_dtypes=True), 1)
            gc.inc(g.iota(siota2[:], [[1, 128]], base=0, channel_multiplier=-1,
                          allow_small_or_imprecise_dtypes=True), 1)

        prep_gs = gs.n

        @block.sync
        def _(sy_):
            ys.wait(sy_)
            vs.wait(sy_)
            sy_.wait_ge(cc.h, cc_target)
            if ranks_layout:
                for ei in range(8):
                    for r in range(8):
                        ys.inc(sy_.dma_start(
                            out=sMe[:, ei * 256:(ei + 1) * 256][r * 16:(r + 1) * 16, :],
                            in_=m0all[r * 8 + ei].rearrange("(p n) -> p n", p=16)), 16)
            else:
                for ei in range(8):
                    ys.inc(sy_.dma_start(
                        out=sMe[:, ei * 256:(ei + 1) * 256],
                        in_=m0all[ei].rearrange("(p n) -> p n", p=128)), 16)
            for ei in range(8):
                for hj in range(16):
                    ys.inc(sy_.dma_start(out=sFc1b[:, ei * 16 + hj:ei * 16 + hj + 1],
                                         in_=fc1b[ei, hj]), 16)
            for j in range(32):
                ys.inc(sy_.dma_start(out=sSh1b[:, j:j + 1], in_=sh1b[j]), 16)
            for j in range(8):
                ys.inc(sy_.dma_start(out=sSh2b[:, j:j + 1], in_=sh2b[j]), 16)
            ys.inc(sy_.dma_start(out=sFc2bf32[:], in_=fc2b), 16)

        @block.vector
        def _(v):
            ys.wait(v)
            gc.wait(v)
            v.tensor_scalar(SEL8[:], siota[:], 0.0, None, ALU.is_equal)
            v.tensor_scalar(IDENTF[:], siota2[:], 0.0, None, ALU.is_equal)
            v.memset(ONESROWF[:], 1.0)
            v.memset(ONESF[:], 1.0)
            v.memset(ONES1024[:], 1.0 / 1024)
            v.memset(sEPS[:], EPS)
            v.memset(sth[:], 0.5)
            v.memset(sthk[:], 1.1)
            vs.inc(v.tensor_copy(sFc2b[:], sFc2bf32[:]))

        dk = 0.25
        for it in range(NIT):
            @block.vector
            def _(v, it=it):
                if it == 0:
                    vs.wait(v)
                else:
                    ts.wait(v)
                    vs.fence(v, v.tensor_copy(sthc[:], pb[1][:, 0:8]))
                for ei in range(8):
                    last = v.tensor_scalar(sjunk[:], sMe[:, ei * 256:(ei + 1) * 256],
                                           sthc[:, ei:ei + 1] if it > 0 else 0.5,
                                           0.0, ALU.is_ge, ALU.add,
                                           accum_out=scnt[:, ei:ei + 1])
                vs.inc(last)

            @block.tensor
            def _(t):
                vs.wait(t)
                ts.inc(t.matmul(pb[0][0:1, 0:8], ONESF[:], scnt[:], start=True, stop=True))

            @block.vector
            def _(v, it=it):
                ts.wait(v)
                vs.fence(v, v.tensor_scalar(sgt[:], pb[0][0:1, 0:8], CAP, None, ALU.is_gt))
                vs.fence(v, v.tensor_scalar(sok[:], sgt[:], -1.0, 1.0, ALU.mult, ALU.add))
                vs.fence(v, v.tensor_tensor(st1[:], sok[:], sth[:], ALU.mult))
                vs.fence(v, v.tensor_tensor(sthk[:], sgt[:], sthk[:], ALU.mult))
                vs.fence(v, v.tensor_tensor(sthk[:], sthk[:], st1[:], ALU.add))
                d = dk / 2 if it + 1 < NIT else dk / 2
                vs.fence(v, v.scalar_tensor_tensor(st1[:], sgt[:], 2.0 * d, sth[:],
                                                   ALU.mult, ALU.add))
                vs.inc(v.tensor_scalar(sth[:], st1[:], -d, None, ALU.add))
            dk = dk / 2

            @block.tensor
            def _(t, it=it):
                vs.wait(t)
                src = sthk if it == NIT - 1 else sth
                ts.inc(t.matmul(pb[1][:, 0:8], ONESROWF[:], src[:], start=True, stop=True))

        @block.vector
        def _(v):
            ts.wait(v)
            vs.inc(v.tensor_copy(sthc[:], pb[1][:, 0:8]))

        for c in range(TOK // 128):
            @block.sync
            def _(sy_, c=c):
                if c == 0:
                    vs.wait(sy_)
                else:
                    ts.wait(sy_)
                ys.inc(sy_.dma_start(out=smloc[:], in_=m0loc[c * 128:(c + 1) * 128, :]), 16)

            @block.vector
            def _(v, c=c):
                ys.wait(v)
                vs.fence(v, v.tensor_tensor(smk[:], smloc[:], sthc[:], ALU.is_ge))
                vs.inc(v.tensor_tensor(smk[:], smk[:], smloc[:], ALU.mult))

            @block.tensor
            def _(t, c=c):
                vs.wait(t)
                ts.inc(t.transpose(pb[2][0:8, 0:128], smk[:], IDENTF[:]))

            @block.vector
            def _(v, c=c):
                ts.wait(v)
                vs.inc(v.tensor_copy(sm0Tm[:, c * 128:(c + 1) * 128], pb[2][0:8, 0:128]))

        routing_vs = vs.n

        @block.gpsimd
        def _(g):
            for i in range(NCT):
                gs.inc(g.dma_start(out=sXg[0][i][:], in_=xT[i * 128:(i + 1) * 128, 0:GRP]), 16)

        def unit_of(grp, u):
            if u < 8:
                return ("sh", u)
            return ("ex", (u - 8) // 4, (u - 8) % 4)

        NU = 40

        def emit_prefetch(sy_, grp, u):
            p = u % 2
            kind = unit_of(grp, u)
            if kind[0] == "sh":
                hs = kind[1]
                for ct in range(NCT):
                    ws.inc(sy_.dma_start(
                        out=sW1[p][ct][:],
                        in_=sh1bf[hs * 512:(hs + 1) * 512, ct * 128:(ct + 1) * 128],
                        transpose=True), 16)
                for j in range(4):
                    ws.inc(sy_.dma_start(
                        out=sW2[p][j][:],
                        in_=sh2bf[0:1024, hs * 512 + j * 128:hs * 512 + (j + 1) * 128],
                        transpose=True), 16)
            else:
                ei, hsl = kind[1], kind[2]
                for ct in range(NCT):
                    ws.inc(sy_.dma_start(
                        out=sW1[p][ct][:],
                        in_=fc1bf[ei][hsl * 512:(hsl + 1) * 512, ct * 128:(ct + 1) * 128],
                        transpose=True), 16)
                for j in range(4):
                    ws.inc(sy_.dma_start(
                        out=sW2[p][j][:],
                        in_=fc2bf[ei][0:1024, hsl * 512 + j * 128:hsl * 512 + (j + 1) * 128],
                        transpose=True), 16)

        @block.sync
        def _(sy_):
            sy_.wait_ge(gs.h, prep_gs)
            emit_prefetch(sy_, 0, 0)

        _UN = {"ws_ready": {1: ws.n}, "ts_done": {}}
        for grp in range(NGRP):
            t0 = grp * GRP
            xg = sXg[grp % 2]
            _UN["ts_done"] = {-1: _UN["ts_done"].get(NU - 1, 0)}
            _UN["ws_ready"] = {0: _UN["ws_ready"].get(NU, ws.n)}

            if grp + 1 < NGRP:
                @block.gpsimd
                def _(g, grp=grp):
                    ts.wait(g)
                    for i in range(NCT):
                        gs.inc(g.dma_start(
                            out=sXg[(grp + 1) % 2][i][:],
                            in_=xT[i * 128:(i + 1) * 128,
                                   (grp + 1) * GRP:(grp + 2) * GRP]), 16)

            @block.tensor
            def _(t, grp=grp, t0=t0):
                gs.wait(t)
                if grp == 0:
                    t.wait_ge(vs.h, routing_vs)
                else:
                    vs.wait(t)
                for ct in range(NCT):
                    last = t.matmul(pb[ct][:], sFc2b[:, ct * 128:(ct + 1) * 128],
                                    sm0Tm[:, t0:t0 + GRP], start=True, stop=True)
                ts.inc(last)

            @block.vector
            def _(v):
                ts.wait(v)
                for ct in range(NCT):
                    last = v.tensor_copy(sacc[:, ct * GRP:(ct + 1) * GRP], pb[ct][:])
                vs.inc(last)

            for u in range(NU):
                p = u % 2
                kind = unit_of(grp, u)

                @block.sync
                def _(sy_, grp=grp, u=u):
                    sy_.wait_ge(ts.h, _UN["ts_done"].get(u - 1, 0))
                    if u + 1 < NU:
                        emit_prefetch(sy_, grp, u + 1)
                    elif grp + 1 < NGRP:
                        emit_prefetch(sy_, grp + 1, 0)
                _UN["ws_ready"][u + 1] = ws.n

                if kind[0] == "ex" and kind[2] == 0:
                    ei = kind[1]
                    @block.tensor
                    def _(t, ei=ei, t0=t0):
                        vs.wait(t)
                        ts.inc(t.matmul(pb[6][:], SEL8[:, ei * 128:(ei + 1) * 128],
                                        sm0Tm[0:8, t0:t0 + GRP], start=True, stop=True))

                    @block.vector
                    def _(v):
                        ts.wait(v)
                        vs.inc(v.tensor_copy(smb[:], pb[6][:]))

                @block.tensor
                def _(t, u=u, p=p):
                    t.wait_ge(ws.h, _UN["ws_ready"].get(u, ws.n))
                    ss.wait(t)
                    if u == 0:
                        vs.wait(t)
                    for j in range(4):
                        mm_chain(t, pb[j][:],
                                 [(sW1[p][ci][:, j * 128:(j + 1) * 128], xg[ci][:])
                                  for ci in range(NCT)],
                                 ts if j == 3 else None)

                @block.scalar
                def _(s, kind=kind):
                    ts.wait(s)
                    vs.wait(s)
                    for j in range(4):
                        if kind[0] == "sh":
                            bcol = sSh1b[:, kind[1] * 4 + j:kind[1] * 4 + j + 1]
                        else:
                            idx = kind[1] * 16 + kind[2] * 4 + j
                            bcol = sFc1b[:, idx:idx + 1]
                        last = s.activation(sHd[:, j * 512:(j + 1) * 512], pb[j][:],
                                            AF.Gelu, bias=bcol)
                    ss.inc(last)

                if kind[0] == "ex":
                    @block.vector
                    def _(v):
                        ss.wait(v)
                        for j in range(4):
                            last = v.tensor_tensor(sHs[:, j * 512:(j + 1) * 512],
                                                   sHd[:, j * 512:(j + 1) * 512],
                                                   smb[:], ALU.mult)
                        vs.inc(last)
                    hsrc = sHs
                else:
                    hsrc = sHd

                for ct in range(NCT):
                    @block.tensor
                    def _(t, ct=ct, p=p, kind=kind, hsrc=hsrc, u_=u):
                        if ct == 0:
                            if kind[0] == "ex":
                                vs.wait(t)
                            else:
                                ss.wait(t)
                        if ct >= 2:
                            vs.wait(t)
                        mm_chain(t, pb[4 + ct % 2][:],
                                 [(sW2[p][j][:, ct * 128:(ct + 1) * 128],
                                   hsrc[:, j * 512:(j + 1) * 512]) for j in range(4)],
                                 ts)
                        if ct == NCT - 1:
                            _UN["ts_done"][u_] = ts.n

                    @block.vector
                    def _(v, ct=ct):
                        ts.wait(v)
                        vs.inc(v.tensor_tensor(sacc[:, ct * GRP:(ct + 1) * GRP],
                                               sacc[:, ct * GRP:(ct + 1) * GRP],
                                               pb[4 + ct % 2][:], ALU.add))

            for ct in range(NCT):
                @block.sync
                def _(sy_, ct=ct, t0=t0, grp=grp):
                    if not (grp == 0 and ct < 2):
                        vs.wait(sy_)
                    ys.inc(sy_.dma_start(
                        out=sxf[ct % 2][:],
                        in_=xT[ct * 128:(ct + 1) * 128, t0:t0 + GRP]), 16)

                @block.vector
                def _(v, ct=ct):
                    ys.wait(v)
                    vs.fence(v, v.tensor_scalar(sxo[:], sacc[:, ct * GRP:(ct + 1) * GRP],
                                                sSh2b[:, ct:ct + 1], None, ALU.add))
                    v.tensor_tensor(sacc[:, ct * GRP:(ct + 1) * GRP], sxo[:],
                                    sxf[ct % 2][:], ALU.add)
                    vs.inc(v.tensor_copy(sy_bf[:], sacc[:, ct * GRP:(ct + 1) * GRP]))

                @block.scalar
                def _(s, ct=ct):
                    vs.wait(s)
                    ss.inc(s.square(sy2[:], sy_bf[:]))

                @block.tensor
                def _(t, ct=ct):
                    ss.wait(t)
                    t.matmul(pb[0][0:1, :], ONES1024[:], sy_bf[:],
                             start=(ct == 0), stop=(ct == NCT - 1))
                    ts.inc(t.matmul(pb[1][0:1, :], ONES1024[:], sy2[:],
                                    start=(ct == 0), stop=(ct == NCT - 1)))

            @block.vector
            def _(v):
                ts.wait(v)
                vs.fence(v, v.tensor_copy(s_m[:], pb[0][0:1, :]))
                vs.fence(v, v.tensor_tensor(s_t1[:], s_m[:], s_m[:], ALU.mult))
                vs.inc(v.tensor_tensor(s_var[:], pb[1][0:1, :], s_t1[:], ALU.subtract))

            @block.scalar
            def _(s):
                vs.wait(s)
                s.activation(sln1[:], s_var[:], AF.Ln, bias=sEPS[0:1, 0:1])
                ss.inc(s.activation(s_r[:], sln1[:], AF.Exp, scale=-0.5))

            @block.vector
            def _(v):
                ss.wait(v)
                vs.inc(v.scalar_tensor_tensor(s_bt[:], s_m[:], -1.0, s_r[:],
                                              ALU.mult, ALU.mult))

            @block.tensor
            def _(t):
                vs.wait(t)
                t.matmul(pb[2][:], ONESROWF[:], s_r[:], start=True, stop=True)
                ts.inc(t.matmul(pb[3][:], ONESROWF[:], s_bt[:], start=True, stop=True))

            for ct in range(NCT):
                @block.vector
                def _(v, ct=ct, grp=grp):
                    if ct == 0:
                        ts.wait(v)
                    if not (grp == 0 and ct < 2):
                        ys.wait(v)
                    v.tensor_tensor(sxo[:], sacc[:, ct * GRP:(ct + 1) * GRP],
                                    pb[2][:], ALU.mult)
                    vs.inc(v.tensor_tensor(sxout[ct % 2][:], sxo[:], pb[3][:], ALU.add))

                @block.sync
                def _(sy_, ct=ct, t0=t0):
                    vs.wait(sy_)
                    ys.inc(sy_.dma_start(
                        out=x4T[ct * 128:(ct + 1) * 128, t0:t0 + GRP],
                        in_=sxout[ct % 2][:]), 16)

        @block.sync
        def _(sy_):
            ys.wait(sy_)


def _prep_ffn1_inputs(x3T_core, m0t_all, m0_core, ins):
    return {
        "f1xT": x3T_core,
        "m0all": None if m0t_all is None else np.ascontiguousarray(
            m0t_all.reshape(8, 8 * TOK)),
        "m0loc": m0_core,
        "fc1w": ins["L1_ffn_fc1_w"],
        "fc1b": ins["L1_ffn_fc1_b"].reshape(8, 16, 128, 1),
        "fc2w": ins["L1_ffn_fc2_w"],
        "fc2b": ins["L1_ffn_fc2_b"],
        "sh1w": ins["L1_ffn_sh1_w"],
        "sh1b": ins["L1_ffn_sh1_b"].reshape(32, 128, 1),
        "sh2w": ins["L1_ffn_sh2_w"],
        "sh2b": ins["L1_ffn_sh2_b"].reshape(8, 128, 1),
    }



LCAP = 10240
NLT = LCAP // 128
GTOK = 8 * TOK

def _mk_ffn1_bf16(nc, io):
    if "sh1bf" in io:
        return
    io["sh1bf"] = nc.dram_tensor("sh1bf", [4096, D], BF16, kind="Internal").ap()
    io["sh2bf"] = nc.dram_tensor("sh2bf", [D, 4096], BF16, kind="Internal").ap()
    io["myfc1bf"] = nc.dram_tensor("myfc1bf", [2048, D], BF16,
                                   kind="Internal").ap()
    io["myfc2bf"] = nc.dram_tensor("myfc2bf", [D, 2048], BF16,
                                   kind="Internal").ap()


FFN1G_IN = [("f1xT", [D, TOK]), ("myfc1", [2048, D]), ("myfc1b", [16, 128, 1]),
            ("myfc2", [D, 2048]), ("myfc2brow", [1, D]), ("cmask", [1, 8]),
            ("sh1w", [4096, D]), ("sh1b", [32, 128, 1]), ("sh2w", [D, 4096]),
            ("sh2b", [8, 128, 1])]


def _ffn1g_prepass(nc, S, io):
    gs = S["gs"]
    _mk_ffn1_bf16(nc, io)
    if "x3tmLoc" in io:
        return
    with ExitStack() as st:
        block = st.enter_context(nc.Block())

        @block.gpsimd
        def _(g):
            gs.inc(g.dma_start(out=io["myfc1bf"], in_=io["myfc1"]), 16)
            gs.inc(g.dma_start(out=io["myfc2bf"], in_=io["myfc2"]), 16)
            gs.inc(g.dma_start(out=io["sh1bf"], in_=io["sh1w"]), 16)
            gs.inc(g.dma_start(out=io["sh2bf"], in_=io["sh2w"]), 16)


def _stage_ffn1g(nc, S, io, phase=9):
    xT, sh1b, sh2b = io["f1xT"], io["sh1b"], io["sh2b"]
    m0all, x4T, x3bfT = io["m0all"], io["x4T"], io["x3bfT"]
    sh1bf, sh2bf = io["sh1bf"], io["sh2bf"]
    myfc1bf, myfc2bf = io["myfc1bf"], io["myfc2bf"]
    gs, gc, wh, ys, ts, vs, ss, cc = (S["gs"], S["gc"], S["wh"], S["ys"], S["ts"],
                                      S["vs"], S["ss"], S["cc"])
    cc_m0t = cc.n

    premoved = "x3tm_done" in io
    if premoved:
        x3tmLoc, yAll, elist = io["x3tmLoc"], io["yAll"], io["elist"]
    else:
        x3tmLoc = nc.dram_tensor("x3tmLoc", [TOK, D], BF16, kind="Internal").ap()
        elist = nc.dram_tensor("elist", [LCAP + 128 + 32768, 2], F32,
                               kind="Internal").ap()
        yAll = nc.dram_tensor("yAll", [GTOK + 128, D], BF16,
                              kind="Internal").ap()
    x3tmAll = nc.dram_tensor("x3tmAll", [GTOK, D], BF16, kind="Internal").ap()
    xeDram = nc.dram_tensor("xeDram", [LCAP, D], BF16, kind="Internal").ap()
    rsOut = nc.dram_tensor("rsOut", [TOK, D], BF16, kind="Internal").ap()
    shOutT = nc.dram_tensor("shOutT", [D, TOK], BF16, kind="Internal").ap()

    NGRP = TOK // GRP
    NIT = 37
    KPRE = 5

    nc = _NamePrefix(nc, "f1_")
    with ExitStack() as st:
        e = st.enter_context
        sF1T = [e(nc.sbuf_tensor(f"sF1T{i}", [128, 2048], BF16,
                                 is_transpose_dest=True)) for i in range(NCT)]
        sF2T = [e(nc.sbuf_tensor(f"sF2T{i}", [128, 1024], BF16,
                                 is_transpose_dest=True)) for i in range(16)]
        sGat = [e(nc.sbuf_tensor(f"sGat{i}", [128, 1024], BF16,
                                 is_transpose_dest=True)) for i in range(2)]
        sTp = sGat
        sW1 = [[e(nc.sbuf_tensor(f"sW1_{p}_{i}", [128, 512], BF16,
                                 is_transpose_dest=True))
                for i in range(NCT)] for p in range(2)]
        sW2 = [[e(nc.sbuf_tensor(f"sW2_{p}_{i}", [128, 1024], BF16,
                                 is_transpose_dest=True))
                for i in range(4)] for p in range(2)]
        sXe = [e(nc.sbuf_tensor(f"sXe{i}", [128, 512], BF16,
                                is_transpose_dest=True)) for i in range(NCT)]
        sXg = [[e(nc.sbuf_tensor(f"sXg{p}_{i}", [128, 512], BF16))
                for i in range(NCT)] for p in range(2)]
        sYout = [e(nc.sbuf_tensor(f"sYout{i}", [128, 1024], BF16))
                 for i in range(2)]
        sHd = e(nc.sbuf_tensor("sHd", [128, 4 * 512], BF16))
        sacc = e(nc.sbuf_tensor("sacc", [128, 8 * GRP], F32))
        sB2T = e(nc.sbuf_tensor("sB2T", [128, 1024], F32))
        sB2row = e(nc.sbuf_tensor("sB2row", [1, 1024], F32))
        sLstAll = e(nc.sbuf_tensor("sLstAll", [128, 2 * NLT], F32))
        sM01 = e(nc.sbuf_tensor("sM01", [128, NLT], F32))
        sIdT = e(nc.sbuf_tensor("sIdT", [128, NLT], F32))
        sIdxAll_i = e(nc.sbuf_tensor("sIdxAll_i", [128, NLT], I32))
        sIdG_i = e(nc.sbuf_tensor("sIdG_i", [128, NLT], I32))
        sF1bC = e(nc.sbuf_tensor("sF1bC", [128, 16], F32))
        sSh1b = e(nc.sbuf_tensor("sSh1b", [128, 32], F32))
        sSh2b = e(nc.sbuf_tensor("sSh2b", [128, 8], F32))
        sPcol = e(nc.sbuf_tensor("sPcol", [128, 1], F32))
        sGp = e(nc.sbuf_tensor("sGp", [128, 1], F32))
        scm = e(nc.sbuf_tensor("scm", [1, 8], F32))
        scm128 = e(nc.sbuf_tensor("scm128", [128, 8], F32))
        ONESROWF = e(nc.sbuf_tensor("ONESROWF", [1, 128], F32))
        ONESF = e(nc.sbuf_tensor("ONESF", [128, 1], F32))
        ONES1024 = e(nc.sbuf_tensor("ONES1024", [128, 1], BF16))
        sEPS = e(nc.sbuf_tensor("sEPS", [128, 1], F32))
        sxf = [e(nc.sbuf_tensor(f"sxf{i}", [128, 512], F32)) for i in range(2)]
        srs = [e(nc.sbuf_tensor(f"srs{i}", [128, 512], BF16,
                                is_transpose_dest=True)) for i in range(2)]
        sybuf = [e(nc.sbuf_tensor(f"sybuf{i}", [128, 512], BF16))
                 for i in range(2)]
        sy_bf = [e(nc.sbuf_tensor(f"sy_bf{i}", [128, 512], BF16))
                 for i in range(2)]
        sy2 = [e(nc.sbuf_tensor(f"sy2_{i}", [128, 512], BF16)) for i in range(2)]
        s_m = e(nc.sbuf_tensor("s_m", [1, 512], F32))
        s_t1 = e(nc.sbuf_tensor("s_t1", [1, 512], F32))
        s_var = e(nc.sbuf_tensor("s_var", [1, 512], F32))
        s_r = e(nc.sbuf_tensor("s_r", [1, 512], F32))
        s_bt = e(nc.sbuf_tensor("s_bt", [1, 512], F32))
        sln1 = e(nc.sbuf_tensor("sln1", [1, 512], F32))
        sxo = e(nc.sbuf_tensor("sxo", [128, 512], F32))
        sxout = [e(nc.sbuf_tensor(f"sxout{i}", [128, 512], F32)) for i in range(2)]
        pb = [e(nc.psum_tensor(f"pb{i}", [128, 512], F32)) for i in range(8)]
        sZeroV = sF1T[0][:, 0:2048]
        sYtmp = sxo

        block = e(nc.Block())

        @block.gpsimd
        def _(g):
            ys.wait(g)
            vs.wait(g)
            ss.wait(g)
            if not premoved:
                gc.inc(g.memset(sZeroV, 0.0), 1)
                g.wait_ge(gc.h, gc.n)
                for a in range(GTOK // 256):
                    gs.inc(g.dma_start(
                        out=yAll[a * 256:(a + 1) * 256, :].rearrange(
                            "(q p) c -> p q c", p=128),
                        in_=sZeroV.rearrange("p (q c) -> p q c", c=1024)), 16)
        zt = gs.n

        tp_ld, tp_st = {}, {}
        if not premoved:
            @block.sync
            def _(sy_):
                ys.wait(sy_)
                vs.wait(sy_)
                NT32 = TOK // 128
                for i in range(NT32):
                    if i >= 2:
                        sy_.wait_ge(ys.h, tp_st[i - 2])
                    wh.inc(sy_.dma_start(
                        out=sTp[i % 2][:],
                        in_=x3bfT[0:1024, i * 128:(i + 1) * 128],
                        transpose=True), 16)
                    tp_ld[i] = wh.n
                    if i >= 1:
                        sy_.wait_ge(wh.h, tp_ld[i - 1])
                        ys.inc(sy_.dma_start(
                            out=x3tmLoc[(i - 1) * 128:i * 128, :],
                            in_=sTp[(i - 1) % 2][:]), 16)
                        tp_st[i - 1] = ys.n
                sy_.wait_ge(wh.h, tp_ld[NT32 - 1])
                ys.inc(sy_.dma_start(
                    out=x3tmLoc[(NT32 - 1) * 128:NT32 * 128, :],
                    in_=sTp[(NT32 - 1) % 2][:]), 16)
                tp_st[NT32 - 1] = ys.n

        @block.sync
        def _(sy_):
            ys.wait(sy_)
            sy_.wait_ge(gs.h, zt)
            for ct in range(NCT):
                for hs in range(4):
                    wh.inc(sy_.dma_start(
                        out=sF1T[ct][:, hs * 512:(hs + 1) * 512],
                        in_=myfc1bf[hs * 512:(hs + 1) * 512,
                                    ct * 128:(ct + 1) * 128],
                        transpose=True), 16)
            for hj in range(16):
                wh.inc(sy_.dma_start(
                    out=sF2T[hj][:],
                    in_=myfc2bf[0:1024, hj * 128:(hj + 1) * 128],
                    transpose=True), 16)
            for j in range(32):
                ys.inc(sy_.dma_start(out=sSh1b[:, j:j + 1], in_=sh1b[j]), 16)
        bias_ld = ys.n

        def emit_prefetch(sy_, hs):
            p = hs % 2
            for ct in range(NCT):
                wh.inc(sy_.dma_start(
                    out=sW1[p][ct][:],
                    in_=sh1bf[hs * 512:(hs + 1) * 512, ct * 128:(ct + 1) * 128],
                    transpose=True), 16)
            for j in range(4):
                wh.inc(sy_.dma_start(
                    out=sW2[p][j][:],
                    in_=sh2bf[0:1024, hs * 512 + j * 128:hs * 512 + (j + 1) * 128],
                    transpose=True), 16)

        NU = 8
        wsr = {}
        xg_done = {}
        sh_hmm = {}
        sh_gelu = {}
        sh_fc2 = {}
        sh_evac = {}
        sh_store = {}
        sh_sdone = {}

        @block.sync
        def _(sy_):
            sy_.wait_ge(gs.h, zt)
            emit_prefetch(sy_, 0)
            for i in range(NCT):
                ys.inc(sy_.dma_start(
                    out=sXg[0][i][:],
                    in_=x3bfT[i * 128:(i + 1) * 128, 0:GRP]), 16)
        wsr[(0, 0)] = wh.n
        xg_done[0] = ys.n

        def emit_shared_group(grp):
            for u in range(NU):
                p = u % 2

                @block.sync
                def _(sy_, grp=grp, u=u):
                    tdone = sh_fc2.get((grp, u - 1, NCT - 1),
                                       sh_fc2.get((grp - 1, NU - 1, NCT - 1), 0))
                    sy_.wait_ge(ts.h, tdone)
                    if u + 1 < NU:
                        emit_prefetch(sy_, u + 1)
                        wsr[(grp, u + 1)] = wh.n
                    elif grp + 1 < NGRP:
                        emit_prefetch(sy_, 0)
                        wsr[(grp + 1, 0)] = wh.n
                    if u == 0 and grp + 1 < NGRP:
                        for i in range(NCT):
                            ys.inc(sy_.dma_start(
                                out=sXg[(grp + 1) % 2][i][:],
                                in_=x3bfT[i * 128:(i + 1) * 128,
                                          (grp + 1) * GRP:(grp + 2) * GRP]), 16)
                        xg_done[grp + 1] = ys.n

                @block.tensor
                def _(t, grp=grp, u=u, p=p):
                    t.wait_ge(wh.h, wsr[(grp, u)])
                    if u == 0:
                        t.wait_ge(ys.h, xg_done[grp])
                    gprev = sh_gelu.get((grp, u - 1),
                                        sh_gelu.get((grp - 1, NU - 1), 0))
                    if gprev:
                        t.wait_ge(ss.h, gprev)
                    for j in range(4):
                        mm_chain(t, pb[j][:],
                                 [(sW1[p][ci][:, j * 128:(j + 1) * 128],
                                   sXg[grp % 2][ci][:]) for ci in range(NCT)],
                                 ts if j == 3 else None)
                sh_hmm[(grp, u)] = ts.n

                @block.scalar
                def _(s, grp=grp, u=u):
                    s.wait_ge(ts.h, sh_hmm[(grp, u)])
                    if grp == 0 and u == 0:
                        s.wait_ge(ys.h, bias_ld)
                    f2prev = sh_fc2.get((grp, u - 1, NCT - 1),
                                        sh_fc2.get((grp - 1, NU - 1, NCT - 1), 0))
                    if f2prev:
                        s.wait_ge(ts.h, f2prev)
                    for j in range(4):
                        last = s.activation(sHd[:, j * 512:(j + 1) * 512],
                                            pb[j][:], AF.Gelu,
                                            bias=sSh1b[:, u * 4 + j:u * 4 + j + 1])
                    ss.inc(last)
                sh_gelu[(grp, u)] = ss.n

                for ct in range(NCT):
                    @block.tensor
                    def _(t, grp=grp, u=u, ct=ct, p=p):
                        if ct == 0:
                            t.wait_ge(ss.h, sh_gelu[(grp, u)])
                        ev = sh_evac.get((grp, u, ct - 2),
                                         sh_evac.get((grp, u - 1, NCT - 2 + ct),
                                                     sh_evac.get((grp - 1, NU - 1,
                                                                  NCT - 2 + ct), 0)))
                        if ev:
                            t.wait_ge(vs.h, ev)
                        mm_chain(t, pb[4 + ct % 2][:],
                                 [(sW2[p][j][:, ct * 128:(ct + 1) * 128],
                                   sHd[:, j * 512:(j + 1) * 512])
                                  for j in range(4)], ts)
                    sh_fc2[(grp, u, ct)] = ts.n

                    @block.vector
                    def _(v, grp=grp, u=u, ct=ct):
                        v.wait_ge(ts.h, sh_fc2[(grp, u, ct)])
                        sl = slice(ct * GRP, (ct + 1) * GRP)
                        if u == 0:
                            vs.inc(v.tensor_copy(sacc[:, sl], pb[4 + ct % 2][:]))
                        else:
                            vs.inc(v.tensor_tensor(sacc[:, sl], sacc[:, sl],
                                                   pb[4 + ct % 2][:], ALU.add))
                    sh_evac[(grp, u, ct)] = vs.n

            for ct in range(NCT):
                @block.vector
                def _(v, grp=grp, ct=ct):
                    pv = sh_store.get((grp, ct - 2),
                                      sh_store.get((grp - 1, NCT - 2 + ct)))
                    if pv:
                        v.wait_ge(ys.h, pv)
                    vs.inc(v.tensor_copy(sybuf[ct % 2][:],
                                         sacc[:, ct * GRP:(ct + 1) * GRP]))
                stv = vs.n

                @block.sync
                def _(sy_, grp=grp, ct=ct, stv=stv):
                    sy_.wait_ge(vs.h, stv)
                    ys.inc(sy_.dma_start(
                        out=shOutT[ct * 128:(ct + 1) * 128,
                                   grp * GRP:(grp + 1) * GRP],
                        in_=sybuf[ct % 2][:]), 16)
                sh_store[(grp, ct)] = ys.n
            sh_sdone[grp] = vs.n

        for grp in range(KPRE):
            emit_shared_group(grp)

        if phase < 2:
            @block.sync
            def _(sy_):
                ys.wait(sy_)
            return
        with ExitStack() as stA:
            eA = stA.enter_context
            sMe = eA(nc.sbuf_tensor("sMe", [128, 8 * 256], F32))
            sjunk = eA(nc.sbuf_tensor("sjunk", [128, 256], F32))
            sids = eA(nc.sbuf_tensor("sids", [128, 256], F32))
            szer = eA(nc.sbuf_tensor("szer", [128, 256], F32))
            smask = eA(nc.sbuf_tensor("smask", [128, 256], F32))
            smsel = eA(nc.sbuf_tensor("smsel", [128, 256], F32))
            stmp = eA(nc.sbuf_tensor("stmp", [128, 256], F32))
            spairs = eA(nc.sbuf_tensor("spairs", [128, 512], F32))
            sslot_i = eA(nc.sbuf_tensor("sslot_i", [128, 256], I32))
            sLT = eA(nc.sbuf_tensor("sLT", [128, 128], F32))
            siota2 = eA(nc.sbuf_tensor("siota2", [128, 128], F32))
            sbase = eA(nc.sbuf_tensor("sbase", [128, 1], F32))
            scnt = eA(nc.sbuf_tensor("scnt", [128, 8], F32))
            sth = eA(nc.sbuf_tensor("sth", [1, 8], F32))
            sthk = eA(nc.sbuf_tensor("sthk", [1, 8], F32))
            sgt = eA(nc.sbuf_tensor("sgt", [1, 8], F32))
            sok = eA(nc.sbuf_tensor("sok", [1, 8], F32))
            st1 = eA(nc.sbuf_tensor("st1", [1, 8], F32))
            sthc = eA(nc.sbuf_tensor("sthc", [128, 8], F32))
            st8 = eA(nc.sbuf_tensor("st8", [128, 8], F32))
            sthv = eA(nc.sbuf_tensor("sthv", [128, 1], F32))

            @block.gpsimd
            def _(g):
                g.wait_ge(cc.h, cc_m0t)
                for ei in range(8):
                    for r in range(8):
                        gs.inc(g.dma_start(
                            out=sMe[:, ei * 256:(ei + 1) * 256][
                                r * 16:(r + 1) * 16, :],
                            in_=m0all[r * 8 + ei].rearrange("(p n) -> p n", p=16)),
                            16)
                for hj in range(16):
                    gs.inc(g.dma_start(out=sF1bC[:, hj:hj + 1],
                                       in_=io["myfc1b"][hj]), 16)
                for j in range(8):
                    gs.inc(g.dma_start(out=sSh2b[:, j:j + 1], in_=sh2b[j]), 16)
                gs.inc(g.dma_start(out=sB2row[:], in_=io["myfc2brow"]), 16)
                gs.inc(g.dma_start(out=scm[:], in_=io["cmask"]), 16)
                gc.inc(g.iota(sids[:], [[1, 256]], base=0, channel_multiplier=256,
                              allow_small_or_imprecise_dtypes=True), 1)
                gc.inc(g.iota(siota2[:], [[1, 128]], base=0, channel_multiplier=-1,
                              allow_small_or_imprecise_dtypes=True), 1)
                gc.inc(g.iota(sPcol[:], [[0, 1]], base=0, channel_multiplier=1,
                              allow_small_or_imprecise_dtypes=True), 1)
            ld_gs = gs.n

            @block.gpsimd
            def _(g):
                if premoved:
                    g.wait_ge(ys.h, io["x3tm_done"])
                else:
                    g.wait_ge(ys.h, tp_st[TOK // 128 - 1])
                cc.inc(g.collective_compute(
                    "AllGather", ALU.bypass, replica_groups=[list(range(8))],
                    ins=[x3tmLoc.opt()], outs=[x3tmAll.opt()]), 1)
            ag2 = cc.n

            @block.vector
            def _(v):
                v.wait_ge(gs.h, ld_gs)
                gc.wait(v)
                v.tensor_scalar(sLT[:], siota2[:], 0.0, None, ALU.is_gt)
                v.memset(ONESROWF[:], 1.0)
                v.memset(ONESF[:], 1.0)
                v.memset(ONES1024[:], 1.0 / 1024)
                v.memset(sEPS[:], EPS)
                v.memset(sth[:], 0.5)
                v.memset(sthk[:], 1.1)
                v.memset(szer[:], 0.0)
                vs.inc(v.memset(spairs[:], 0.0))

            if not premoved:
                @block.sync
                def _(sy_):
                    vs.wait(sy_)
                    ys.inc(sy_.dma_start(
                        out=elist[0:LCAP + 128, :].rearrange(
                            "(p q) w -> p (q w)", p=128),
                        in_=spairs[:, 0:2 * (LCAP + 128) // 128]), 16)
            pf = ys.n

            @block.tensor
            def _(t):
                vs.wait(t)
                t.matmul(pb[6][:, 0:8], ONESROWF[:], scm[:], start=True, stop=True)
                ts.inc(t.matmul(pb[7][:], ONESROWF[:], sB2row[0:1, 0:512],
                                start=True, stop=True))

            @block.vector
            def _(v):
                ts.wait(v)
                vs.fence(v, v.tensor_copy(scm128[:], pb[6][:, 0:8]))
                vs.inc(v.tensor_copy(sB2T[:, 0:512], pb[7][:]))

            @block.tensor
            def _(t):
                vs.wait(t)
                ts.inc(t.matmul(pb[6][:], ONESROWF[:], sB2row[0:1, 512:1024],
                                start=True, stop=True))

            @block.vector
            def _(v):
                ts.wait(v)
                vs.inc(v.tensor_copy(sB2T[:, 512:1024], pb[6][:]))

            dk = 0.25
            for it in range(NIT):
                @block.vector
                def _(v, it=it):
                    if it == 0:
                        vs.wait(v)
                    else:
                        ts.wait(v)
                        vs.fence(v, v.tensor_copy(sthc[:], pb[7][:, 0:8]))
                    for ei in range(8):
                        last = v.tensor_scalar(
                            sjunk[:], sMe[:, ei * 256:(ei + 1) * 256],
                            sthc[:, ei:ei + 1] if it > 0 else 0.5,
                            0.0, ALU.is_ge, ALU.add,
                            accum_out=scnt[:, ei:ei + 1])
                    vs.inc(last)

                @block.tensor
                def _(t):
                    vs.wait(t)
                    ts.inc(t.matmul(pb[6][0:1, 0:8], ONESF[:], scnt[:],
                                    start=True, stop=True))

                @block.vector
                def _(v, it=it):
                    ts.wait(v)
                    vs.fence(v, v.tensor_scalar(sgt[:], pb[6][0:1, 0:8], CAP,
                                                None, ALU.is_gt))
                    vs.fence(v, v.tensor_scalar(sok[:], sgt[:], -1.0, 1.0,
                                                ALU.mult, ALU.add))
                    vs.fence(v, v.tensor_tensor(st1[:], sok[:], sth[:], ALU.mult))
                    vs.fence(v, v.tensor_tensor(sthk[:], sgt[:], sthk[:], ALU.mult))
                    vs.fence(v, v.tensor_tensor(sthk[:], sthk[:], st1[:], ALU.add))
                    d = dk / 2
                    vs.fence(v, v.scalar_tensor_tensor(st1[:], sgt[:], 2.0 * d,
                                                       sth[:], ALU.mult, ALU.add))
                    vs.inc(v.tensor_scalar(sth[:], st1[:], -d, None, ALU.add))
                dk = dk / 2

                @block.tensor
                def _(t, it=it):
                    vs.wait(t)
                    src = sthk if it == NIT - 1 else sth
                    ts.inc(t.matmul(pb[7][:, 0:8], ONESROWF[:], src[:],
                                    start=True, stop=True))

            @block.vector
            def _(v):
                ts.wait(v)
                vs.fence(v, v.tensor_copy(sthc[:], pb[7][:, 0:8]))
                vs.fence(v, v.tensor_tensor(st8[:], sthc[:], scm128[:], ALU.mult))
                vs.fence(v, v.reduce_sum(sthv[:], st8[:],
                                         axis=mybir.AxisListType.X))
                for ei in range(8):
                    if ei == 0:
                        vs.fence(v, v.tensor_scalar(
                            smsel[:], sMe[:, 0:256], scm128[:, 0:1], None,
                            ALU.mult))
                    else:
                        vs.fence(v, v.tensor_scalar(
                            stmp[:], sMe[:, ei * 256:(ei + 1) * 256],
                            scm128[:, ei:ei + 1], None, ALU.mult))
                        vs.fence(v, v.tensor_tensor(smsel[:], smsel[:], stmp[:],
                                                    ALU.add))
                vs.fence(v, v.tensor_scalar(smask[:], smsel[:], sthv[:, 0:1],
                                            None, ALU.is_ge))
                vs.fence(v, v.tensor_tensor(smsel[:], smsel[:], smask[:],
                                            ALU.mult))
                vs.inc(v.tensor_tensor_scan(sjunk[:], smask[:], szer[:], 0.0,
                                            ALU.add, ALU.add))

            @block.tensor
            def _(t):
                vs.wait(t)
                ts.inc(t.matmul(pb[6][:, 0:1], sLT[:], sjunk[:, 255:256],
                                start=True, stop=True))

            @block.vector
            def _(v):
                ts.wait(v)
                vs.fence(v, v.tensor_copy(sbase[:], pb[6][:, 0:1]))
                vs.fence(v, v.tensor_tensor(sjunk[:], sjunk[:], smask[:],
                                            ALU.subtract))
                vs.fence(v, v.tensor_scalar(sjunk[:], sjunk[:], sbase[:, 0:1],
                                            1.0, ALU.add, ALU.add))
                vs.fence(v, v.tensor_scalar(stmp[:], sids[:], float(LCAP + 1),
                                            None, ALU.add))
                vs.fence(v, v.tensor_tensor(sjunk[:], sjunk[:], stmp[:],
                                            ALU.subtract))
                vs.fence(v, v.tensor_tensor(sjunk[:], sjunk[:], smask[:],
                                            ALU.mult))
                vs.fence(v, v.tensor_tensor(sjunk[:], sjunk[:], stmp[:],
                                            ALU.add))
                vs.fence(v, v.tensor_copy(sslot_i[:], sjunk[:]))
                v.wait_ge(ys.h, pf)
                pview = spairs[:].rearrange("p (n w) -> p n w", w=2)
                vs.fence(v, v.tensor_copy(
                    pview[:, :, 0:1],
                    sids[:].rearrange("p (n o) -> p n o", o=1)))
                vs.inc(v.tensor_copy(
                    pview[:, :, 1:2],
                    smsel[:].rearrange("p (n o) -> p n o", o=1)))

            @block.gpsimd
            def _(g):
                vs.wait(g)
                for n in range(256):
                    gs.inc(g.indirect_dma_start(
                        out=elist,
                        out_offset=bass.IndirectOffsetOnAxis(sslot_i[:, n:n + 1], 0),
                        in_=spairs[:, 2 * n:2 * n + 2], in_offset=None), 16)
            sc256 = gs.n

            @block.gpsimd
            def _(g):
                g.wait_ge(gs.h, sc256)
                for t in range(NLT):
                    gs.inc(g.dma_start(
                        out=sLstAll[:, 2 * t:2 * t + 2],
                        in_=elist[1 + 128 * t:1 + 128 * (t + 1), :]), 16)
            lst = gs.n

            @block.vector
            def _(v):
                v.wait_ge(gs.h, lst)
                lv = sLstAll[:].rearrange("p (t w) -> p t w", w=2)
                m3 = sM01[:].rearrange("p (t o) -> p t o", o=1)
                i3 = sIdT[:].rearrange("p (t o) -> p t o", o=1)
                vs.fence(v, v.tensor_scalar(m3, lv[:, :, 1:2], 0.0, None,
                                            ALU.is_gt))
                vs.fence(v, v.tensor_scalar(sGp[:], sPcol[:], float(GTOK), None,
                                            ALU.add))
                vs.fence(v, v.tensor_scalar(i3, lv[:, :, 0:1], sGp[:, 0:1], None,
                                            ALU.subtract))
                vs.fence(v, v.tensor_tensor(sIdT[:], sIdT[:], sM01[:], ALU.mult))
                vs.fence(v, v.tensor_scalar(sIdT[:], sIdT[:], sGp[:, 0:1], None,
                                            ALU.add))
                vs.fence(v, v.tensor_copy(sIdxAll_i[:], sIdT[:]))
                vs.inc(v.tensor_copy(
                    sIdG_i[:].rearrange("p (t o) -> p t o", o=1), lv[:, :, 0:1]))
            idxv = vs.n

        if phase < 3:
            @block.sync
            def _(sy_):
                ys.wait(sy_)
            return
        sHe = e(nc.sbuf_tensor("sHe", [128, 16 * 512], BF16))

        gat, xe, xst = {}, {}, {}
        fc1mm, gelu_ss, fc2ts, evac_vs, sc_done = {}, {}, {}, {}, {}

        def emit_expert_blk(blk):
            for pr in range(2):
                @block.gpsimd
                def _(g, blk=blk, pr=pr):
                    if blk == 0 and pr == 0:
                        g.wait_ge(cc.h, ag2)
                        g.wait_ge(vs.h, idxv)
                    for t in range(4 * blk + 2 * pr, 4 * blk + 2 * pr + 2):
                        if t >= 2:
                            g.wait_ge(ys.h, xst[t - 2])
                        gs.inc(g.indirect_dma_start(
                            out=sGat[t % 2][:], out_offset=None, in_=x3tmAll,
                            in_offset=bass.IndirectOffsetOnAxis(
                                sIdG_i[:, t:t + 1], 0)), 16)
                        gat[t] = gs.n

                @block.sync
                def _(sy_, blk=blk, pr=pr):
                    for t in range(4 * blk + 2 * pr, 4 * blk + 2 * pr + 2):
                        sy_.wait_ge(gs.h, gat[t])
                        ys.inc(sy_.dma_start(
                            out=xeDram[t * 128:(t + 1) * 128, :],
                            in_=sGat[t % 2][:]), 16)
                        xst[t] = ys.n

            @block.sync
            def _(sy_, blk=blk):
                sy_.wait_ge(ys.h, xst[4 * blk + 3])
                if blk >= 1:
                    sy_.wait_ge(ts.h, fc1mm[(blk - 1, 15)])
                for ct in range(NCT):
                    wh.inc(sy_.dma_start(
                        out=sXe[ct][:],
                        in_=xeDram[blk * 512:(blk + 1) * 512,
                                   ct * 128:(ct + 1) * 128], transpose=True), 16)
                xe[blk] = wh.n

            for hj in range(16):
                @block.tensor
                def _(t, blk=blk, hj=hj):
                    if hj == 0:
                        t.wait_ge(wh.h, xe[blk])
                        pv = evac_vs.get(4 * blk - 1)
                        if pv:
                            t.wait_ge(vs.h, pv)
                    g2 = gelu_ss.get((blk, hj - 2),
                                     gelu_ss.get((blk - 1, 14 + hj)))
                    if g2:
                        t.wait_ge(ss.h, g2)
                    mm_chain(t, pb[6 + hj % 2][:],
                             [(sF1T[ct][:, hj * 128:(hj + 1) * 128],
                               sXe[ct][:]) for ct in range(NCT)], ts)
                fc1mm[(blk, hj)] = ts.n

                @block.scalar
                def _(s, blk=blk, hj=hj):
                    s.wait_ge(ts.h, fc1mm[(blk, hj)])
                    ss.inc(s.activation(sHe[:, hj * 512:(hj + 1) * 512],
                                        pb[6 + hj % 2][:], AF.Gelu,
                                        bias=sF1bC[:, hj:hj + 1]))
                gelu_ss[(blk, hj)] = ss.n

            for tsl in range(4):
                tg = 4 * blk + tsl

                @block.tensor
                def _(t, blk=blk, tsl=tsl, tg=tg):
                    if tsl == 0:
                        t.wait_ge(ss.h, gelu_ss[(blk, 15)])
                    pv = evac_vs.get(tg - 1)
                    if pv:
                        t.wait_ge(vs.h, pv)
                    for half in range(2):
                        mm_chain(t, pb[6 + half][:],
                                 [(sHe[:, hj * 512 + tsl * 128:
                                       hj * 512 + tsl * 128 + 128],
                                   sF2T[hj][:, half * 512:(half + 1) * 512])
                                  for hj in range(16)],
                                 ts if half == 1 else None)
                fc2ts[tg] = ts.n

                @block.vector
                def _(v, tg=tg):
                    v.wait_ge(ts.h, fc2ts[tg])
                    pg = sc_done.get(tg - 2)
                    if pg:
                        v.wait_ge(gs.h, pg)
                    w_ap = sLstAll[:, 2 * tg + 1:2 * tg + 2]
                    vs.fence(v, v.tensor_tensor(sYtmp[:], pb[6][:],
                                                sB2T[:, 0:512], ALU.add))
                    vs.fence(v, v.tensor_scalar(sYout[tg % 2][:, 0:512],
                                                sYtmp[:], w_ap, None, ALU.mult))
                    vs.fence(v, v.tensor_tensor(sYtmp[:], pb[7][:],
                                                sB2T[:, 512:1024], ALU.add))
                    vs.inc(v.tensor_scalar(sYout[tg % 2][:, 512:1024], sYtmp[:],
                                           w_ap, None, ALU.mult))
                evac_vs[tg] = vs.n

                @block.gpsimd
                def _(g, tg=tg):
                    g.wait_ge(vs.h, evac_vs[tg])
                    gs.inc(g.indirect_dma_start(
                        out=yAll,
                        out_offset=bass.IndirectOffsetOnAxis(
                            sIdxAll_i[:, tg:tg + 1], 0),
                        in_=sYout[tg % 2][:], in_offset=None), 16)
                sc_done[tg] = gs.n

        sh_at = {}
        for k in range(KPRE, NGRP):
            sh_at[2 + 7 * (k - KPRE)] = k
        for blk in range(NLT // 4):
            if blk in sh_at:
                emit_shared_group(sh_at[blk])
            emit_expert_blk(blk)

        @block.gpsimd
        def _(g):
            g.wait_ge(gs.h, sc_done[NLT - 1])
            cc.inc(g.collective_compute(
                "ReduceScatter", ALU.add, replica_groups=[list(range(8))],
                ins=[yAll[0:GTOK, :].opt()], outs=[rsOut.opt()]), 1)
        rs_done = cc.n
        rs_half = cc.n

        if "rsDbg" in io:
            @block.gpsimd
            def _(g):
                g.wait_ge(cc.h, rs_done)
                gs.inc(g.dma_start(out=io["rsDbg"], in_=rsOut), 16)
                gs.inc(g.dma_start(out=io["yDbg"], in_=yAll), 16)

        ld_done, rsadd_vs, stats_ts = {}, {}, {}
        for grp in range(NGRP):
            t0 = grp * GRP
            for ct in range(NCT):
                @block.sync
                def _(sy_, ct=ct, t0=t0, grp=grp):
                    if grp == 0 and ct == 0:
                        sy_.wait_ge(cc.h, rs_half)
                    if grp == 0 and ct == NCT // 2:
                        sy_.wait_ge(cc.h, rs_done)
                    pv = rsadd_vs.get((grp, ct - 2),
                                      rsadd_vs.get((grp - 1, NCT - 2 + ct)))
                    if pv:
                        sy_.wait_ge(vs.h, pv)
                    ys.inc(sy_.dma_start(
                        out=sybuf[ct % 2][:],
                        in_=shOutT[ct * 128:(ct + 1) * 128, t0:t0 + GRP]), 16)
                    ys.inc(sy_.dma_start(
                        out=sxf[ct % 2][:],
                        in_=xT[ct * 128:(ct + 1) * 128, t0:t0 + GRP]), 16)
                    ys.inc(sy_.dma_start(
                        out=srs[ct % 2][:],
                        in_=rsOut[t0:t0 + GRP, ct * 128:(ct + 1) * 128],
                        transpose=True), 16)
                ld_done[(grp, ct)] = ys.n

                @block.vector
                def _(v, ct=ct, grp=grp):
                    v.wait_ge(ys.h, ld_done[(grp, ct)])
                    st_prev = stats_ts.get((grp, ct - 2),
                                           stats_ts.get((grp - 1, NCT - 2 + ct)))
                    if st_prev:
                        v.wait_ge(ts.h, st_prev)
                    sl = slice(ct * GRP, (ct + 1) * GRP)
                    vs.fence(v, v.tensor_tensor(sacc[:, sl], sybuf[ct % 2][:],
                                                srs[ct % 2][:], ALU.add))
                    vs.fence(v, v.tensor_scalar(sxo[:], sacc[:, sl],
                                                sSh2b[:, ct:ct + 1], None,
                                                ALU.add))
                    v.tensor_tensor(sacc[:, sl], sxo[:], sxf[ct % 2][:], ALU.add)
                    vs.inc(v.tensor_copy(sy_bf[ct % 2][:], sacc[:, sl]))
                rsadd_vs[(grp, ct)] = vs.n

                @block.scalar
                def _(s, ct=ct):
                    vs.wait(s)
                    ss.inc(s.square(sy2[ct % 2][:], sy_bf[ct % 2][:]))

                @block.tensor
                def _(t, ct=ct, grp=grp):
                    ss.wait(t)
                    t.matmul(pb[0][0:1, :], ONES1024[:], sy_bf[ct % 2][:],
                             start=(ct == 0), stop=(ct == NCT - 1))
                    ts.inc(t.matmul(pb[1][0:1, :], ONES1024[:], sy2[ct % 2][:],
                                    start=(ct == 0), stop=(ct == NCT - 1)))
                stats_ts[(grp, ct)] = ts.n

            @block.vector
            def _(v):
                ts.wait(v)
                vs.fence(v, v.tensor_copy(s_m[:], pb[0][0:1, :]))
                vs.fence(v, v.tensor_tensor(s_t1[:], s_m[:], s_m[:], ALU.mult))
                vs.inc(v.tensor_tensor(s_var[:], pb[1][0:1, :], s_t1[:],
                                       ALU.subtract))

            @block.scalar
            def _(s):
                vs.wait(s)
                s.activation(sln1[:], s_var[:], AF.Ln, bias=sEPS[0:1, 0:1])
                ss.inc(s.activation(s_r[:], sln1[:], AF.Exp, scale=-0.5))

            @block.vector
            def _(v):
                ss.wait(v)
                vs.inc(v.scalar_tensor_tensor(s_bt[:], s_m[:], -1.0, s_r[:],
                                              ALU.mult, ALU.mult))

            @block.tensor
            def _(t):
                vs.wait(t)
                t.matmul(pb[2][:], ONESROWF[:], s_r[:], start=True, stop=True)
                ts.inc(t.matmul(pb[3][:], ONESROWF[:], s_bt[:], start=True,
                                stop=True))

            for ct in range(NCT):
                @block.vector
                def _(v, ct=ct, grp=grp):
                    if ct == 0:
                        ts.wait(v)
                    if not (grp == 0 and ct < 2):
                        ys.wait(v)
                    v.tensor_tensor(sxo[:], sacc[:, ct * GRP:(ct + 1) * GRP],
                                    pb[2][:], ALU.mult)
                    vs.inc(v.tensor_tensor(sxout[ct % 2][:], sxo[:], pb[3][:],
                                           ALU.add))

                @block.sync
                def _(sy_, ct=ct, t0=t0):
                    vs.wait(sy_)
                    ys.inc(sy_.dma_start(
                        out=x4T[ct * 128:(ct + 1) * 128, t0:t0 + GRP],
                        in_=sxout[ct % 2][:]), 16)

        @block.sync
        def _(sy_):
            ys.wait(sy_)


def build_ffn1g():
    nc = bass.Bass("TRN2", target_bir_lowering=False, detect_race_conditions=False)
    io = {n: nc.dram_tensor(n, sh, F32, kind="ExternalInput").ap()
          for n, sh in FFN1G_IN}
    io["m0all"] = nc.dram_tensor("m0all", [64, TOK], F32, kind="ExternalInput").ap()
    io["x3bfT"] = nc.dram_tensor("x3bfT", [D, TOK], BF16,
                                 kind="ExternalInput").ap()
    io["x4T"] = nc.dram_tensor("x4T", [D, TOK], F32, kind="ExternalOutput").ap()
    with ExitStack() as st0:
        S = make_sems(nc, st0)
        _ffn1g_prepass(nc, S, io)
        _stage_ffn1g(nc, S, io)
    return nc


def _prep_ffn1g_inputs(core, ins):
    return {
        "myfc1": ins["L1_ffn_fc1_w"][core],
        "myfc1b": ins["L1_ffn_fc1_b"][core].reshape(16, 128, 1),
        "myfc2": ins["L1_ffn_fc2_w"][core],
        "myfc2brow": ins["L1_ffn_fc2_b"][core].reshape(1, D),
        "cmask": np.eye(8, dtype=np.float32)[core].reshape(1, 8),
        "sh1w": ins["L1_ffn_sh1_w"],
        "sh1b": ins["L1_ffn_sh1_b"].reshape(32, 128, 1),
        "sh2w": ins["L1_ffn_sh2_w"],
        "sh2b": ins["L1_ffn_sh2_b"].reshape(8, 128, 1),
    }


def _stage_collective(nc, S, m0t_loc, ccout):
    with ExitStack() as st:
        block = st.enter_context(nc.Block())

        @block.gpsimd
        def _(g):
            S["ys"].wait(g)
            S["cc"].inc(g.collective_compute(
                "AllGather", ALU.bypass, replica_groups=[list(range(8))],
                ins=[m0t_loc.opt()], outs=[ccout.opt()]), 1)


def build_fused():
    nc = bass.Bass("TRN2", target_bir_lowering=False, detect_race_conditions=False)
    io = {}
    for n, sh in GLA_IN + FFN0_IN[1:] + MHA_IN[1:] + FFN1G_IN[1:]:
        io[n] = nc.dram_tensor(n, sh, F32, kind="ExternalInput").ap()
    x1T = nc.dram_tensor("x1T", [D, TOK], F32, kind="Internal").ap()
    x2T = nc.dram_tensor("x2T", [D, TOK], F32, kind="Internal").ap()
    x3T = nc.dram_tensor("x3T", [D, TOK], F32, kind="Internal").ap()
    m0 = nc.dram_tensor("m0", [TOK, 8], F32, kind="Internal").ap()
    m0t = nc.dram_tensor("m0t", [8, TOK], F32, kind="Internal").ap()
    ccout = nc.dram_tensor("ccout", [64, TOK], F32, kind="Internal").ap()
    x3bfT = nc.dram_tensor("x3bfT", [D, TOK], BF16, kind="Internal").ap()
    io["x3bfT"] = x3bfT
    io["x3tmLoc"] = nc.dram_tensor("x3tmLoc", [TOK, D], BF16,
                                   kind="Internal").ap()
    io["yAll"] = nc.dram_tensor("yAll", [GTOK + 128, D], BF16,
                                kind="Internal").ap()
    io["elist"] = nc.dram_tensor("elist", [LCAP + 128 + 32768, 2], F32,
                                 kind="Internal").ap()
    _mk_ffn1_bf16(nc, io)
    io["x1T"] = x1T
    io["f0xT"] = x1T
    io["x2T"] = x2T
    io["mxT"] = x2T
    io["x3T"] = x3T
    io["f1xT"] = x3T
    io["m0"] = m0
    io["m0t"] = m0t
    io["m0all"] = ccout
    io["x4T"] = nc.dram_tensor("x4T", [D, TOK], F32, kind="ExternalOutput").ap()
    with ExitStack() as st0:
        S = make_sems(nc, st0)
        _stage_gla(nc, S, io)
        _stage_ffn0(nc, S, io)
        _stage_mha(nc, S, io)
        _ffn1g_prepass(nc, S, io)
        _stage_collective(nc, S, m0t, ccout)
        _stage_ffn1g(nc, S, io)
    return nc


def _prep_fused(x_shard, core, ins):
    d = {}
    d.update(_prep_gla_inputs(x_shard, ins))
    d.update({k: v for k, v in _prep_ffn0_inputs(None, ins).items() if k != "f0xT"})
    d.update({k: v for k, v in _prep_mha_inputs(None, ins).items() if k != "mxT"})
    d.update(_prep_ffn1g_inputs(core, ins))
    return d


def _to_xT(x_shard):
    return np.ascontiguousarray(
        np.concatenate([x_shard[0].T, x_shard[1].T], axis=1)).astype(np.float32)


def _from_xT(xT):
    return np.stack([xT[:, 0:TB].T, xT[:, TB:2 * TB].T])


def kernel(**inputs):
    ins = inputs
    x = np.asarray(ins["L0_attn_x"], dtype=np.float32)

    nc = build_fused()
    in_maps = []
    for c in range(8):
        d = _prep_fused(x[2 * c:2 * c + 2], c, ins)
        in_maps.append(d)
    res = run_bass_kernel_spmd(nc, in_maps, core_ids=list(range(8)))
    out = np.concatenate([_from_xT(np.asarray(res.results[c]["x4T"])) for c in range(8)],
                         axis=0)
    return np.ascontiguousarray(out).astype(np.float32)


def kernel_unfused(**inputs):
    ins = inputs
    x = np.asarray(ins["L0_attn_x"], dtype=np.float32)

    nc = build_gla()
    in_maps = [_prep_gla_inputs(x[2 * c:2 * c + 2], ins) for c in range(8)]
    res = run_bass_kernel_spmd(nc, in_maps, core_ids=list(range(8)))
    x1T = [np.asarray(res.results[c]["x1T"]) for c in range(8)]

    nc = build_ffn0()
    in_maps = [_prep_ffn0_inputs(x1T[c], ins) for c in range(8)]
    res = run_bass_kernel_spmd(nc, in_maps, core_ids=list(range(8)))
    x2T = [np.asarray(res.results[c]["x2T"]) for c in range(8)]

    nc = build_mha()
    in_maps = [_prep_mha_inputs(x2T[c], ins) for c in range(8)]
    res = run_bass_kernel_spmd(nc, in_maps, core_ids=list(range(8)))
    x3T = [np.asarray(res.results[c]["x3T"]) for c in range(8)]
    m0 = [np.asarray(res.results[c]["m0"]) for c in range(8)]
    m0t_all = np.ascontiguousarray(
        np.concatenate([np.asarray(res.results[c]["m0t"]) for c in range(8)], axis=1))

    nc = build_ffn1()
    in_maps = [_prep_ffn1_inputs(x3T[c], m0t_all, m0[c], ins) for c in range(8)]
    res = run_bass_kernel_spmd(nc, in_maps, core_ids=list(range(8)))
    out = np.concatenate([_from_xT(np.asarray(res.results[c]["x4T"])) for c in range(8)],
                         axis=0)
    return np.ascontiguousarray(out).astype(np.float32)


# version 21
# speedup vs baseline: 1.0124x; 1.0124x over previous
import numpy as np
from contextlib import ExitStack

import concourse.bass as bass
import concourse.mybir as mybir
from concourse.bass_utils import run_bass_kernel_spmd

F32 = mybir.dt.float32
BF16 = mybir.dt.bfloat16
I32 = mybir.dt.int32
AF = mybir.ActivationFunctionType
ALU = mybir.AluOpType

D = 1024
TB = 2048
NB = 2
TOK = NB * TB
CH = 128
GRP = 512
NG = TB // GRP
NCT = D // 128
EPS = 1e-5
QSC = float(128 ** -0.5)


class SemCtr:
    def __init__(self, stack, nc, name):
        self.h = stack.enter_context(nc.semaphore(name))
        self.n = 0

    def inc(self, instr, k=1):
        instr.then_inc(self.h, k)
        self.n += k
        return self.n

    def wait(self, eng, val=None):
        eng.wait_ge(self.h, self.n if val is None else val)

    def fence(self, eng, instr):
        self.inc(instr)
        eng.wait_ge(self.h, self.n)
        return instr


class _NamePrefix:

    def __init__(self, nc, pfx):
        self._nc = nc
        self._pfx = pfx

    def sbuf_tensor(self, name, *a, **k):
        return self._nc.sbuf_tensor(self._pfx + name, *a, **k)

    def psum_tensor(self, name, *a, **k):
        return self._nc.psum_tensor(self._pfx + name, *a, **k)

    def __getattr__(self, x):
        return getattr(self._nc, x)


def mm_chain(t, out_ap, pairs, sem=None, start=True, stop=True):
    last = None
    n = len(pairs)
    for i, (l, r) in enumerate(pairs):
        last = t.matmul(out_ap, l, r,
                        start=(start and i == 0), stop=(stop and i == n - 1))
    if sem is not None:
        sem.inc(last)
    return last



GLA_IN = [("xT", [D, TOK]), ("wq", [D, 512]), ("wk", [D, 512]), ("wv", [D, D]),
          ("gw1", [D, 16]), ("gw2", [16, 512]), ("gb2", [1, 512]), ("wg", [D, D]),
          ("gnw", [2, 128, 1]), ("wo", [D, D])]


def make_sems(nc, st):
    return {n: SemCtr(st, nc, n)
            for n in ["gs", "gc", "ws", "wh", "ys", "ts", "vs", "ss", "cc"]}


def build_gla():
    nc = bass.Bass("TRN2", target_bir_lowering=False, detect_race_conditions=False)
    io = {n: nc.dram_tensor(n, sh, F32, kind="ExternalInput").ap() for n, sh in GLA_IN}
    io["x1T"] = nc.dram_tensor("x1T", [D, TOK], F32, kind="ExternalOutput").ap()
    with ExitStack() as st0:
        _stage_gla(nc, make_sems(nc, st0), io)
    return nc


def _stage_gla(nc, S, io):
    xT, wq, wk, wv, gw1, gw2, gb2, wg, gnw, wo, x1T = (
        io["xT"], io["wq"], io["wk"], io["wv"], io["gw1"], io["gw2"], io["gb2"],
        io["wg"], io["gnw"], io["wo"], io["x1T"])
    gs, gc, ws, ys, ts, vs, ss = (S["gs"], S["gc"], S["ws"], S["ys"], S["ts"],
                                  S["vs"], S["ss"])

    nc = _NamePrefix(nc, "ga_")
    with ExitStack() as st:
        e = st.enter_context
        sWq = [e(nc.sbuf_tensor(f"sWq{i}", [128, 512], BF16)) for i in range(NCT)]
        sWk = [e(nc.sbuf_tensor(f"sWk{i}", [128, 512], BF16)) for i in range(NCT)]
        sWv = [e(nc.sbuf_tensor(f"sWv{i}", [128, D], BF16)) for i in range(NCT)]
        sWg = [e(nc.sbuf_tensor(f"sWg{i}", [128, D], BF16)) for i in range(NCT)]
        sWo = [e(nc.sbuf_tensor(f"sWo{i}", [128, D], BF16)) for i in range(NCT)]
        sGw1 = [e(nc.sbuf_tensor(f"sGw1{i}", [128, 16], BF16)) for i in range(NCT)]
        sGw2 = e(nc.sbuf_tensor("sGw2", [16, 512], BF16))
        siota = e(nc.sbuf_tensor("siota", [128, 512], F32))
        TRI_INCL = e(nc.sbuf_tensor("TRI_INCL", [128, 128], BF16))
        TRI_GT = e(nc.sbuf_tensor("TRI_GT", [128, 128], BF16))
        MASK4 = e(nc.sbuf_tensor("MASK4", [128, 512], BF16))
        ONES16 = e(nc.sbuf_tensor("ONES16", [128, 1], BF16))
        ONES256 = e(nc.sbuf_tensor("ONES256", [128, 1], BF16))
        ONES1024 = e(nc.sbuf_tensor("ONES1024", [128, 1], BF16))
        ONESROWF = e(nc.sbuf_tensor("ONESROWF", [1, 128], F32))
        sEPS = e(nc.sbuf_tensor("sEPS", [128, 1], F32))
        GNW = e(nc.sbuf_tensor("GNW", [128, 2], F32))
        sgb2row = e(nc.sbuf_tensor("sgb2row", [1, 512], F32))
        GB2T = e(nc.sbuf_tensor("GB2T", [128, 512], F32))
        sX = [e(nc.sbuf_tensor(f"sX{i}", [128, TB], BF16)) for i in range(NCT)]
        sS = e(nc.sbuf_tensor("sS", [128, 1024], F32))
        sS_bf = e(nc.sbuf_tensor("sS_bf", [128, 1024], BF16))
        sq = e(nc.sbuf_tensor("sq", [128, 4 * GRP], BF16))
        sk = e(nc.sbuf_tensor("sk", [128, 4 * GRP], BF16))
        ssg = e(nc.sbuf_tensor("ssg", [128, 8 * GRP], BF16))
        sof = e(nc.sbuf_tensor("sof", [128, 8 * GRP], BF16))
        sz1 = e(nc.sbuf_tensor("sz1", [16, GRP], BF16))
        szb = e(nc.sbuf_tensor("szb", [128, 512], F32))
        stA = e(nc.sbuf_tensor("stA", [128, 512], F32))
        stB = e(nc.sbuf_tensor("stB", [128, 512], F32))
        stC = e(nc.sbuf_tensor("stC", [128, 512], BF16))
        sgk_bf = e(nc.sbuf_tensor("sgk_bf", [128, 512], BF16))
        sexpA = e(nc.sbuf_tensor("sexpA", [128, 512], BF16))
        sexpNegA = e(nc.sbuf_tensor("sexpNegA", [128, 512], BF16))
        sexpT = e(nc.sbuf_tensor("sexpT", [128, 512], BF16))
        sk2 = e(nc.sbuf_tensor("sk2", [128, 512], BF16))
        sv_tm = e(nc.sbuf_tensor("sv_tm", [128, 1024], BF16))
        sQp = e(nc.sbuf_tensor("sQp", [128, 512], BF16))
        sKp = e(nc.sbuf_tensor("sKp", [128, 512], BF16))
        sPm = e(nc.sbuf_tensor("sPm", [128, 512], BF16))
        so2 = e(nc.sbuf_tensor("so2", [128, 1024], BF16))
        sAL = e(nc.sbuf_tensor("sAL", [128, 4], F32))
        srstd = e(nc.sbuf_tensor("srstd", [1, 512], F32))
        sln1 = e(nc.sbuf_tensor("sln1", [1, 512], F32))
        sogate = e(nc.sbuf_tensor("sogate", [128, 1024], BF16))
        sxf = [e(nc.sbuf_tensor(f"sxf{i}", [128, 512], F32)) for i in range(2)]
        sy = e(nc.sbuf_tensor("sy", [128, 8 * 512], F32))
        sy_bf = e(nc.sbuf_tensor("sy_bf", [128, 512], BF16))
        sy2 = e(nc.sbuf_tensor("sy2", [128, 512], BF16))
        s_m = e(nc.sbuf_tensor("s_m", [1, 512], F32))
        s_t1 = e(nc.sbuf_tensor("s_t1", [1, 512], F32))
        s_var = e(nc.sbuf_tensor("s_var", [1, 512], F32))
        s_r = e(nc.sbuf_tensor("s_r", [1, 512], F32))
        s_bt = e(nc.sbuf_tensor("s_bt", [1, 512], F32))
        sxo = e(nc.sbuf_tensor("sxo", [128, 512], F32))
        sxout = [e(nc.sbuf_tensor(f"sxout{i}", [128, 512], F32)) for i in range(2)]
        pb = [e(nc.psum_tensor(f"pb{i}", [128, 512], F32)) for i in range(8)]

        block = e(nc.Block())

        @block.gpsimd
        def _(g):
            for i in range(NCT):
                gs.inc(g.dma_start(out=sWq[i][:], in_=wq[i * 128:(i + 1) * 128, :]), 16)
                gs.inc(g.dma_start(out=sWk[i][:], in_=wk[i * 128:(i + 1) * 128, :]), 16)
                gs.inc(g.dma_start(out=sWv[i][:], in_=wv[i * 128:(i + 1) * 128, :]), 16)
                gs.inc(g.dma_start(out=sWg[i][:], in_=wg[i * 128:(i + 1) * 128, :]), 16)
                gs.inc(g.dma_start(out=sWo[i][:], in_=wo[i * 128:(i + 1) * 128, :]), 16)
                gs.inc(g.dma_start(out=sGw1[i][:], in_=gw1[i * 128:(i + 1) * 128, :]), 16)
            gs.inc(g.dma_start(out=sGw2[:], in_=gw2), 16)
            gc.inc(g.iota(siota[:], [[0, 4], [1, 128]], base=0, channel_multiplier=-1,
                          allow_small_or_imprecise_dtypes=True), 1)

        @block.sync
        def _(sy_):
            ys.inc(sy_.dma_start(out=sgb2row[:], in_=gb2), 16)
            ys.inc(sy_.dma_start(out=GNW[:, 0:1], in_=gnw[0]), 16)
            ys.inc(sy_.dma_start(out=GNW[:, 1:2], in_=gnw[1]), 16)

        @block.vector
        def _(v):
            gs.wait(v)
            gc.wait(v)
            ys.wait(v)
            v.tensor_scalar(TRI_INCL[:], siota[:, 0:128], 0.0, 1.0 / 16, ALU.is_ge, ALU.mult)
            v.tensor_scalar(TRI_GT[:], siota[:, 0:128], 0.0, 1.0 / 16, ALU.is_lt, ALU.mult)
            v.tensor_scalar(MASK4[:], siota[:], 0.0, None, ALU.is_ge)
            v.memset(ONES16[:], 1.0 / 16)
            v.memset(ONES256[:], 1.0 / 256)
            v.memset(ONES1024[:], 1.0 / 1024)
            v.memset(sEPS[:], EPS)
            vs.inc(v.memset(ONESROWF[:], 1.0))

        @block.tensor
        def _(t):
            vs.wait(t)
            ts.inc(t.matmul(pb[0][:], ONESROWF[:], sgb2row[:], start=True, stop=True))

        @block.vector
        def _(v):
            ts.wait(v)
            vs.inc(v.tensor_copy(GB2T[:], pb[0][:]))

        init_vs = vs.n

        for b in range(NB):
            @block.gpsimd
            def _(g, b=b):
                if b > 0:
                    ts.wait(g)
                for i in range(NCT):
                    gs.inc(g.dma_start(
                        out=sX[i][:], in_=xT[i * 128:(i + 1) * 128, b * TB:(b + 1) * TB]), 16)
                gc.inc(g.memset(sS[:], 0.0), 1)

            xb_gs = gs.n
            xb_gc = gc.n

            for grp in range(NG):
                t0 = grp * GRP

                @block.tensor
                def _(t, t0=t0, b=b, grp=grp):
                    t.wait_ge(gs.h, xb_gs)
                    t.wait_ge(gc.h, xb_gc)
                    if b == 0 and grp == 0:
                        t.wait_ge(vs.h, init_vs)
                    else:
                        vs.wait(t)
                    mm_chain(t, pb[6][0:16, :],
                             [(sGw1[ci][:], sX[ci][:, t0:t0 + GRP]) for ci in range(NCT)],
                             ts)

                @block.vector
                def _(v):
                    ts.wait(v)
                    vs.inc(v.tensor_copy(sz1[:], pb[6][0:16, :]))

                sqk = {}
                for jt in range(4):
                    @block.tensor
                    def _(t, jt=jt, t0=t0):
                        pv = sqk.get(jt - 2)
                        if pv:
                            t.wait_ge(ss.h, pv)
                        mm_chain(t, pb[jt % 2][:],
                                 [(sWq[ci][:, jt * 128:(jt + 1) * 128],
                                   sX[ci][:, t0:t0 + GRP]) for ci in range(NCT)], ts)
                        mm_chain(t, pb[2 + jt % 2][:],
                                 [(sWk[ci][:, jt * 128:(jt + 1) * 128],
                                   sX[ci][:, t0:t0 + GRP]) for ci in range(NCT)], ts)

                    @block.scalar
                    def _(s, jt=jt):
                        ts.wait(s)
                        s.activation(sq[:, jt * GRP:(jt + 1) * GRP], pb[jt % 2][:],
                                     AF.Copy, scale=QSC)
                        ss.inc(s.copy(sk[:, jt * GRP:(jt + 1) * GRP], pb[2 + jt % 2][:]))
                    sqk[jt] = ss.n

                sgv = {}
                for jt in range(8):
                    @block.tensor
                    def _(t, jt=jt, t0=t0):
                        pv = sgv.get(jt - 2)
                        if pv:
                            t.wait_ge(vs.h, pv)
                        mm_chain(t, pb[4 + jt % 2][:],
                                 [(sWg[ci][:, jt * 128:(jt + 1) * 128],
                                   sX[ci][:, t0:t0 + GRP]) for ci in range(NCT)], ts)

                    @block.scalar
                    def _(s, jt=jt):
                        ts.wait(s)
                        vs.wait(s)
                        ss.inc(s.activation(stC[:], pb[4 + jt % 2][:], AF.Tanh, scale=0.5))

                    @block.vector
                    def _(v, jt=jt):
                        ss.wait(v)
                        v.tensor_scalar(stB[:], stC[:], 0.5, 0.5, ALU.mult, ALU.add)
                        vs.inc(v.tensor_tensor(ssg[:, jt * GRP:(jt + 1) * GRP],
                                               stB[:], pb[4 + jt % 2][:], ALU.mult))
                    sgv[jt] = vs.n

                for c in range(4):
                    tc = t0 + c * CH

                    @block.tensor
                    def _(t, c=c, tc=tc):
                        vs.wait(t)
                        ts.inc(t.matmul(pb[0][:], sz1[:, c * CH:(c + 1) * CH], sGw2[:],
                                        start=True, stop=True))

                    @block.vector
                    def _(v):
                        ts.wait(v)
                        vs.inc(v.tensor_tensor(szb[:], pb[0][:], GB2T[:], ALU.add))

                    @block.scalar
                    def _(s):
                        vs.wait(s)
                        s.activation(stA[:], szb[:], AF.Abs)
                        s.activation(stB[:], stA[:], AF.Exp, scale=-1.0)
                        ss.inc(s.activation(stA[:], stB[:], AF.Ln, bias=1.0))

                    @block.vector
                    def _(v):
                        ss.wait(v)
                        v.tensor_scalar_min(stB[:], szb[:], 0.0)
                        v.tensor_tensor(stB[:], stB[:], stA[:], ALU.subtract)
                        vs.inc(v.tensor_copy(sgk_bf[:], stB[:]))

                    @block.tensor
                    def _(t):
                        vs.wait(t)
                        for h in range(4):
                            t.matmul(pb[1][:, h * 128:(h + 1) * 128],
                                     sgk_bf[:, h * 128:(h + 1) * 128], TRI_INCL[:],
                                     start=True, stop=True)
                            t.matmul(pb[2][:, h:h + 1],
                                     sgk_bf[:, h * 128:(h + 1) * 128], ONES16[:],
                                     start=True, stop=True)
                        ts.inc(t.matmul(pb[0][:], TRI_GT[:], sgk_bf[:],
                                        start=True, stop=True))

                    @block.scalar
                    def _(s):
                        ts.wait(s)
                        s.activation(sexpA[:], pb[1][:], AF.Exp)
                        s.activation(sexpNegA[:], pb[1][:], AF.Exp, scale=-1.0)
                        s.activation(sexpT[:], pb[0][:], AF.Exp)
                        ss.inc(s.activation(sAL[:], pb[2][:, 0:4], AF.Exp))

                    @block.tensor
                    def _(t, tc=tc):
                        ss.wait(t)
                        mm_chain(t, pb[0][:],
                                 [(sX[ci][:, tc:tc + CH], sWk[ci][:]) for ci in range(NCT)])
                        mm_chain(t, pb[2][:],
                                 [(sX[ci][:, tc:tc + CH], sWv[ci][:, 0:512])
                                  for ci in range(NCT)])
                        mm_chain(t, pb[3][:],
                                 [(sX[ci][:, tc:tc + CH], sWv[ci][:, 512:1024])
                                  for ci in range(NCT)], ts)

                    @block.vector
                    def _(v, c=c):
                        ts.wait(v)
                        v.tensor_tensor(sk2[:], pb[0][:], sexpT[:], ALU.mult)
                        v.tensor_copy(sv_tm[:, 0:512], pb[2][:])
                        v.tensor_copy(sv_tm[:, 512:1024], pb[3][:])
                        v.tensor_copy(sS_bf[:], sS[:])
                        for h in range(4):
                            sl = slice(h * 128, (h + 1) * 128)
                            v.tensor_tensor(sQp[:, sl], sq[:, h * GRP + c * CH:
                                                           h * GRP + (c + 1) * CH],
                                            sexpA[:, sl], ALU.mult)
                            v.tensor_tensor(sKp[:, sl], sk[:, h * GRP + c * CH:
                                                           h * GRP + (c + 1) * CH],
                                            sexpNegA[:, sl], ALU.mult)
                        vs.inc(v.engine_nop())

                    @block.tensor
                    def _(t):
                        vs.wait(t)
                        for h in range(4):
                            sl = slice(h * 128, (h + 1) * 128)
                            last = t.matmul(pb[1][:, sl], sKp[:, sl], sQp[:, sl],
                                            start=True, stop=True)
                        ts.inc(last)

                    @block.vector
                    def _(v):
                        ts.wait(v)
                        vs.inc(v.tensor_tensor(sPm[:], pb[1][:], MASK4[:], ALU.mult))

                    @block.tensor
                    def _(t):
                        vs.wait(t)
                        for h in range(4):
                            for vt in range(2):
                                slot = pb[4 + h // 2][:, (2 * (h % 2) + vt) * 128:
                                                      (2 * (h % 2) + vt + 1) * 128]
                                vsl = slice(h * 256 + vt * 128, h * 256 + (vt + 1) * 128)
                                t.matmul(slot, sS_bf[:, vsl],
                                         sQp[:, h * 128:(h + 1) * 128],
                                         start=True, stop=False)
                                last = t.matmul(slot, sv_tm[:, vsl],
                                                sPm[:, h * 128:(h + 1) * 128],
                                                start=False, stop=True)
                        for h in range(4):
                            slot = pb[6 + h // 2][:, (h % 2) * 256:(h % 2 + 1) * 256]
                            last = t.matmul(slot, sk2[:, h * 128:(h + 1) * 128],
                                            sv_tm[:, h * 256:(h + 1) * 256],
                                            start=True, stop=True)
                        ts.inc(last)

                    @block.vector
                    def _(v):
                        ts.wait(v)
                        for h in range(4):
                            slot = pb[6 + h // 2][:, (h % 2) * 256:(h % 2 + 1) * 256]
                            last = v.scalar_tensor_tensor(
                                sS[:, h * 256:(h + 1) * 256],
                                sS[:, h * 256:(h + 1) * 256], sAL[:, h:h + 1], slot,
                                ALU.mult, ALU.add)
                        vs.inc(last)

                    @block.scalar
                    def _(s):
                        ts.wait(s)
                        s.square(so2[:, 0:512], pb[4][:])
                        ss.inc(s.square(so2[:, 512:1024], pb[5][:]))

                    @block.tensor
                    def _(t):
                        ss.wait(t)
                        for h in range(4):
                            for vt in range(2):
                                last = t.matmul(
                                    pb[1][0:1, h * 128:(h + 1) * 128], ONES256[:],
                                    so2[:, (2 * h + vt) * 128:(2 * h + vt + 1) * 128],
                                    start=(vt == 0), stop=(vt == 1))
                        ts.inc(last)

                    @block.scalar
                    def _(s):
                        ts.wait(s)
                        s.activation(sln1[:], pb[1][0:1, 0:512], AF.Ln, bias=sEPS[0:1, 0:1])
                        ss.inc(s.activation(srstd[:], sln1[:], AF.Exp, scale=-0.5))

                    @block.tensor
                    def _(t):
                        ss.wait(t)
                        ts.inc(t.matmul(pb[0][:], ONESROWF[:], srstd[:],
                                        start=True, stop=True))

                    @block.vector
                    def _(v, c=c):
                        ts.wait(v)
                        for h in range(4):
                            for vt in range(2):
                                slot = pb[4 + h // 2][:, (2 * (h % 2) + vt) * 128:
                                                      (2 * (h % 2) + vt + 1) * 128]
                                osl = slice((2 * h + vt) * 128, (2 * h + vt + 1) * 128)
                                v.scalar_tensor_tensor(
                                    sogate[:, osl], slot, GNW[:, vt:vt + 1],
                                    ssg[:, (2 * h + vt) * GRP + c * CH:
                                         (2 * h + vt) * GRP + (c + 1) * CH],
                                    ALU.mult, ALU.mult)
                                last = v.tensor_tensor(
                                    sof[:, (2 * h + vt) * GRP + c * CH:
                                        (2 * h + vt) * GRP + (c + 1) * CH],
                                    sogate[:, osl], pb[0][:, h * 128:(h + 1) * 128],
                                    ALU.mult)
                        vs.inc(last)

                for ct in range(NCT):
                    @block.sync
                    def _(sy_, ct=ct, b=b, t0=t0, grp=grp):
                        if not (b == 0 and grp == 0 and ct < 2):
                            vs.wait(sy_)
                        ys.inc(sy_.dma_start(
                            out=sxf[ct % 2][:],
                            in_=xT[ct * 128:(ct + 1) * 128,
                                   b * TB + t0:b * TB + t0 + GRP]), 16)

                    @block.tensor
                    def _(t, ct=ct):
                        if ct == 0:
                            vs.wait(t)
                        if ct >= 2:
                            vs.wait(t)
                        mm_chain(t, pb[2 + ct % 2][:],
                                 [(sWo[vt][:, ct * 128:(ct + 1) * 128],
                                   sof[:, vt * GRP:(vt + 1) * GRP]) for vt in range(NCT)],
                                 ts)

                    @block.vector
                    def _(v, ct=ct):
                        ts.wait(v)
                        ys.wait(v)
                        vs.inc(v.tensor_tensor(sy[:, ct * GRP:(ct + 1) * GRP],
                                               pb[2 + ct % 2][:], sxf[ct % 2][:], ALU.add))

                    @block.vector
                    def _(v, ct=ct):
                        vs.inc(v.tensor_copy(sy_bf[:], sy[:, ct * GRP:(ct + 1) * GRP]))

                    @block.scalar
                    def _(s, ct=ct):
                        vs.wait(s)
                        ss.inc(s.square(sy2[:], sy_bf[:]))

                    @block.tensor
                    def _(t, ct=ct):
                        ss.wait(t)
                        t.matmul(pb[0][0:1, :], ONES1024[:], sy_bf[:],
                                 start=(ct == 0), stop=(ct == NCT - 1))
                        ts.inc(t.matmul(pb[1][0:1, :], ONES1024[:], sy2[:],
                                        start=(ct == 0), stop=(ct == NCT - 1)))

                @block.vector
                def _(v):
                    ts.wait(v)
                    vs.fence(v, v.tensor_copy(s_m[:], pb[0][0:1, :]))
                    vs.fence(v, v.tensor_tensor(s_t1[:], s_m[:], s_m[:], ALU.mult))
                    vs.inc(v.tensor_tensor(s_var[:], pb[1][0:1, :], s_t1[:], ALU.subtract))

                @block.scalar
                def _(s):
                    vs.wait(s)
                    s.activation(sln1[:], s_var[:], AF.Ln, bias=sEPS[0:1, 0:1])
                    ss.inc(s.activation(s_r[:], sln1[:], AF.Exp, scale=-0.5))

                @block.vector
                def _(v):
                    ss.wait(v)
                    vs.inc(v.scalar_tensor_tensor(s_bt[:], s_m[:], -1.0, s_r[:],
                                                  ALU.mult, ALU.mult))

                @block.tensor
                def _(t):
                    vs.wait(t)
                    t.matmul(pb[4][:], ONESROWF[:], s_r[:], start=True, stop=True)
                    ts.inc(t.matmul(pb[5][:], ONESROWF[:], s_bt[:], start=True, stop=True))

                for ct in range(NCT):
                    @block.vector
                    def _(v, ct=ct, b=b, grp=grp):
                        if ct == 0:
                            ts.wait(v)
                        if not (b == 0 and grp == 0 and ct < 2):
                            ys.wait(v)
                        v.tensor_tensor(sxo[:], sy[:, ct * GRP:(ct + 1) * GRP],
                                        pb[4][:], ALU.mult)
                        vs.inc(v.tensor_tensor(sxout[ct % 2][:], sxo[:], pb[5][:],
                                               ALU.add))

                    @block.sync
                    def _(sy_, ct=ct, b=b, t0=t0):
                        vs.wait(sy_)
                        ys.inc(sy_.dma_start(
                            out=x1T[ct * 128:(ct + 1) * 128,
                                    b * TB + t0:b * TB + t0 + GRP],
                            in_=sxout[ct % 2][:]), 16)

        @block.sync
        def _(sy_):
            ys.wait(sy_)


def _prep_gla_inputs(x_shard, ins):
    xT = np.ascontiguousarray(
        np.concatenate([x_shard[0].T, x_shard[1].T], axis=1)).astype(np.float32)
    return {
        "xT": xT,
        "wq": ins["L0_attn_Wq"], "wk": ins["L0_attn_Wk"], "wv": ins["L0_attn_Wv"],
        "gw1": ins["L0_attn_gk_w1"], "gw2": ins["L0_attn_gk_w2"],
        "gb2": ins["L0_attn_gk_b2"].reshape(1, 512),
        "wg": ins["L0_attn_Wg"],
        "gnw": ins["L0_attn_g_norm_w"].reshape(2, 128, 1),
        "wo": ins["L0_attn_Wo"],
    }





FFN0_IN = [("f0xT", [D, TOK]), ("wsi", [D, 2048]), ("wso", [D, D]),
           ("w1s", [D, 2048]), ("w2", [8, D, D]), ("wgt", [D, 8]), ("gbias", [1, 8])]


def build_ffn0(debug=False):
    nc = bass.Bass("TRN2", target_bir_lowering=False, detect_race_conditions=False)
    io = {n: nc.dram_tensor(n, sh, F32, kind="ExternalInput").ap() for n, sh in FFN0_IN}
    io["x2T"] = nc.dram_tensor("x2T", [D, TOK], F32, kind="ExternalOutput").ap()
    with ExitStack() as st0:
        _stage_ffn0(nc, make_sems(nc, st0), io, debug=debug)
    return nc


def _stage_ffn0(nc, S, io, debug=False):
    xT, wsi, wso, w1s, w2, wgt, gbias, x2T = (
        io["f0xT"], io["wsi"], io["wso"], io["w1s"], io["w2"], io["wgt"],
        io["gbias"], io["x2T"])
    gs, gc, ws, ys, ts, vs, ss = (S["gs"], S["gc"], S["ws"], S["ys"], S["ts"],
                                  S["vs"], S["ss"])
    if debug:
        d_ssh = nc.dram_tensor("d_ssh", [128, 8 * GRP], BF16, kind="ExternalOutput").ap()
        d_acc1 = nc.dram_tensor("d_acc1", [128, 8 * GRP], F32, kind="ExternalOutput").ap()
        d_hid = nc.dram_tensor("d_hid", [128, 8 * GRP], BF16, kind="ExternalOutput").ap()
        d_wdT = nc.dram_tensor("d_wdT", [8, GRP], F32, kind="ExternalOutput").ap()
        d_accF = nc.dram_tensor("d_accF", [128, 8 * GRP], F32, kind="ExternalOutput").ap()

    NGRP = TOK // GRP

    nc = _NamePrefix(nc, "f0_")
    with ExitStack() as st:
        e = st.enter_context
        sWsi = [e(nc.sbuf_tensor(f"sWsi{i}", [128, 2048], BF16)) for i in range(NCT)]
        sWso = [e(nc.sbuf_tensor(f"sWso{i}", [128, D], BF16)) for i in range(NCT)]
        sW1 = [e(nc.sbuf_tensor(f"sW1{i}", [128, 2048], BF16)) for i in range(NCT)]
        sWgt = [e(nc.sbuf_tensor(f"sWgt{i}", [128, 8], BF16)) for i in range(NCT)]
        sW2b = [e(nc.sbuf_tensor(f"sW2b{i}", [128, 8192], BF16)) for i in range(2)]
        sXg = [[e(nc.sbuf_tensor(f"sXg{p}_{i}", [128, GRP], BF16)) for i in range(NCT)]
               for p in range(2)]
        siota = e(nc.sbuf_tensor("siota", [128, 128], F32))
        siota8 = e(nc.sbuf_tensor("siota8", [8, 1024], F32))
        SEL8 = e(nc.sbuf_tensor("SEL8", [8, 1024], F32))
        IDENTF = e(nc.sbuf_tensor("IDENTF", [128, 128], F32))
        ONES1024 = e(nc.sbuf_tensor("ONES1024", [128, 1], BF16))
        ONESROWF = e(nc.sbuf_tensor("ONESROWF", [1, 128], F32))
        sEPS = e(nc.sbuf_tensor("sEPS", [128, 1], F32))
        sgbrow = e(nc.sbuf_tensor("sgbrow", [1, 8], F32))
        GBT = e(nc.sbuf_tensor("GBT", [128, 8], F32))
        s_sh = e(nc.sbuf_tensor("s_sh", [128, 8 * GRP], BF16))
        s_hid = s_sh
        s_hs = e(nc.sbuf_tensor("s_hs", [128, 8 * GRP], BF16))
        sacc = e(nc.sbuf_tensor("sacc", [128, 8 * GRP], F32))
        stC = e(nc.sbuf_tensor("stC", [128, 512], BF16))
        stB = e(nc.sbuf_tensor("stB", [128, 512], BF16))
        stS = e(nc.sbuf_tensor("stS", [128, 512], BF16))
        szb = e(nc.sbuf_tensor("szb", [128, 8], F32))
        sE = e(nc.sbuf_tensor("sE", [128, 8], F32))
        sm1 = e(nc.sbuf_tensor("sm1", [128, 1], F32))
        sE2 = e(nc.sbuf_tensor("sE2", [128, 8], F32))
        sm2 = e(nc.sbuf_tensor("sm2", [128, 1], F32))
        skeep = e(nc.sbuf_tensor("skeep", [128, 8], F32))
        swm = e(nc.sbuf_tensor("swm", [128, 8], F32))
        sden = e(nc.sbuf_tensor("sden", [128, 1], F32))
        srd = e(nc.sbuf_tensor("srd", [128, 1], F32))
        swd = e(nc.sbuf_tensor("swd", [128, 8], F32))
        swdT = e(nc.sbuf_tensor("swdT", [8, GRP], F32))
        smb = e(nc.sbuf_tensor("smb", [128, GRP], BF16))
        sxf = [e(nc.sbuf_tensor(f"sxf{i}", [128, 512], F32)) for i in range(2)]
        sy = sacc
        sy_bf = e(nc.sbuf_tensor("sy_bf", [128, 512], BF16))
        sy2 = e(nc.sbuf_tensor("sy2", [128, 512], BF16))
        s_m = e(nc.sbuf_tensor("s_m", [1, 512], F32))
        s_t1 = e(nc.sbuf_tensor("s_t1", [1, 512], F32))
        s_var = e(nc.sbuf_tensor("s_var", [1, 512], F32))
        s_r = e(nc.sbuf_tensor("s_r", [1, 512], F32))
        s_bt = e(nc.sbuf_tensor("s_bt", [1, 512], F32))
        sln1 = e(nc.sbuf_tensor("sln1", [1, 512], F32))
        sxo = e(nc.sbuf_tensor("sxo", [128, 512], F32))
        sxout = [e(nc.sbuf_tensor(f"sxout{i}", [128, 512], F32)) for i in range(2)]
        pb = [e(nc.psum_tensor(f"pb{i}", [128, 512], F32)) for i in range(8)]

        block = e(nc.Block())

        @block.gpsimd
        def _(g):
            for i in range(NCT):
                gs.inc(g.dma_start(out=sWsi[i][:], in_=wsi[i * 128:(i + 1) * 128, :]), 16)
                gs.inc(g.dma_start(out=sWso[i][:], in_=wso[i * 128:(i + 1) * 128, :]), 16)
                gs.inc(g.dma_start(out=sW1[i][:], in_=w1s[i * 128:(i + 1) * 128, :]), 16)
                gs.inc(g.dma_start(out=sWgt[i][:], in_=wgt[i * 128:(i + 1) * 128, :]), 16)
            gc.inc(g.iota(siota[:], [[1, 128]], base=0, channel_multiplier=-1,
                          allow_small_or_imprecise_dtypes=True), 1)
            gc.inc(g.iota(siota8[:], [[1, 8], [0, 128]], base=0, channel_multiplier=-1,
                          allow_small_or_imprecise_dtypes=True), 1)
            ys.wait(g)
            vs.wait(g)
            ss.wait(g)
            for i in range(NCT):
                gs.inc(g.dma_start(out=sXg[0][i][:], in_=xT[i * 128:(i + 1) * 128, 0:GRP]), 16)

        @block.sync
        def _(sy_):
            ys.wait(sy_)
            vs.wait(sy_)
            ys.inc(sy_.dma_start(out=sgbrow[:], in_=gbias), 16)

        @block.vector
        def _(v):
            gs.wait(v)
            gc.wait(v)
            ys.wait(v)
            v.tensor_scalar(IDENTF[:], siota[:], 0.0, None, ALU.is_equal)
            v.tensor_scalar(SEL8[:], siota8[:], 0.0, None, ALU.is_equal)
            v.memset(ONES1024[:], 1.0 / 1024)
            v.memset(sEPS[:], EPS)
            vs.inc(v.memset(ONESROWF[:], 1.0))

        @block.tensor
        def _(t):
            vs.wait(t)
            ts.inc(t.matmul(pb[0][:, 0:8], ONESROWF[:], sgbrow[:], start=True, stop=True))

        @block.vector
        def _(v):
            ts.wait(v)
            vs.inc(v.tensor_copy(GBT[:], pb[0][:, 0:8]))

        @block.gpsimd
        def _(g):
            ws.inc(g.dma_start(out=sW2b[0][:].rearrange("p (a f) -> p a f", a=8),
                               in_=w2[0].rearrange("(a p) f -> p a f", p=128)), 16)

        for grp in range(NGRP):
            t0 = grp * GRP
            cur = grp % 2

            if grp + 1 < NGRP:
                @block.gpsimd
                def _(g, grp=grp):
                    if grp >= 1:
                        ts.wait(g)
                    for i in range(NCT):
                        gs.inc(g.dma_start(
                            out=sXg[(grp + 1) % 2][i][:],
                            in_=xT[i * 128:(i + 1) * 128,
                                   (grp + 1) * GRP:(grp + 2) * GRP]), 16)

            xg = sXg[cur]
            xg_gs = gs.n if grp + 1 >= NGRP else gs.n

            vsw = {}
            for which, wmat, sdst in ((0, sWsi, s_sh),):
                for jt in range(8):
                    @block.tensor
                    def _(t, jt=jt, wmat=wmat, grp=grp, which=which):
                        if grp == 0 and which == 0 and jt == 0:
                            gs.wait(t)
                            vs.wait(t)
                        elif which == 0 and jt == 0 and grp > 0:
                            vs.wait(t)
                        k = which * 8 + jt
                        pv = vsw.get(k - 2)
                        if pv:
                            t.wait_ge(vs.h, pv)
                        mm_chain(t, pb[jt % 2][:],
                                 [(wmat[ci][:, jt * 128:(jt + 1) * 128], xg[ci][:])
                                  for ci in range(NCT)])
                        mm_chain(t, pb[2 + jt % 2][:],
                                 [(wmat[ci][:, 1024 + jt * 128:1024 + (jt + 1) * 128],
                                   xg[ci][:]) for ci in range(NCT)], ts)

                    @block.scalar
                    def _(s, jt=jt):
                        ts.wait(s)
                        vs.wait(s)
                        ss.inc(s.activation(stC[:], pb[jt % 2][:], AF.Tanh, scale=0.5))

                    @block.vector
                    def _(v, jt=jt, sdst=sdst):
                        ss.wait(v)
                        v.tensor_scalar(stB[:], stC[:], 0.5, 0.5, ALU.mult, ALU.add)
                        v.tensor_tensor(stS[:], stB[:], pb[jt % 2][:], ALU.mult)
                        vs.inc(v.tensor_tensor(sdst[:, jt * GRP:(jt + 1) * GRP],
                                               stS[:], pb[2 + jt % 2][:], ALU.mult))
                    vsw[which * 8 + jt] = vs.n

            if debug and grp == 0:
                @block.sync
                def _(sy_):
                    vs.wait(sy_)
                    ys.inc(sy_.dma_start(out=d_ssh, in_=s_sh[:]), 16)

            vso = {}
            for ct in range(NCT):
                @block.tensor
                def _(t, ct=ct):
                    if ct == 0:
                        vs.wait(t)
                    pv = vso.get(ct - 2)
                    if pv:
                        t.wait_ge(vs.h, pv)
                    mm_chain(t, pb[4 + ct % 2][:],
                             [(sWso[jt][:, ct * 128:(ct + 1) * 128],
                               s_sh[:, jt * GRP:(jt + 1) * GRP]) for jt in range(NCT)],
                             ts)

                @block.vector
                def _(v, ct=ct):
                    ts.wait(v)
                    vs.inc(v.tensor_copy(sacc[:, ct * GRP:(ct + 1) * GRP],
                                         pb[4 + ct % 2][:]))
                vso[ct] = vs.n

            if debug and grp == 0:
                @block.sync
                def _(sy_):
                    vs.wait(sy_)
                    ys.inc(sy_.dma_start(out=d_acc1, in_=sacc[:]), 16)

                @block.tensor
                def _(t):
                    ys.wait(t)

            for which, wmat, sdst in ((1, sW1, s_hid),):
                for jt in range(8):
                    @block.tensor
                    def _(t, jt=jt, wmat=wmat):
                        k = which * 8 + jt
                        pv = vsw.get(k - 2)
                        if pv:
                            t.wait_ge(vs.h, pv)
                        if jt == 0:
                            vs.wait(t)
                        mm_chain(t, pb[jt % 2][:],
                                 [(wmat[ci][:, jt * 128:(jt + 1) * 128], xg[ci][:])
                                  for ci in range(NCT)])
                        mm_chain(t, pb[2 + jt % 2][:],
                                 [(wmat[ci][:, 1024 + jt * 128:1024 + (jt + 1) * 128],
                                   xg[ci][:]) for ci in range(NCT)], ts)

                    @block.scalar
                    def _(s, jt=jt):
                        ts.wait(s)
                        vs.wait(s)
                        ss.inc(s.activation(stC[:], pb[jt % 2][:], AF.Tanh, scale=0.5))

                    @block.vector
                    def _(v, jt=jt, sdst=sdst):
                        ss.wait(v)
                        v.tensor_scalar(stB[:], stC[:], 0.5, 0.5, ALU.mult, ALU.add)
                        v.tensor_tensor(stS[:], stB[:], pb[jt % 2][:], ALU.mult)
                        vs.inc(v.tensor_tensor(sdst[:, jt * GRP:(jt + 1) * GRP],
                                               stS[:], pb[2 + jt % 2][:], ALU.mult))
                    vsw[which * 8 + jt] = vs.n

            if debug and grp == 0:
                @block.sync
                def _(sy_):
                    vs.wait(sy_)
                    ys.inc(sy_.dma_start(out=d_hid, in_=s_hid[:]), 16)

            for c in range(4):
                @block.tensor
                def _(t, c=c):
                    vs.wait(t)
                    mm_chain(t, pb[6][:, 0:8],
                             [(xg[ci][:, c * 128:(c + 1) * 128], sWgt[ci][:])
                              for ci in range(NCT)], ts)

                @block.vector
                def _(v):
                    ts.wait(v)
                    vs.inc(v.tensor_tensor(szb[:], pb[6][:, 0:8], GBT[:], ALU.add))

                @block.scalar
                def _(s):
                    vs.wait(s)
                    ss.inc(s.activation(sE[:], szb[:], AF.Exp))

                @block.vector
                def _(v):
                    ss.wait(v)
                    vs.fence(v, v.reduce_max(sm1[:], sE[:], axis=mybir.AxisListType.X))
                    vs.fence(v, v.tensor_scalar(sE2[:], sE[:], sm1[:, 0:1], None, ALU.is_equal))
                    vs.fence(v, v.scalar_tensor_tensor(sE2[:], sE2[:], -1e30, sE[:], ALU.mult, ALU.add))
                    vs.fence(v, v.reduce_max(sm2[:], sE2[:], axis=mybir.AxisListType.X))
                    vs.fence(v, v.tensor_scalar(skeep[:], sE[:], sm2[:, 0:1], None, ALU.is_ge))
                    vs.fence(v, v.tensor_tensor(swm[:], sE[:], skeep[:], ALU.mult))
                    vs.fence(v, v.reduce_sum(sden[:], sE[:], axis=mybir.AxisListType.X))
                    vs.fence(v, v.reciprocal(srd[:], sden[:]))
                    vs.inc(v.tensor_scalar(swd[:], swm[:], srd[:, 0:1], None, ALU.mult))

                @block.tensor
                def _(t, c=c):
                    vs.wait(t)
                    ts.inc(t.transpose(pb[7][0:8, 0:128], swd[:], IDENTF[:]))

                @block.vector
                def _(v, c=c):
                    ts.wait(v)
                    vs.inc(v.tensor_copy(swdT[:, c * 128:(c + 1) * 128], pb[7][0:8, 0:128]))

            if debug and grp == 0:
                @block.sync
                def _(sy_):
                    vs.wait(sy_)
                    ys.inc(sy_.dma_start(out=d_wdT, in_=swdT[:]), 16)

            vex = {}
            for ex in range(8):
                nxt_g, nxt_e = (grp, ex + 1) if ex + 1 < 8 else (grp + 1, 0)
                if nxt_g < NGRP:
                    @block.gpsimd
                    def _(g, nxt_g=nxt_g, nxt_e=nxt_e, ex=ex):
                        ts.wait(g)
                        ws.inc(g.dma_start(
                            out=sW2b[(ex + 1) % 2][:].rearrange("p (a f) -> p a f", a=8),
                            in_=w2[nxt_e].rearrange("(a p) f -> p a f", p=128)), 16)

                @block.tensor
                def _(t, ex=ex):
                    vs.wait(t)
                    ts.inc(t.matmul(pb[6][:], SEL8[:, ex * 128:(ex + 1) * 128],
                                    swdT[:], start=True, stop=True))

                @block.vector
                def _(v):
                    ts.wait(v)
                    vs.inc(v.tensor_copy(smb[:], pb[6][:]))

                @block.vector
                def _(v):
                    for jt in range(NCT):
                        last = v.tensor_tensor(s_hs[:, jt * GRP:(jt + 1) * GRP],
                                               s_hid[:, jt * GRP:(jt + 1) * GRP],
                                               smb[:], ALU.mult)
                    vs.inc(last)

                for ct in range(NCT):
                    @block.tensor
                    def _(t, ct=ct, ex=ex):
                        if ct == 0:
                            vs.wait(t)
                            ws.wait(t)
                        pv = vex.get((ex, ct - 2), vex.get((ex - 1, NCT - 2 + ct)))
                        if pv:
                            t.wait_ge(vs.h, pv)
                        mm_chain(t, pb[4 + ct % 2][:],
                                 [(sW2b[ex % 2][:, jt * 1024 + ct * 128:
                                                jt * 1024 + (ct + 1) * 128],
                                   s_hs[:, jt * GRP:(jt + 1) * GRP])
                                  for jt in range(NCT)], ts)

                    @block.vector
                    def _(v, ct=ct):
                        ts.wait(v)
                        vs.inc(v.tensor_tensor(sacc[:, ct * GRP:(ct + 1) * GRP],
                                               sacc[:, ct * GRP:(ct + 1) * GRP],
                                               pb[4 + ct % 2][:], ALU.add))
                    vex[(ex, ct)] = vs.n

            if debug and grp == 0:
                @block.sync
                def _(sy_):
                    vs.wait(sy_)
                    ys.inc(sy_.dma_start(out=d_accF, in_=sacc[:]), 16)

                @block.vector
                def _(v):
                    ys.wait(v)

            for ct in range(NCT):
                @block.sync
                def _(sy_, ct=ct, t0=t0, grp=grp):
                    if not (grp == 0 and ct < 2):
                        vs.wait(sy_)
                    ys.inc(sy_.dma_start(
                        out=sxf[ct % 2][:],
                        in_=xT[ct * 128:(ct + 1) * 128, t0:t0 + GRP]), 16)

                @block.vector
                def _(v, ct=ct):
                    ys.wait(v)
                    v.tensor_tensor(sy[:, ct * GRP:(ct + 1) * GRP],
                                    sy[:, ct * GRP:(ct + 1) * GRP],
                                    sxf[ct % 2][:], ALU.add)
                    vs.inc(v.tensor_copy(sy_bf[:], sy[:, ct * GRP:(ct + 1) * GRP]))

                @block.scalar
                def _(s, ct=ct):
                    vs.wait(s)
                    ss.inc(s.square(sy2[:], sy_bf[:]))

                @block.tensor
                def _(t, ct=ct):
                    ss.wait(t)
                    t.matmul(pb[0][0:1, :], ONES1024[:], sy_bf[:],
                             start=(ct == 0), stop=(ct == NCT - 1))
                    ts.inc(t.matmul(pb[1][0:1, :], ONES1024[:], sy2[:],
                                    start=(ct == 0), stop=(ct == NCT - 1)))

            @block.vector
            def _(v):
                ts.wait(v)
                vs.fence(v, v.tensor_copy(s_m[:], pb[0][0:1, :]))
                vs.fence(v, v.tensor_tensor(s_t1[:], s_m[:], s_m[:], ALU.mult))
                vs.inc(v.tensor_tensor(s_var[:], pb[1][0:1, :], s_t1[:], ALU.subtract))

            @block.scalar
            def _(s):
                vs.wait(s)
                s.activation(sln1[:], s_var[:], AF.Ln, bias=sEPS[0:1, 0:1])
                ss.inc(s.activation(s_r[:], sln1[:], AF.Exp, scale=-0.5))

            @block.vector
            def _(v):
                ss.wait(v)
                vs.inc(v.scalar_tensor_tensor(s_bt[:], s_m[:], -1.0, s_r[:],
                                              ALU.mult, ALU.mult))

            @block.tensor
            def _(t):
                vs.wait(t)
                t.matmul(pb[2][:], ONESROWF[:], s_r[:], start=True, stop=True)
                ts.inc(t.matmul(pb[3][:], ONESROWF[:], s_bt[:], start=True, stop=True))

            for ct in range(NCT):
                @block.vector
                def _(v, ct=ct, grp=grp):
                    if ct == 0:
                        ts.wait(v)
                    if not (grp == 0 and ct < 2):
                        ys.wait(v)
                    v.tensor_tensor(sxo[:], sy[:, ct * GRP:(ct + 1) * GRP],
                                    pb[2][:], ALU.mult)
                    vs.inc(v.tensor_tensor(sxout[ct % 2][:], sxo[:], pb[3][:], ALU.add))

                @block.sync
                def _(sy_, ct=ct, t0=t0):
                    vs.wait(sy_)
                    ys.inc(sy_.dma_start(
                        out=x2T[ct * 128:(ct + 1) * 128, t0:t0 + GRP],
                        in_=sxout[ct % 2][:]), 16)

        @block.sync
        def _(sy_):
            ys.wait(sy_)


def _prep_ffn0_inputs(x1T_core, ins):
    return {
        "f0xT": x1T_core,
        "wsi": ins["L0_ffn_w_shared_in"], "wso": ins["L0_ffn_w_shared_out"],
        "w1s": ins["L0_ffn_w1_shared"], "w2": ins["L0_ffn_w2"],
        "wgt": ins["L0_ffn_w_gate"], "gbias": ins["L0_ffn_gate_bias"].reshape(1, 8),
    }





MHA_IN = [("mxT", [D, TOK]), ("qkvw", [D, 3072]), ("qkvb", [24, 128, 1]),
          ("qkvbrow", [1, 1024]), ("outw", [D, D]), ("outb", [8, 128, 1]),
          ("projw", [8, D]), ("projb", [1, 8]), ("ebias", [1, 8])]


def build_mha(phase=9):
    nc = bass.Bass("TRN2", target_bir_lowering=False, detect_race_conditions=False)
    io = {n: nc.dram_tensor(n, sh, F32, kind="ExternalInput").ap() for n, sh in MHA_IN}
    io["x3T"] = nc.dram_tensor("x3T", [D, TOK], F32, kind="ExternalOutput").ap()
    io["m0"] = nc.dram_tensor("m0", [TOK, 8], F32, kind="ExternalOutput").ap()
    io["m0t"] = nc.dram_tensor("m0t", [8, TOK], F32, kind="ExternalOutput").ap()
    io["x3bfT"] = nc.dram_tensor("x3bfT", [D, TOK], BF16, kind="ExternalOutput").ap()
    with ExitStack() as st0:
        _stage_mha(nc, make_sems(nc, st0), io, phase=phase)
    return nc


def _stage_mha(nc, S, io, phase=9):
    xT, qkvw, qkvb, qkvbrow, outw, outb, projw, projb, ebias = (
        io["mxT"], io["qkvw"], io["qkvb"], io["qkvbrow"], io["outw"], io["outb"],
        io["projw"], io["projb"], io["ebias"])
    x3T, m0, m0t = io["x3T"], io["m0"], io["m0t"]
    x3bfT = io["x3bfT"]
    gs, gc, ws, ys, ts, vs, ss = (S["gs"], S["gc"], S["ws"], S["ys"], S["ts"],
                                  S["vs"], S["ss"])
    wh = S["wh"]
    aoDram = nc.dram_tensor("aoDram", [D, TB], BF16, kind="Internal").ap()
    fused_pre = "x3tmLoc" in io
    _tp_ld, _tp_st = {}, {}

    NKT = TB // 128

    nc = _NamePrefix(nc, "mh_")
    _e_sTp4, _e_sZt, _e_sZf = [], [], []
    with ExitStack() as st:
        e = st.enter_context
        sWqkv = [e(nc.sbuf_tensor(f"sWqkv{i}", [128, 3072], BF16)) for i in range(NCT)]
        sWout = [e(nc.sbuf_tensor(f"sWout{i}", [128, D], BF16)) for i in range(NCT)]
        sPWn = e(nc.sbuf_tensor("sPWn", [8, D], F32))
        sProjW = [e(nc.sbuf_tensor(f"sProjW{i}", [128, 8], BF16)) for i in range(NCT)]
        QB = e(nc.sbuf_tensor("QB", [128, 24], F32))
        QBS = e(nc.sbuf_tensor("QBS", [128, 8], F32))
        OB = e(nc.sbuf_tensor("OB", [128, 8], F32))
        sprow = e(nc.sbuf_tensor("sprow", [1, 8], F32))
        sqbrow = e(nc.sbuf_tensor("sqbrow", [1, 1024], BF16))
        ONESROWB = e(nc.sbuf_tensor("ONESROWB", [1, 128], BF16))
        serow = e(nc.sbuf_tensor("serow", [1, 8], F32))
        PBT = e(nc.sbuf_tensor("PBT", [128, 8], F32))
        EBT = e(nc.sbuf_tensor("EBT", [128, 8], F32))
        VBT = e(nc.sbuf_tensor("VBT", [128, 1024], BF16))
        siota = e(nc.sbuf_tensor("siota", [128, 128], F32))
        IDENTF = e(nc.sbuf_tensor("IDENTF", [128, 128], F32))
        ONES1024 = e(nc.sbuf_tensor("ONES1024", [128, 1], BF16))
        ONESB = e(nc.sbuf_tensor("ONESB", [128, 1], BF16))
        ONESF = e(nc.sbuf_tensor("ONESF", [128, 1], F32))
        ONESROWF = e(nc.sbuf_tensor("ONESROWF", [1, 128], F32))
        sEPS = e(nc.sbuf_tensor("sEPS", [128, 1], F32))
        sX = [e(nc.sbuf_tensor(f"sX{i}", [128, TB], BF16)) for i in range(NCT)]
        sV = [e(nc.sbuf_tensor(f"sV{i}", [128, 16 * 65], BF16)) for i in range(NKT)]
        sqT = e(nc.sbuf_tensor("sqT", [128, TB], BF16))
        skT = e(nc.sbuf_tensor("skT", [128, TB], BF16))
        sk2s = e(nc.sbuf_tensor("sk2s", [128, 512], BF16))
        sAoW = e(nc.sbuf_tensor("sAoW", [128, TB], BF16))
        sAob = [e(nc.sbuf_tensor(f"sAob{i}", [128, 512], BF16)) for i in range(NCT)]
        sE0 = e(nc.sbuf_tensor("sE0", [128, 1024], BF16))
        sE1 = e(nc.sbuf_tensor("sE1", [128, 1024], BF16))
        sEx = [None, None]
        snc = e(nc.sbuf_tensor("snc", [128, 1], F32))
        sb1 = e(nc.sbuf_tensor("sb1", [1, 8], F32))
        sb2 = e(nc.sbuf_tensor("sb2", [1, 8], F32))
        sqmax = e(nc.sbuf_tensor("sqmax", [128, 4], F32))
        srd0 = e(nc.sbuf_tensor("srd0", [1, 512], F32))
        srd1 = e(nc.sbuf_tensor("srd1", [1, 512], F32))
        srdb = e(nc.sbuf_tensor("srdb", [64, 512], BF16))
        szb = e(nc.sbuf_tensor("szb", [128, 8], F32))
        sm1 = e(nc.sbuf_tensor("sm1", [128, 1], F32))
        sE2r = e(nc.sbuf_tensor("sE2r", [128, 8], F32))
        sm2 = e(nc.sbuf_tensor("sm2", [128, 1], F32))
        skeep = e(nc.sbuf_tensor("skeep", [128, 8], F32))
        swm = e(nc.sbuf_tensor("swm", [128, 8], F32))
        swsum = e(nc.sbuf_tensor("swsum", [128, 1], F32))
        slnw = e(nc.sbuf_tensor("slnw", [128, 8], F32))
        slns = e(nc.sbuf_tensor("slns", [128, 1], F32))
        swn = e(nc.sbuf_tensor("swn", [128, 8], F32))
        swnT = e(nc.sbuf_tensor("swnT", [8, 128], F32))
        sxf = [e(nc.sbuf_tensor(f"sxf{i}", [128, 512], F32)) for i in range(2)]
        syb = e(nc.sbuf_tensor("syb", [128, 8 * 512], BF16))
        sy2 = e(nc.sbuf_tensor("sy2", [128, 512], BF16))
        s_m = e(nc.sbuf_tensor("s_m", [1, 512], F32))
        s_t1 = e(nc.sbuf_tensor("s_t1", [1, 512], F32))
        s_var = e(nc.sbuf_tensor("s_var", [1, 512], F32))
        s_r = e(nc.sbuf_tensor("s_r", [1, 512], F32))
        s_bt = e(nc.sbuf_tensor("s_bt", [1, 512], F32))
        sln1 = e(nc.sbuf_tensor("sln1", [1, 512], F32))
        sxo = e(nc.sbuf_tensor("sxo", [128, 512], F32))
        sxout = [e(nc.sbuf_tensor(f"sxout{i}", [128, 512], F32)) for i in range(2)]
        pU = [e(nc.psum_tensor(f"pU{i}", [128, 1024], F32)) for i in range(2)]
        pC = e(nc.psum_tensor("pC", [128, 512], F32))
        pD = e(nc.psum_tensor("pD", [128, 512], F32))
        pE = e(nc.psum_tensor("pE", [128, 512], F32))
        pF = e(nc.psum_tensor("pF", [128, 512], F32))
        sEx[0], sEx[1] = sE0, sE1

        if fused_pre:
            sTp4 = [e(nc.sbuf_tensor(f"sTp4_{i}", [128, 1024], BF16,
                                     is_transpose_dest=True)) for i in range(3)]
            sZf = e(nc.sbuf_tensor("sZf", [128, 2 * (LCAP + 128) // 128], F32))
        block = e(nc.Block())

        @block.gpsimd
        def _(g):
            ys.wait(g)
            vs.wait(g)
            ss.wait(g)
            for i in range(NCT):
                gs.inc(g.dma_start(out=sWqkv[i][:], in_=qkvw[i * 128:(i + 1) * 128, :]), 16)
                gs.inc(g.dma_start(out=sWout[i][:], in_=outw[i * 128:(i + 1) * 128, :]), 16)
            gc.inc(g.iota(siota[:], [[1, 128]], base=0, channel_multiplier=-1,
                          allow_small_or_imprecise_dtypes=True), 1)
            for kt in range(NKT):
                gc.inc(g.memset(sV[kt][:], 1.0), 1)

        if fused_pre:
            @block.gpsimd
            def _(g):
                gs.inc(g.dma_start(out=io["sh1bf"], in_=io["sh1w"]), 16)
                gs.inc(g.dma_start(out=io["sh2bf"], in_=io["sh2w"]), 16)
                gs.inc(g.dma_start(out=io["myfc1bf"], in_=io["myfc1"]), 16)
                gs.inc(g.dma_start(out=io["myfc2bf"], in_=io["myfc2"]), 16)
                gc.inc(g.memset(sTp4[0][:], 0.0), 1)
                gc.inc(g.memset(sZf[:], 0.0), 1)
                g.wait_ge(gc.h, gc.n)
                for a in range(GTOK // 128):
                    gs.inc(g.dma_start(
                        out=io["yAll"][a * 128:(a + 1) * 128, :],
                        in_=sTp4[0][:]), 16)
                gs.inc(g.dma_start(
                    out=io["elist"][0:LCAP + 128, :].rearrange(
                        "(p q) w -> p (q w)", p=128),
                    in_=sZf[:]), 16)
            _zf_done = [gs.n]

        @block.sync
        def _(sy_):
            ys.wait(sy_)
            vs.wait(sy_)
            for j in range(24):
                ys.inc(sy_.dma_start(out=QB[:, j:j + 1], in_=qkvb[j]), 16)
            for j in range(8):
                ys.inc(sy_.dma_start(out=OB[:, j:j + 1], in_=outb[j]), 16)
            ys.inc(sy_.dma_start(out=sPWn[:], in_=projw), 16)
            ys.inc(sy_.dma_start(out=sprow[:], in_=projb), 16)
            ys.inc(sy_.dma_start(out=serow[:], in_=ebias), 16)

        @block.gpsimd
        def _(g):
            gs.inc(g.dma_start(out=sqbrow[:], in_=qkvbrow), 16)


        @block.vector
        def _(v):
            gs.wait(v)
            gc.wait(v)
            ys.wait(v)
            v.tensor_scalar(IDENTF[:], siota[:], 0.0, None, ALU.is_equal)
            v.memset(ONES1024[:], 1.0 / 1024)
            v.memset(ONESB[:], 1.0)
            v.memset(ONESF[:], 1.0)
            v.memset(sEPS[:], EPS)
            v.memset(ONESROWF[:], 1.0)
            v.memset(ONESROWB[:], 1.0)
            vs.inc(v.tensor_scalar(QBS[:], QB[:, 0:8], 0.125, None, ALU.mult))

        @block.tensor
        def _(t):
            vs.wait(t)
            t.matmul(pC[:, 0:8], ONESROWF[:], sprow[:], start=True, stop=True)
            t.matmul(pD[:, 0:8], ONESROWF[:], serow[:], start=True, stop=True)
            t.matmul(pE[:], ONESROWB[:], sqbrow[0:1, 0:512], start=True, stop=True)
            t.matmul(pF[:], ONESROWB[:], sqbrow[0:1, 512:1024], start=True, stop=True)
            for i in range(NCT):
                last = t.transpose(pC[:, 16 + i * 8:24 + i * 8],
                                   sPWn[0:8, i * 128:(i + 1) * 128], IDENTF[0:8, 0:8])
            ts.inc(last)

        @block.vector
        def _(v):
            ts.wait(v)
            v.tensor_copy(PBT[:], pC[:, 0:8])
            v.tensor_copy(EBT[:], pD[:, 0:8])
            v.tensor_copy(VBT[:, 0:512], pE[:])
            v.tensor_copy(VBT[:, 512:1024], pF[:])
            for i in range(NCT):
                last = v.tensor_copy(sProjW[i][:], pC[:, 16 + i * 8:24 + i * 8])
            vs.inc(last)

        init_vs = vs.n

        for b in range(NB):
            @block.gpsimd
            def _(g, b=b):
                if b > 0:
                    ts.wait(g)
                for i in range(NCT):
                    gs.inc(g.dma_start(
                        out=sX[i][:], in_=xT[i * 128:(i + 1) * 128, b * TB:(b + 1) * TB]), 16)

            xb_gs = gs.n

            if phase < 1:
                break
            for kt in range(NKT):
                @block.tensor
                def _(t, kt=kt, b=b):
                    t.wait_ge(gs.h, xb_gs)
                    if b == 0 and kt == 0:
                        t.wait_ge(vs.h, init_vs)
                    else:
                        vs.wait(t)
                    tc = kt * 128
                    mm_chain(t, pE[:],
                             [(sX[ci][:, tc:tc + 128], sWqkv[ci][:, 2048:2560])
                              for ci in range(NCT)])
                    mm_chain(t, pF[:],
                             [(sX[ci][:, tc:tc + 128], sWqkv[ci][:, 2560:3072])
                              for ci in range(NCT)], ts)

                @block.vector
                def _(v, kt=kt):
                    ts.wait(v)
                    vs.fence(v, v.tensor_tensor(sxo[:], pE[:], VBT[:, 0:512], ALU.add))
                    v.tensor_copy(
                        sV[kt][:].rearrange("p (h w) -> p h w", w=65)[:, 0:8, 0:64],
                        sxo[:].rearrange("p (h w) -> p h w", w=64))
                    vs.fence(v, v.tensor_tensor(sxo[:], pF[:], VBT[:, 512:1024], ALU.add))
                    vs.inc(v.tensor_copy(
                        sV[kt][:].rearrange("p (h w) -> p h w", w=65)[:, 8:16, 0:64],
                        sxo[:].rearrange("p (h w) -> p h w", w=64)))

            if phase < 2:
                continue
            for hp in range(8):
                for tb in range(4):
                    @block.tensor
                    def _(t, hp=hp, tb=tb):
                        vs.wait(t)
                        ss.wait(t)
                        mm_chain(t, pE[:],
                                 [(sWqkv[ci][:, hp * 128:(hp + 1) * 128],
                                   sX[ci][:, tb * GRP:(tb + 1) * GRP])
                                  for ci in range(NCT)])
                        mm_chain(t, pF[:],
                                 [(sWqkv[ci][:, 1024 + hp * 128:1024 + (hp + 1) * 128],
                                   sX[ci][:, tb * GRP:(tb + 1) * GRP])
                                  for ci in range(NCT)], ts)

                    @block.vector
                    def _(v, hp=hp, tb=tb):
                        ts.wait(v)
                        sl = slice(tb * GRP, (tb + 1) * GRP)
                        v.tensor_scalar(sqT[:, sl], pE[:], 0.125, QBS[:, hp:hp + 1],
                                        ALU.mult, ALU.add)
                        vs.inc(v.tensor_scalar(skT[:, sl], pF[:], QB[:, 8 + hp:9 + hp],
                                               None, ALU.add))

                    @block.vector
                    def _(v, tb=tb):
                        sl = slice(tb * GRP, (tb + 1) * GRP)
                        vs.fence(v, v.tensor_tensor(sy2[:], sqT[:, sl], sqT[:, sl],
                                                    ALU.mult))
                        vs.fence(v, v.tensor_tensor(sk2s[:], skT[:, sl],
                                                    skT[:, sl], ALU.mult))
                        vs.fence(v, v.reduce_max(sqmax[:, 2:3], sy2[:],
                                                 axis=mybir.AxisListType.X))
                        if tb == 0:
                            vs.fence(v, v.tensor_copy(sqmax[:, 0:1], sqmax[:, 2:3]))
                        else:
                            vs.fence(v, v.tensor_tensor(sqmax[:, 0:1], sqmax[:, 0:1],
                                                        sqmax[:, 2:3], ALU.max))
                        vs.fence(v, v.reduce_max(sqmax[:, 3:4], sk2s[:],
                                                 axis=mybir.AxisListType.X))
                        if tb == 0:
                            vs.inc(v.tensor_copy(sqmax[:, 1:2], sqmax[:, 3:4]))
                        else:
                            vs.inc(v.tensor_tensor(sqmax[:, 1:2], sqmax[:, 1:2],
                                                   sqmax[:, 3:4], ALU.max))

                @block.tensor
                def _(t):
                    vs.wait(t)
                    t.matmul(pC[0:1, 24:25], ONESF[0:64, :], sqmax[0:64, 0:1],
                             start=True, stop=True)
                    t.matmul(pC[0:1, 25:26], ONESF[64:128, :], sqmax[64:128, 0:1],
                             start=True, stop=True)
                    t.matmul(pC[0:1, 26:27], ONESF[0:64, :], sqmax[0:64, 1:2],
                             start=True, stop=True)
                    ts.inc(t.matmul(pC[0:1, 27:28], ONESF[64:128, :], sqmax[64:128, 1:2],
                                    start=True, stop=True))

                @block.scalar
                def _(s):
                    ts.wait(s)
                    ss.inc(s.activation(sb1[0:1, 0:4], pC[0:1, 24:28], AF.Ln))

                @block.vector
                def _(v):
                    ss.wait(v)
                    vs.fence(v, v.tensor_tensor(sb2[0:1, 0:2], sb1[0:1, 0:2],
                                                sb1[0:1, 2:4], ALU.add))
                    vs.fence(v, v.reduce_max(sb2[0:1, 4:5], sb2[0:1, 0:2],
                                             axis=mybir.AxisListType.X))
                    vs.inc(v.engine_nop())

                @block.scalar
                def _(s):
                    vs.wait(s)
                    ss.inc(s.activation(sb2[0:1, 5:6], sb2[0:1, 4:5], AF.Exp, scale=0.5))

                @block.vector
                def _(v):
                    ss.wait(v)
                    vs.inc(v.tensor_scalar(sb2[0:1, 6:7], sb2[0:1, 5:6], -1.0, None,
                                           ALU.mult))

                @block.tensor
                def _(t):
                    vs.wait(t)
                    ts.inc(t.matmul(pC[:, 24:25], ONESROWF[:], sb2[0:1, 6:7],
                                    start=True, stop=True))

                @block.vector
                def _(v):
                    ts.wait(v)
                    vs.inc(v.tensor_copy(snc[:], pC[:, 24:25]))

                for qb in range(4 if phase >= 3 else 0):
                    qsl = slice(qb * GRP, (qb + 1) * GRP)

                    def _scores(t, kt, qsl=qsl):
                        t.matmul(pU[kt % 2][:, 0:512],
                                 skT[0:64, kt * 128:(kt + 1) * 128],
                                 sqT[0:64, qsl], start=True, stop=True)
                        return t.matmul(pU[kt % 2][:, 512:1024],
                                        skT[64:128, kt * 128:(kt + 1) * 128],
                                        sqT[64:128, qsl], start=True, stop=True)

                    sc_ready = {}

                    @block.tensor
                    def _(t, _scores=_scores):
                        vs.wait(t)
                        _scores(t, 0)
                        ts.inc(_scores(t, 1))
                    sc_ready[0] = sc_ready[1] = ts.n

                    for kt in range(16):
                        u = kt % 2

                        @block.scalar
                        def _(s, u=u, kt=kt, sc_ready=sc_ready):
                            s.wait_ge(ts.h, sc_ready[kt])
                            ss.inc(s.activation(sEx[u][:], pU[u][:], AF.Exp,
                                                bias=snc[:, 0:1]))

                        @block.tensor
                        def _(t, hp=hp, kt=kt, u=u, _scores=_scores):
                            ss.wait(t)
                            t.matmul(pC[0:65, :],
                                     sV[kt][:, (2 * hp) * 65:(2 * hp) * 65 + 65],
                                     sEx[u][:, 0:512],
                                     start=(kt == 0), stop=(kt == 15))
                            last = t.matmul(pD[0:65, :],
                                            sV[kt][:, (2 * hp + 1) * 65:(2 * hp + 1) * 65 + 65],
                                            sEx[u][:, 512:1024],
                                            start=(kt == 0), stop=(kt == 15))
                            if kt + 2 < 16:
                                last = _scores(t, kt + 2)
                            ts.inc(last)
                        sc_ready[kt + 2] = ts.n

                    @block.scalar
                    def _(s):
                        ts.wait(s)
                        s.activation(sln1[:], pC[64:65, :], AF.Ln)
                        s.activation(srd0[:], sln1[:], AF.Exp, scale=-1.0)
                        s.activation(sln1[:], pD[64:65, :], AF.Ln)
                        ss.inc(s.activation(srd1[:], sln1[:], AF.Exp, scale=-1.0))

                    @block.tensor
                    def _(t):
                        ss.wait(t)
                        t.matmul(pE[0:64, :], ONESROWF[0:1, 0:64], srd0[:],
                                 start=True, stop=True)
                        ts.inc(t.matmul(pF[0:64, :], ONESROWF[0:1, 0:64], srd1[:],
                                        start=True, stop=True))

                    @block.vector
                    def _(v, hp=hp, qsl=qsl, qb=qb, b=b):
                        ts.wait(v)
                        if qb == 0 and not (hp == 0 and b == 0):
                            ys.wait(v)
                        vs.fence(v, v.tensor_copy(srdb[:], pE[0:64, :]))
                        v.tensor_tensor(sAoW[0:64, qsl], pC[0:64, :], srdb[:],
                                        ALU.mult)
                        vs.fence(v, v.tensor_copy(srdb[:], pF[0:64, :]))
                        vs.inc(v.tensor_tensor(sAoW[64:128, qsl], pD[0:64, :],
                                               srdb[:], ALU.mult))

                @block.sync
                def _(sy_, hp=hp):
                    vs.wait(sy_)
                    ys.inc(sy_.dma_start(out=aoDram[hp * 128:(hp + 1) * 128, :],
                                         in_=sAoW[:]), 16)

            if phase < 4:
                continue
            for tb in range(4):
                t0 = tb * GRP

                @block.sync
                def _(sy_, tb=tb, b=b):
                    sy_.wait_ge(ys.h, ys.n)
                    if not (b == 0 and tb == 0):
                        ts.wait(sy_)
                    for vt in range(NCT):
                        ys.inc(sy_.dma_start(
                            out=sAob[vt][:],
                            in_=aoDram[vt * 128:(vt + 1) * 128,
                                       tb * GRP:(tb + 1) * GRP]), 16)

                for ct in range(NCT):
                    @block.sync
                    def _(sy_, ct=ct, b=b, t0=t0, tb=tb):
                        if not (b == 0 and tb == 0 and ct < 2):
                            vs.wait(sy_)
                        ys.inc(sy_.dma_start(
                            out=sxf[ct % 2][:],
                            in_=xT[ct * 128:(ct + 1) * 128,
                                   b * TB + t0:b * TB + t0 + GRP]), 16)

                    @block.tensor
                    def _(t, ct=ct, tb=tb):
                        vs.wait(t)
                        ys.wait(t)
                        mm_chain(t, [pE, pF][ct % 2][:],
                                 [(sWout[vt][:, ct * 128:(ct + 1) * 128],
                                   sAob[vt][:]) for vt in range(NCT)], ts)

                    @block.vector
                    def _(v, ct=ct):
                        ts.wait(v)
                        ys.wait(v)
                        vs.fence(v, v.tensor_scalar([pE, pF][ct % 2][:] if False else sxo[:],
                                                    [pE, pF][ct % 2][:],
                                                    OB[:, ct:ct + 1], None, ALU.add))
                        vs.inc(v.tensor_tensor(syb[:, ct * GRP:(ct + 1) * GRP], sxo[:],
                                                sxf[ct % 2][:], ALU.add))

                    @block.scalar
                    def _(s, ct=ct):
                        vs.wait(s)
                        ss.inc(s.square(sy2[:], syb[:, ct * GRP:(ct + 1) * GRP]))

                    @block.tensor
                    def _(t, ct=ct):
                        ss.wait(t)
                        t.matmul(pC[0:1, 0:512], ONES1024[:],
                                 syb[:, ct * GRP:(ct + 1) * GRP],
                                 start=(ct == 0), stop=(ct == NCT - 1))
                        ts.inc(t.matmul(pD[0:1, 0:512], ONES1024[:], sy2[:],
                                        start=(ct == 0), stop=(ct == NCT - 1)))

                @block.vector
                def _(v):
                    ts.wait(v)
                    vs.fence(v, v.tensor_copy(s_m[:], pC[0:1, 0:512]))
                    vs.fence(v, v.tensor_tensor(s_t1[:], s_m[:], s_m[:], ALU.mult))
                    vs.inc(v.tensor_tensor(s_var[:], pD[0:1, 0:512], s_t1[:],
                                           ALU.subtract))

                @block.scalar
                def _(s):
                    vs.wait(s)
                    s.activation(sln1[:], s_var[:], AF.Ln, bias=sEPS[0:1, 0:1])
                    ss.inc(s.activation(s_r[:], sln1[:], AF.Exp, scale=-0.5))

                @block.vector
                def _(v):
                    ss.wait(v)
                    vs.inc(v.scalar_tensor_tensor(s_bt[:], s_m[:], -1.0, s_r[:],
                                                  ALU.mult, ALU.mult))

                @block.tensor
                def _(t):
                    vs.wait(t)
                    t.matmul(pC[:], ONESROWF[:], s_r[:], start=True, stop=True)
                    ts.inc(t.matmul(pD[:], ONESROWF[:], s_bt[:], start=True, stop=True))

                for ct in range(NCT):
                    @block.vector
                    def _(v, ct=ct, b=b, tb=tb):
                        if ct == 0:
                            ts.wait(v)
                        if not (b == 0 and tb == 0 and ct < 2):
                            ys.wait(v)
                        v.tensor_tensor(sxo[:], syb[:, ct * GRP:(ct + 1) * GRP],
                                        pC[:], ALU.mult)
                        vs.fence(v, v.tensor_tensor(sxout[ct % 2][:], sxo[:], pD[:],
                                                    ALU.add))
                        stash = sqT if ct < 4 else skT
                        vs.inc(v.tensor_copy(stash[:, (ct % 4) * GRP:(ct % 4 + 1) * GRP],
                                             sxout[ct % 2][:]))

                    @block.sync
                    def _(sy_, ct=ct, b=b, t0=t0):
                        vs.wait(sy_)
                        ys.inc(sy_.dma_start(
                            out=x3T[ct * 128:(ct + 1) * 128,
                                    b * TB + t0:b * TB + t0 + GRP],
                            in_=sxout[ct % 2][:]), 16)
                        stash = sqT if ct < 4 else skT
                        ys.inc(sy_.dma_start(
                            out=x3bfT[ct * 128:(ct + 1) * 128,
                                      b * TB + t0:b * TB + t0 + GRP],
                            in_=stash[:, (ct % 4) * GRP:(ct % 4 + 1) * GRP]), 16)

                if fused_pre:
                    @block.sync
                    def _(sy_, b=b, tb=tb):
                        sy_.wait_ge(ys.h, ys.n)
                        if b == 0 and tb == 0:
                            sy_.wait_ge(gs.h, _zf_done[0])
                        cch = b * 4 + tb
                        for j in range(4):
                            i = 4 * cch + j
                            if i >= 3:
                                sy_.wait_ge(ys.h, _tp_st[i - 3])
                            wh.inc(sy_.dma_start(
                                out=sTp4[i % 3][:],
                                in_=x3bfT[0:1024, i * 128:(i + 1) * 128],
                                transpose=True), 16)
                            _tp_ld[i] = wh.n
                            if i >= 1:
                                sy_.wait_ge(wh.h, _tp_ld[i - 1])
                                ys.inc(sy_.dma_start(
                                    out=io["x3tmLoc"][(i - 1) * 128:i * 128, :],
                                    in_=sTp4[(i - 1) % 3][:]), 16)
                                _tp_st[i - 1] = ys.n

                for c in range(4 if phase >= 5 else 0):
                    @block.tensor
                    def _(t, c=c):
                        vs.wait(t)
                        mm_chain(t, pE[:, 0:8],
                                 [((sqT if ci < 4 else skT)[:, (ci % 4) * GRP + c * 128:
                                                            (ci % 4) * GRP + (c + 1) * 128],
                                   sProjW[ci][:]) for ci in range(NCT)], ts)

                    @block.vector
                    def _(v):
                        ts.wait(v)
                        vs.inc(v.tensor_tensor(szb[:], pE[:, 0:8], PBT[:], ALU.add))

                    @block.scalar
                    def _(s):
                        vs.wait(s)
                        ss.inc(s.activation(swm[:], szb[:], AF.Tanh, scale=0.5))

                    @block.vector
                    def _(v):
                        ss.wait(v)
                        vs.fence(v, v.tensor_scalar(swm[:], swm[:], 0.5, 0.5,
                                                    ALU.mult, ALU.add))
                        vs.fence(v, v.tensor_tensor(swm[:], swm[:], EBT[:], ALU.add))
                        vs.fence(v, v.tensor_scalar(swm[:], swm[:], 1e-12, None,
                                                    ALU.max))
                        vs.fence(v, v.reduce_max(sm1[:], swm[:],
                                                 axis=mybir.AxisListType.X))
                        vs.fence(v, v.tensor_scalar(sE2r[:], swm[:], sm1[:, 0:1], None,
                                                    ALU.is_equal))
                        vs.fence(v, v.scalar_tensor_tensor(sE2r[:], sE2r[:], -1e30,
                                                           swm[:], ALU.mult, ALU.add))
                        vs.fence(v, v.reduce_max(sm2[:], sE2r[:],
                                                 axis=mybir.AxisListType.X))
                        vs.fence(v, v.tensor_scalar(skeep[:], swm[:], sm2[:, 0:1], None,
                                                    ALU.is_ge))
                        vs.fence(v, v.tensor_tensor(swm[:], swm[:], skeep[:], ALU.mult))
                        vs.fence(v, v.reduce_sum(swsum[:], swm[:],
                                                 axis=mybir.AxisListType.X))
                        vs.inc(v.tensor_scalar(swm[:], swm[:], 1e-30, None, ALU.max))

                    @block.scalar
                    def _(s):
                        vs.wait(s)
                        s.activation(slnw[:], swm[:], AF.Ln)
                        ss.inc(s.activation(slns[:], swsum[:], AF.Ln))

                    @block.vector
                    def _(v):
                        ss.wait(v)
                        vs.inc(v.tensor_scalar(slnw[:], slnw[:], slns[:, 0:1], None,
                                               ALU.subtract))

                    @block.scalar
                    def _(s):
                        vs.wait(s)
                        s.wait_ge(ys.h, ys.n)
                        ss.inc(s.activation(swn[:], slnw[:], AF.Exp))

                    @block.tensor
                    def _(t):
                        ss.wait(t)
                        ts.inc(t.transpose(pF[0:8, 0:128], swn[:], IDENTF[:]))

                    @block.vector
                    def _(v):
                        ts.wait(v)
                        vs.inc(v.tensor_copy(swnT[:], pF[0:8, 0:128]))

                    @block.sync
                    def _(sy_, b=b, t0=t0, c=c):
                        vs.wait(sy_)
                        row0 = b * TB + t0 + c * 128
                        ys.inc(sy_.dma_start(out=m0[row0:row0 + 128, :], in_=swn[:]), 16)
                        ys.inc(sy_.dma_start(out=m0t[:, row0:row0 + 128], in_=swnT[:]), 16)

        if fused_pre:
            NT32 = TOK // 128
            @block.sync
            def _(sy_):
                sy_.wait_ge(wh.h, _tp_ld[NT32 - 1])
                ys.inc(sy_.dma_start(
                    out=io["x3tmLoc"][(NT32 - 1) * 128:NT32 * 128, :],
                    in_=sTp4[(NT32 - 1) % 3][:]), 16)
            io["x3tm_done"] = ys.n

        @block.sync
        def _(sy_):
            ys.wait(sy_)


def _qkv_perm():
    q = [192 * h + d for h in range(16) for d in range(64)]
    k = [192 * h + 64 + d for h in range(16) for d in range(64)]
    v = [192 * h + 128 + d for h in range(16) for d in range(64)]
    return np.array(q + k + v)


def _prep_mha_inputs(x2T_core, ins):
    perm = _qkv_perm()
    qkvw = np.ascontiguousarray(ins["L1_attn_qkv_w"][:, perm])
    qkvb = np.ascontiguousarray(ins["L1_attn_qkv_b"][perm])
    return {
        "mxT": x2T_core,
        "qkvw": qkvw,
        "qkvb": qkvb.reshape(24, 128, 1),
        "qkvbrow": np.ascontiguousarray(qkvb[2048:].reshape(1, 1024)),
        "outw": ins["L1_attn_out_w"],
        "outb": ins["L1_attn_out_b"].reshape(8, 128, 1),
        "projw": ins["L1_ffn_proj_w"],
        "projb": ins["L1_ffn_proj_b"].reshape(1, 8),
        "ebias": ins["L1_ffn_expert_bias"].reshape(1, 8),
    }





CAP = 10240.0

FFN1_IN = [("f1xT", [D, TOK]), ("fc1w", [8, 2048, D]), ("fc1b", [8, 16, 128, 1]),
           ("fc2w", [8, D, 2048]), ("fc2b", [8, D]), ("sh1w", [4096, D]),
           ("sh1b", [32, 128, 1]), ("sh2w", [D, 4096]), ("sh2b", [8, 128, 1])]


def build_ffn1():
    nc = bass.Bass("TRN2", target_bir_lowering=False, detect_race_conditions=False)
    io = {n: nc.dram_tensor(n, sh, F32, kind="ExternalInput").ap() for n, sh in FFN1_IN}
    io["m0all"] = nc.dram_tensor("m0all", [8, 8 * TOK], F32, kind="ExternalInput").ap()
    io["m0loc"] = nc.dram_tensor("m0loc", [TOK, 8], F32, kind="ExternalInput").ap()
    io["x4T"] = nc.dram_tensor("x4T", [D, TOK], F32, kind="ExternalOutput").ap()
    with ExitStack() as st0:
        _stage_ffn1(nc, make_sems(nc, st0), io, ranks_layout=False)
    return nc


def _stage_ffn1(nc, S, io, ranks_layout=False):
    xT, fc1w, fc1b, fc2w, fc2b, sh1w, sh1b, sh2w, sh2b = (
        io["f1xT"], io["fc1w"], io["fc1b"], io["fc2w"], io["fc2b"], io["sh1w"],
        io["sh1b"], io["sh2w"], io["sh2b"])
    m0all, m0loc, x4T = io["m0all"], io["m0loc"], io["x4T"]
    gs, gc, ws, ys, ts, vs, ss, cc = (S["gs"], S["gc"], S["wh"], S["ys"], S["ts"],
                                      S["vs"], S["ss"], S["cc"])
    cc_target = S["cc"].n

    fc1bf = nc.dram_tensor("fc1bf", [8, 2048, D], BF16, kind="Internal").ap()
    fc2bf = nc.dram_tensor("fc2bf", [8, D, 2048], BF16, kind="Internal").ap()
    sh1bf = nc.dram_tensor("sh1bf", [4096, D], BF16, kind="Internal").ap()
    sh2bf = nc.dram_tensor("sh2bf", [D, 4096], BF16, kind="Internal").ap()

    NGRP = TOK // GRP
    NIT = 37

    nc = _NamePrefix(nc, "f1_")
    with ExitStack() as st:
        e = st.enter_context
        sMe = e(nc.sbuf_tensor("sMe", [128, 8 * 256], F32))
        sjunk = e(nc.sbuf_tensor("sjunk", [128, 256], F32))
        scnt = e(nc.sbuf_tensor("scnt", [128, 8], F32))
        sth = e(nc.sbuf_tensor("sth", [1, 8], F32))
        sthk = e(nc.sbuf_tensor("sthk", [1, 8], F32))
        sgt = e(nc.sbuf_tensor("sgt", [1, 8], F32))
        sok = e(nc.sbuf_tensor("sok", [1, 8], F32))
        st1 = e(nc.sbuf_tensor("st1", [1, 8], F32))
        sthc = e(nc.sbuf_tensor("sthc", [128, 8], F32))
        smloc = e(nc.sbuf_tensor("smloc", [128, 8], F32))
        smk = e(nc.sbuf_tensor("smk", [128, 8], F32))
        sm0Tm = e(nc.sbuf_tensor("sm0Tm", [8, TOK], BF16))
        siota = e(nc.sbuf_tensor("siota", [8, 1024], F32))
        siota2 = e(nc.sbuf_tensor("siota2", [128, 128], F32))
        SEL8 = e(nc.sbuf_tensor("SEL8", [8, 1024], BF16))
        IDENTF = e(nc.sbuf_tensor("IDENTF", [128, 128], F32))
        ONESROWF = e(nc.sbuf_tensor("ONESROWF", [1, 128], F32))
        ONESF = e(nc.sbuf_tensor("ONESF", [128, 1], F32))
        ONES1024 = e(nc.sbuf_tensor("ONES1024", [128, 1], BF16))
        sEPS = e(nc.sbuf_tensor("sEPS", [128, 1], F32))
        sFc1b = e(nc.sbuf_tensor("sFc1b", [128, 128], F32))
        sSh1b = e(nc.sbuf_tensor("sSh1b", [128, 32], F32))
        sSh2b = e(nc.sbuf_tensor("sSh2b", [128, 8], F32))
        sFc2b = e(nc.sbuf_tensor("sFc2b", [8, D], BF16))
        sFc2bf32 = e(nc.sbuf_tensor("sFc2bf32", [8, D], F32))
        sXg = [[e(nc.sbuf_tensor(f"sXg{p}_{i}", [128, GRP], BF16)) for i in range(NCT)]
               for p in range(2)]
        sW1 = [[e(nc.sbuf_tensor(f"sW1_{p}_{i}", [128, 512], BF16, is_transpose_dest=True))
                for i in range(NCT)] for p in range(2)]
        sW2 = [[e(nc.sbuf_tensor(f"sW2_{p}_{i}", [128, 1024], BF16, is_transpose_dest=True))
                for i in range(4)] for p in range(2)]
        sHd = e(nc.sbuf_tensor("sHd", [128, 4 * 512], BF16))
        sHs = e(nc.sbuf_tensor("sHs", [128, 4 * 512], BF16))
        smb = e(nc.sbuf_tensor("smb", [128, GRP], BF16))
        sacc = e(nc.sbuf_tensor("sacc", [128, 8 * GRP], F32))
        sb2sb = e(nc.sbuf_tensor("sb2sb", [128, GRP], F32))
        sxf = [e(nc.sbuf_tensor(f"sxf{i}", [128, 512], F32)) for i in range(2)]
        sy_bf = e(nc.sbuf_tensor("sy_bf", [128, 512], BF16))
        sy2 = e(nc.sbuf_tensor("sy2", [128, 512], BF16))
        s_m = e(nc.sbuf_tensor("s_m", [1, 512], F32))
        s_t1 = e(nc.sbuf_tensor("s_t1", [1, 512], F32))
        s_var = e(nc.sbuf_tensor("s_var", [1, 512], F32))
        s_r = e(nc.sbuf_tensor("s_r", [1, 512], F32))
        s_bt = e(nc.sbuf_tensor("s_bt", [1, 512], F32))
        sln1 = e(nc.sbuf_tensor("sln1", [1, 512], F32))
        sxo = e(nc.sbuf_tensor("sxo", [128, 512], F32))
        sxout = [e(nc.sbuf_tensor(f"sxout{i}", [128, 512], F32)) for i in range(2)]
        pb = [e(nc.psum_tensor(f"pb{i}", [128, 512], F32)) for i in range(8)]

        block = e(nc.Block())

        @block.gpsimd
        def _(g):
            gs.inc(g.dma_start(out=fc1bf, in_=fc1w), 16)
            gs.inc(g.dma_start(out=fc2bf, in_=fc2w), 16)
            gs.inc(g.dma_start(out=sh1bf, in_=sh1w), 16)
            gs.inc(g.dma_start(out=sh2bf, in_=sh2w), 16)
            ys.wait(g)
            vs.wait(g)
            ss.wait(g)
            gc.inc(g.iota(siota[:], [[1, 8], [0, 128]], base=0, channel_multiplier=-1,
                          allow_small_or_imprecise_dtypes=True), 1)
            gc.inc(g.iota(siota2[:], [[1, 128]], base=0, channel_multiplier=-1,
                          allow_small_or_imprecise_dtypes=True), 1)

        prep_gs = gs.n

        @block.sync
        def _(sy_):
            ys.wait(sy_)
            vs.wait(sy_)
            sy_.wait_ge(cc.h, cc_target)
            if ranks_layout:
                for ei in range(8):
                    for r in range(8):
                        ys.inc(sy_.dma_start(
                            out=sMe[:, ei * 256:(ei + 1) * 256][r * 16:(r + 1) * 16, :],
                            in_=m0all[r * 8 + ei].rearrange("(p n) -> p n", p=16)), 16)
            else:
                for ei in range(8):
                    ys.inc(sy_.dma_start(
                        out=sMe[:, ei * 256:(ei + 1) * 256],
                        in_=m0all[ei].rearrange("(p n) -> p n", p=128)), 16)
            for ei in range(8):
                for hj in range(16):
                    ys.inc(sy_.dma_start(out=sFc1b[:, ei * 16 + hj:ei * 16 + hj + 1],
                                         in_=fc1b[ei, hj]), 16)
            for j in range(32):
                ys.inc(sy_.dma_start(out=sSh1b[:, j:j + 1], in_=sh1b[j]), 16)
            for j in range(8):
                ys.inc(sy_.dma_start(out=sSh2b[:, j:j + 1], in_=sh2b[j]), 16)
            ys.inc(sy_.dma_start(out=sFc2bf32[:], in_=fc2b), 16)

        @block.vector
        def _(v):
            ys.wait(v)
            gc.wait(v)
            v.tensor_scalar(SEL8[:], siota[:], 0.0, None, ALU.is_equal)
            v.tensor_scalar(IDENTF[:], siota2[:], 0.0, None, ALU.is_equal)
            v.memset(ONESROWF[:], 1.0)
            v.memset(ONESF[:], 1.0)
            v.memset(ONES1024[:], 1.0 / 1024)
            v.memset(sEPS[:], EPS)
            v.memset(sth[:], 0.5)
            v.memset(sthk[:], 1.1)
            vs.inc(v.tensor_copy(sFc2b[:], sFc2bf32[:]))

        dk = 0.25
        for it in range(NIT):
            @block.vector
            def _(v, it=it):
                if it == 0:
                    vs.wait(v)
                else:
                    ts.wait(v)
                    vs.fence(v, v.tensor_copy(sthc[:], pb[1][:, 0:8]))
                for ei in range(8):
                    last = v.tensor_scalar(sjunk[:], sMe[:, ei * 256:(ei + 1) * 256],
                                           sthc[:, ei:ei + 1] if it > 0 else 0.5,
                                           0.0, ALU.is_ge, ALU.add,
                                           accum_out=scnt[:, ei:ei + 1])
                vs.inc(last)

            @block.tensor
            def _(t):
                vs.wait(t)
                ts.inc(t.matmul(pb[0][0:1, 0:8], ONESF[:], scnt[:], start=True, stop=True))

            @block.vector
            def _(v, it=it):
                ts.wait(v)
                vs.fence(v, v.tensor_scalar(sgt[:], pb[0][0:1, 0:8], CAP, None, ALU.is_gt))
                vs.fence(v, v.tensor_scalar(sok[:], sgt[:], -1.0, 1.0, ALU.mult, ALU.add))
                vs.fence(v, v.tensor_tensor(st1[:], sok[:], sth[:], ALU.mult))
                vs.fence(v, v.tensor_tensor(sthk[:], sgt[:], sthk[:], ALU.mult))
                vs.fence(v, v.tensor_tensor(sthk[:], sthk[:], st1[:], ALU.add))
                d = dk / 2 if it + 1 < NIT else dk / 2
                vs.fence(v, v.scalar_tensor_tensor(st1[:], sgt[:], 2.0 * d, sth[:],
                                                   ALU.mult, ALU.add))
                vs.inc(v.tensor_scalar(sth[:], st1[:], -d, None, ALU.add))
            dk = dk / 2

            @block.tensor
            def _(t, it=it):
                vs.wait(t)
                src = sthk if it == NIT - 1 else sth
                ts.inc(t.matmul(pb[1][:, 0:8], ONESROWF[:], src[:], start=True, stop=True))

        @block.vector
        def _(v):
            ts.wait(v)
            vs.inc(v.tensor_copy(sthc[:], pb[1][:, 0:8]))

        for c in range(TOK // 128):
            @block.sync
            def _(sy_, c=c):
                if c == 0:
                    vs.wait(sy_)
                else:
                    ts.wait(sy_)
                ys.inc(sy_.dma_start(out=smloc[:], in_=m0loc[c * 128:(c + 1) * 128, :]), 16)

            @block.vector
            def _(v, c=c):
                ys.wait(v)
                vs.fence(v, v.tensor_tensor(smk[:], smloc[:], sthc[:], ALU.is_ge))
                vs.inc(v.tensor_tensor(smk[:], smk[:], smloc[:], ALU.mult))

            @block.tensor
            def _(t, c=c):
                vs.wait(t)
                ts.inc(t.transpose(pb[2][0:8, 0:128], smk[:], IDENTF[:]))

            @block.vector
            def _(v, c=c):
                ts.wait(v)
                vs.inc(v.tensor_copy(sm0Tm[:, c * 128:(c + 1) * 128], pb[2][0:8, 0:128]))

        routing_vs = vs.n

        @block.gpsimd
        def _(g):
            for i in range(NCT):
                gs.inc(g.dma_start(out=sXg[0][i][:], in_=xT[i * 128:(i + 1) * 128, 0:GRP]), 16)

        def unit_of(grp, u):
            if u < 8:
                return ("sh", u)
            return ("ex", (u - 8) // 4, (u - 8) % 4)

        NU = 40

        def emit_prefetch(sy_, grp, u):
            p = u % 2
            kind = unit_of(grp, u)
            if kind[0] == "sh":
                hs = kind[1]
                for ct in range(NCT):
                    ws.inc(sy_.dma_start(
                        out=sW1[p][ct][:],
                        in_=sh1bf[hs * 512:(hs + 1) * 512, ct * 128:(ct + 1) * 128],
                        transpose=True), 16)
                for j in range(4):
                    ws.inc(sy_.dma_start(
                        out=sW2[p][j][:],
                        in_=sh2bf[0:1024, hs * 512 + j * 128:hs * 512 + (j + 1) * 128],
                        transpose=True), 16)
            else:
                ei, hsl = kind[1], kind[2]
                for ct in range(NCT):
                    ws.inc(sy_.dma_start(
                        out=sW1[p][ct][:],
                        in_=fc1bf[ei][hsl * 512:(hsl + 1) * 512, ct * 128:(ct + 1) * 128],
                        transpose=True), 16)
                for j in range(4):
                    ws.inc(sy_.dma_start(
                        out=sW2[p][j][:],
                        in_=fc2bf[ei][0:1024, hsl * 512 + j * 128:hsl * 512 + (j + 1) * 128],
                        transpose=True), 16)

        @block.sync
        def _(sy_):
            sy_.wait_ge(gs.h, prep_gs)
            emit_prefetch(sy_, 0, 0)

        _UN = {"ws_ready": {1: ws.n}, "ts_done": {}}
        for grp in range(NGRP):
            t0 = grp * GRP
            xg = sXg[grp % 2]
            _UN["ts_done"] = {-1: _UN["ts_done"].get(NU - 1, 0)}
            _UN["ws_ready"] = {0: _UN["ws_ready"].get(NU, ws.n)}

            if grp + 1 < NGRP:
                @block.gpsimd
                def _(g, grp=grp):
                    ts.wait(g)
                    for i in range(NCT):
                        gs.inc(g.dma_start(
                            out=sXg[(grp + 1) % 2][i][:],
                            in_=xT[i * 128:(i + 1) * 128,
                                   (grp + 1) * GRP:(grp + 2) * GRP]), 16)

            @block.tensor
            def _(t, grp=grp, t0=t0):
                gs.wait(t)
                if grp == 0:
                    t.wait_ge(vs.h, routing_vs)
                else:
                    vs.wait(t)
                for ct in range(NCT):
                    last = t.matmul(pb[ct][:], sFc2b[:, ct * 128:(ct + 1) * 128],
                                    sm0Tm[:, t0:t0 + GRP], start=True, stop=True)
                ts.inc(last)

            @block.vector
            def _(v):
                ts.wait(v)
                for ct in range(NCT):
                    last = v.tensor_copy(sacc[:, ct * GRP:(ct + 1) * GRP], pb[ct][:])
                vs.inc(last)

            for u in range(NU):
                p = u % 2
                kind = unit_of(grp, u)

                @block.sync
                def _(sy_, grp=grp, u=u):
                    sy_.wait_ge(ts.h, _UN["ts_done"].get(u - 1, 0))
                    if u + 1 < NU:
                        emit_prefetch(sy_, grp, u + 1)
                    elif grp + 1 < NGRP:
                        emit_prefetch(sy_, grp + 1, 0)
                _UN["ws_ready"][u + 1] = ws.n

                if kind[0] == "ex" and kind[2] == 0:
                    ei = kind[1]
                    @block.tensor
                    def _(t, ei=ei, t0=t0):
                        vs.wait(t)
                        ts.inc(t.matmul(pb[6][:], SEL8[:, ei * 128:(ei + 1) * 128],
                                        sm0Tm[0:8, t0:t0 + GRP], start=True, stop=True))

                    @block.vector
                    def _(v):
                        ts.wait(v)
                        vs.inc(v.tensor_copy(smb[:], pb[6][:]))

                @block.tensor
                def _(t, u=u, p=p):
                    t.wait_ge(ws.h, _UN["ws_ready"].get(u, ws.n))
                    ss.wait(t)
                    if u == 0:
                        vs.wait(t)
                    for j in range(4):
                        mm_chain(t, pb[j][:],
                                 [(sW1[p][ci][:, j * 128:(j + 1) * 128], xg[ci][:])
                                  for ci in range(NCT)],
                                 ts if j == 3 else None)

                @block.scalar
                def _(s, kind=kind):
                    ts.wait(s)
                    vs.wait(s)
                    for j in range(4):
                        if kind[0] == "sh":
                            bcol = sSh1b[:, kind[1] * 4 + j:kind[1] * 4 + j + 1]
                        else:
                            idx = kind[1] * 16 + kind[2] * 4 + j
                            bcol = sFc1b[:, idx:idx + 1]
                        last = s.activation(sHd[:, j * 512:(j + 1) * 512], pb[j][:],
                                            AF.Gelu, bias=bcol)
                    ss.inc(last)

                if kind[0] == "ex":
                    @block.vector
                    def _(v):
                        ss.wait(v)
                        for j in range(4):
                            last = v.tensor_tensor(sHs[:, j * 512:(j + 1) * 512],
                                                   sHd[:, j * 512:(j + 1) * 512],
                                                   smb[:], ALU.mult)
                        vs.inc(last)
                    hsrc = sHs
                else:
                    hsrc = sHd

                for ct in range(NCT):
                    @block.tensor
                    def _(t, ct=ct, p=p, kind=kind, hsrc=hsrc, u_=u):
                        if ct == 0:
                            if kind[0] == "ex":
                                vs.wait(t)
                            else:
                                ss.wait(t)
                        if ct >= 2:
                            vs.wait(t)
                        mm_chain(t, pb[4 + ct % 2][:],
                                 [(sW2[p][j][:, ct * 128:(ct + 1) * 128],
                                   hsrc[:, j * 512:(j + 1) * 512]) for j in range(4)],
                                 ts)
                        if ct == NCT - 1:
                            _UN["ts_done"][u_] = ts.n

                    @block.vector
                    def _(v, ct=ct):
                        ts.wait(v)
                        vs.inc(v.tensor_tensor(sacc[:, ct * GRP:(ct + 1) * GRP],
                                               sacc[:, ct * GRP:(ct + 1) * GRP],
                                               pb[4 + ct % 2][:], ALU.add))

            for ct in range(NCT):
                @block.sync
                def _(sy_, ct=ct, t0=t0, grp=grp):
                    if not (grp == 0 and ct < 2):
                        vs.wait(sy_)
                    ys.inc(sy_.dma_start(
                        out=sxf[ct % 2][:],
                        in_=xT[ct * 128:(ct + 1) * 128, t0:t0 + GRP]), 16)

                @block.vector
                def _(v, ct=ct):
                    ys.wait(v)
                    vs.fence(v, v.tensor_scalar(sxo[:], sacc[:, ct * GRP:(ct + 1) * GRP],
                                                sSh2b[:, ct:ct + 1], None, ALU.add))
                    v.tensor_tensor(sacc[:, ct * GRP:(ct + 1) * GRP], sxo[:],
                                    sxf[ct % 2][:], ALU.add)
                    vs.inc(v.tensor_copy(sy_bf[:], sacc[:, ct * GRP:(ct + 1) * GRP]))

                @block.scalar
                def _(s, ct=ct):
                    vs.wait(s)
                    ss.inc(s.square(sy2[:], sy_bf[:]))

                @block.tensor
                def _(t, ct=ct):
                    ss.wait(t)
                    t.matmul(pb[0][0:1, :], ONES1024[:], sy_bf[:],
                             start=(ct == 0), stop=(ct == NCT - 1))
                    ts.inc(t.matmul(pb[1][0:1, :], ONES1024[:], sy2[:],
                                    start=(ct == 0), stop=(ct == NCT - 1)))

            @block.vector
            def _(v):
                ts.wait(v)
                vs.fence(v, v.tensor_copy(s_m[:], pb[0][0:1, :]))
                vs.fence(v, v.tensor_tensor(s_t1[:], s_m[:], s_m[:], ALU.mult))
                vs.inc(v.tensor_tensor(s_var[:], pb[1][0:1, :], s_t1[:], ALU.subtract))

            @block.scalar
            def _(s):
                vs.wait(s)
                s.activation(sln1[:], s_var[:], AF.Ln, bias=sEPS[0:1, 0:1])
                ss.inc(s.activation(s_r[:], sln1[:], AF.Exp, scale=-0.5))

            @block.vector
            def _(v):
                ss.wait(v)
                vs.inc(v.scalar_tensor_tensor(s_bt[:], s_m[:], -1.0, s_r[:],
                                              ALU.mult, ALU.mult))

            @block.tensor
            def _(t):
                vs.wait(t)
                t.matmul(pb[2][:], ONESROWF[:], s_r[:], start=True, stop=True)
                ts.inc(t.matmul(pb[3][:], ONESROWF[:], s_bt[:], start=True, stop=True))

            for ct in range(NCT):
                @block.vector
                def _(v, ct=ct, grp=grp):
                    if ct == 0:
                        ts.wait(v)
                    if not (grp == 0 and ct < 2):
                        ys.wait(v)
                    v.tensor_tensor(sxo[:], sacc[:, ct * GRP:(ct + 1) * GRP],
                                    pb[2][:], ALU.mult)
                    vs.inc(v.tensor_tensor(sxout[ct % 2][:], sxo[:], pb[3][:], ALU.add))

                @block.sync
                def _(sy_, ct=ct, t0=t0):
                    vs.wait(sy_)
                    ys.inc(sy_.dma_start(
                        out=x4T[ct * 128:(ct + 1) * 128, t0:t0 + GRP],
                        in_=sxout[ct % 2][:]), 16)

        @block.sync
        def _(sy_):
            ys.wait(sy_)


def _prep_ffn1_inputs(x3T_core, m0t_all, m0_core, ins):
    return {
        "f1xT": x3T_core,
        "m0all": None if m0t_all is None else np.ascontiguousarray(
            m0t_all.reshape(8, 8 * TOK)),
        "m0loc": m0_core,
        "fc1w": ins["L1_ffn_fc1_w"],
        "fc1b": ins["L1_ffn_fc1_b"].reshape(8, 16, 128, 1),
        "fc2w": ins["L1_ffn_fc2_w"],
        "fc2b": ins["L1_ffn_fc2_b"],
        "sh1w": ins["L1_ffn_sh1_w"],
        "sh1b": ins["L1_ffn_sh1_b"].reshape(32, 128, 1),
        "sh2w": ins["L1_ffn_sh2_w"],
        "sh2b": ins["L1_ffn_sh2_b"].reshape(8, 128, 1),
    }



LCAP = 10240
NLT = LCAP // 128
GTOK = 8 * TOK

def _mk_ffn1_bf16(nc, io):
    if "sh1bf" in io:
        return
    io["sh1bf"] = nc.dram_tensor("sh1bf", [4096, D], BF16, kind="Internal").ap()
    io["sh2bf"] = nc.dram_tensor("sh2bf", [D, 4096], BF16, kind="Internal").ap()
    io["myfc1bf"] = nc.dram_tensor("myfc1bf", [2048, D], BF16,
                                   kind="Internal").ap()
    io["myfc2bf"] = nc.dram_tensor("myfc2bf", [D, 2048], BF16,
                                   kind="Internal").ap()


FFN1G_IN = [("f1xT", [D, TOK]), ("myfc1", [2048, D]), ("myfc1b", [16, 128, 1]),
            ("myfc2", [D, 2048]), ("myfc2brow", [1, D]), ("cmask", [1, 8]),
            ("sh1w", [4096, D]), ("sh1b", [32, 128, 1]), ("sh2w", [D, 4096]),
            ("sh2b", [8, 128, 1])]


def _ffn1g_prepass(nc, S, io):
    gs = S["gs"]
    _mk_ffn1_bf16(nc, io)
    if "x3tmLoc" in io:
        return
    with ExitStack() as st:
        block = st.enter_context(nc.Block())

        @block.gpsimd
        def _(g):
            gs.inc(g.dma_start(out=io["myfc1bf"], in_=io["myfc1"]), 16)
            gs.inc(g.dma_start(out=io["myfc2bf"], in_=io["myfc2"]), 16)
            gs.inc(g.dma_start(out=io["sh1bf"], in_=io["sh1w"]), 16)
            gs.inc(g.dma_start(out=io["sh2bf"], in_=io["sh2w"]), 16)


def _stage_ffn1g(nc, S, io, phase=9):
    xT, sh1b, sh2b = io["f1xT"], io["sh1b"], io["sh2b"]
    m0all, x4T, x3bfT = io["m0all"], io["x4T"], io["x3bfT"]
    sh1bf, sh2bf = io["sh1bf"], io["sh2bf"]
    myfc1bf, myfc2bf = io["myfc1bf"], io["myfc2bf"]
    gs, gc, wh, ys, ts, vs, ss, cc = (S["gs"], S["gc"], S["wh"], S["ys"], S["ts"],
                                      S["vs"], S["ss"], S["cc"])
    cc_m0t = cc.n

    premoved = "x3tm_done" in io
    if premoved:
        x3tmLoc, yAll, elist = io["x3tmLoc"], io["yAll"], io["elist"]
    else:
        x3tmLoc = nc.dram_tensor("x3tmLoc", [TOK, D], BF16, kind="Internal").ap()
        elist = nc.dram_tensor("elist", [LCAP + 128 + 32768, 2], F32,
                               kind="Internal").ap()
        yAll = nc.dram_tensor("yAll", [GTOK + 128, D], BF16,
                              kind="Internal").ap()
    x3tmAll = nc.dram_tensor("x3tmAll", [GTOK, D], BF16, kind="Internal").ap()
    xeDram = nc.dram_tensor("xeDram", [LCAP, D], BF16, kind="Internal").ap()
    rsOut = nc.dram_tensor("rsOut", [TOK, D], BF16, kind="Internal").ap()
    shOutT = nc.dram_tensor("shOutT", [D, TOK], BF16, kind="Internal").ap()

    NGRP = TOK // GRP
    NIT = 37
    KPRE = 5

    nc = _NamePrefix(nc, "f1_")
    with ExitStack() as st:
        e = st.enter_context
        sF1T = [e(nc.sbuf_tensor(f"sF1T{i}", [128, 2048], BF16,
                                 is_transpose_dest=True)) for i in range(NCT)]
        sF2T = [e(nc.sbuf_tensor(f"sF2T{i}", [128, 1024], BF16,
                                 is_transpose_dest=True)) for i in range(16)]
        sGat = [e(nc.sbuf_tensor(f"sGat{i}", [128, 1024], BF16,
                                 is_transpose_dest=True)) for i in range(2)]
        sTp = sGat
        sW1 = [[e(nc.sbuf_tensor(f"sW1_{p}_{i}", [128, 512], BF16,
                                 is_transpose_dest=True))
                for i in range(NCT)] for p in range(2)]
        sW2 = [[e(nc.sbuf_tensor(f"sW2_{p}_{i}", [128, 1024], BF16,
                                 is_transpose_dest=True))
                for i in range(4)] for p in range(2)]
        sXe = [e(nc.sbuf_tensor(f"sXe{i}", [128, 512], BF16,
                                is_transpose_dest=True)) for i in range(NCT)]
        sXg = [[e(nc.sbuf_tensor(f"sXg{p}_{i}", [128, 512], BF16))
                for i in range(NCT)] for p in range(2)]
        sYout = [e(nc.sbuf_tensor(f"sYout{i}", [128, 1024], BF16))
                 for i in range(2)]
        sHd = e(nc.sbuf_tensor("sHd", [128, 4 * 512], BF16))
        sacc = e(nc.sbuf_tensor("sacc", [128, 8 * GRP], F32))
        sB2T = e(nc.sbuf_tensor("sB2T", [128, 1024], F32))
        sB2row = e(nc.sbuf_tensor("sB2row", [1, 1024], F32))
        sLstAll = e(nc.sbuf_tensor("sLstAll", [128, 2 * NLT], F32))
        sM01 = e(nc.sbuf_tensor("sM01", [128, NLT], F32))
        sIdT = e(nc.sbuf_tensor("sIdT", [128, NLT], F32))
        sIdxAll_i = e(nc.sbuf_tensor("sIdxAll_i", [128, NLT], I32))
        sIdG_i = e(nc.sbuf_tensor("sIdG_i", [128, NLT], I32))
        sF1bC = e(nc.sbuf_tensor("sF1bC", [128, 16], F32))
        sSh1b = e(nc.sbuf_tensor("sSh1b", [128, 32], F32))
        sSh2b = e(nc.sbuf_tensor("sSh2b", [128, 8], F32))
        sPcol = e(nc.sbuf_tensor("sPcol", [128, 1], F32))
        sGp = e(nc.sbuf_tensor("sGp", [128, 1], F32))
        scm = e(nc.sbuf_tensor("scm", [1, 8], F32))
        scm128 = e(nc.sbuf_tensor("scm128", [128, 8], F32))
        ONESROWF = e(nc.sbuf_tensor("ONESROWF", [1, 128], F32))
        ONESF = e(nc.sbuf_tensor("ONESF", [128, 1], F32))
        ONES1024 = e(nc.sbuf_tensor("ONES1024", [128, 1], BF16))
        sEPS = e(nc.sbuf_tensor("sEPS", [128, 1], F32))
        sxf = [e(nc.sbuf_tensor(f"sxf{i}", [128, 512], F32)) for i in range(2)]
        srs = [e(nc.sbuf_tensor(f"srs{i}", [128, 512], BF16,
                                is_transpose_dest=True)) for i in range(2)]
        sybuf = [e(nc.sbuf_tensor(f"sybuf{i}", [128, 512], BF16))
                 for i in range(2)]
        sy_bf = [e(nc.sbuf_tensor(f"sy_bf{i}", [128, 512], BF16))
                 for i in range(2)]
        sy2 = [e(nc.sbuf_tensor(f"sy2_{i}", [128, 512], BF16)) for i in range(2)]
        s_m = e(nc.sbuf_tensor("s_m", [1, 512], F32))
        s_t1 = e(nc.sbuf_tensor("s_t1", [1, 512], F32))
        s_var = e(nc.sbuf_tensor("s_var", [1, 512], F32))
        s_r = e(nc.sbuf_tensor("s_r", [1, 512], F32))
        s_bt = e(nc.sbuf_tensor("s_bt", [1, 512], F32))
        sln1 = e(nc.sbuf_tensor("sln1", [1, 512], F32))
        sxo = e(nc.sbuf_tensor("sxo", [128, 512], F32))
        sxout = [e(nc.sbuf_tensor(f"sxout{i}", [128, 512], F32)) for i in range(2)]
        pb = [e(nc.psum_tensor(f"pb{i}", [128, 512], F32)) for i in range(8)]
        sZeroV = sF1T[0][:, 0:2048]
        sYtmp = sxo

        block = e(nc.Block())

        @block.gpsimd
        def _(g):
            ys.wait(g)
            vs.wait(g)
            ss.wait(g)
            if not premoved:
                gc.inc(g.memset(sZeroV, 0.0), 1)
                g.wait_ge(gc.h, gc.n)
                for a in range(GTOK // 256):
                    gs.inc(g.dma_start(
                        out=yAll[a * 256:(a + 1) * 256, :].rearrange(
                            "(q p) c -> p q c", p=128),
                        in_=sZeroV.rearrange("p (q c) -> p q c", c=1024)), 16)
        zt = gs.n

        tp_ld, tp_st = {}, {}
        if not premoved:
            @block.sync
            def _(sy_):
                ys.wait(sy_)
                vs.wait(sy_)
                NT32 = TOK // 128
                for i in range(NT32):
                    if i >= 2:
                        sy_.wait_ge(ys.h, tp_st[i - 2])
                    wh.inc(sy_.dma_start(
                        out=sTp[i % 2][:],
                        in_=x3bfT[0:1024, i * 128:(i + 1) * 128],
                        transpose=True), 16)
                    tp_ld[i] = wh.n
                    if i >= 1:
                        sy_.wait_ge(wh.h, tp_ld[i - 1])
                        ys.inc(sy_.dma_start(
                            out=x3tmLoc[(i - 1) * 128:i * 128, :],
                            in_=sTp[(i - 1) % 2][:]), 16)
                        tp_st[i - 1] = ys.n
                sy_.wait_ge(wh.h, tp_ld[NT32 - 1])
                ys.inc(sy_.dma_start(
                    out=x3tmLoc[(NT32 - 1) * 128:NT32 * 128, :],
                    in_=sTp[(NT32 - 1) % 2][:]), 16)
                tp_st[NT32 - 1] = ys.n

        @block.sync
        def _(sy_):
            ys.wait(sy_)
            sy_.wait_ge(gs.h, zt)
            for ct in range(NCT):
                for hs in range(4):
                    wh.inc(sy_.dma_start(
                        out=sF1T[ct][:, hs * 512:(hs + 1) * 512],
                        in_=myfc1bf[hs * 512:(hs + 1) * 512,
                                    ct * 128:(ct + 1) * 128],
                        transpose=True), 16)
            for hj in range(16):
                wh.inc(sy_.dma_start(
                    out=sF2T[hj][:],
                    in_=myfc2bf[0:1024, hj * 128:(hj + 1) * 128],
                    transpose=True), 16)
            for j in range(32):
                ys.inc(sy_.dma_start(out=sSh1b[:, j:j + 1], in_=sh1b[j]), 16)
        bias_ld = ys.n

        def emit_prefetch(sy_, hs):
            p = hs % 2
            for ct in range(NCT):
                wh.inc(sy_.dma_start(
                    out=sW1[p][ct][:],
                    in_=sh1bf[hs * 512:(hs + 1) * 512, ct * 128:(ct + 1) * 128],
                    transpose=True), 16)
            for j in range(4):
                wh.inc(sy_.dma_start(
                    out=sW2[p][j][:],
                    in_=sh2bf[0:1024, hs * 512 + j * 128:hs * 512 + (j + 1) * 128],
                    transpose=True), 16)

        NU = 8
        wsr = {}
        xg_done = {}
        sh_hmm = {}
        sh_gelu = {}
        sh_fc2 = {}
        sh_evac = {}
        sh_store = {}
        sh_sdone = {}

        @block.sync
        def _(sy_):
            sy_.wait_ge(gs.h, zt)
            emit_prefetch(sy_, 0)
            for i in range(NCT):
                ys.inc(sy_.dma_start(
                    out=sXg[0][i][:],
                    in_=x3bfT[i * 128:(i + 1) * 128, 0:GRP]), 16)
        wsr[(0, 0)] = wh.n
        xg_done[0] = ys.n

        def emit_shared_group(grp):
            for u in range(NU):
                p = u % 2

                @block.sync
                def _(sy_, grp=grp, u=u):
                    tdone = sh_fc2.get((grp, u - 1, NCT - 1),
                                       sh_fc2.get((grp - 1, NU - 1, NCT - 1), 0))
                    sy_.wait_ge(ts.h, tdone)
                    if u + 1 < NU:
                        emit_prefetch(sy_, u + 1)
                        wsr[(grp, u + 1)] = wh.n
                    elif grp + 1 < NGRP:
                        emit_prefetch(sy_, 0)
                        wsr[(grp + 1, 0)] = wh.n
                    if u == 0 and grp + 1 < NGRP:
                        for i in range(NCT):
                            ys.inc(sy_.dma_start(
                                out=sXg[(grp + 1) % 2][i][:],
                                in_=x3bfT[i * 128:(i + 1) * 128,
                                          (grp + 1) * GRP:(grp + 2) * GRP]), 16)
                        xg_done[grp + 1] = ys.n

                @block.tensor
                def _(t, grp=grp, u=u, p=p):
                    t.wait_ge(wh.h, wsr[(grp, u)])
                    if u == 0:
                        t.wait_ge(ys.h, xg_done[grp])
                    gprev = sh_gelu.get((grp, u - 1),
                                        sh_gelu.get((grp - 1, NU - 1), 0))
                    if gprev:
                        t.wait_ge(ss.h, gprev)
                    for j in range(4):
                        mm_chain(t, pb[j][:],
                                 [(sW1[p][ci][:, j * 128:(j + 1) * 128],
                                   sXg[grp % 2][ci][:]) for ci in range(NCT)],
                                 ts if j == 3 else None)
                sh_hmm[(grp, u)] = ts.n

                @block.scalar
                def _(s, grp=grp, u=u):
                    s.wait_ge(ts.h, sh_hmm[(grp, u)])
                    if grp == 0 and u == 0:
                        s.wait_ge(ys.h, bias_ld)
                    f2prev = sh_fc2.get((grp, u - 1, NCT - 1),
                                        sh_fc2.get((grp - 1, NU - 1, NCT - 1), 0))
                    if f2prev:
                        s.wait_ge(ts.h, f2prev)
                    for j in range(4):
                        last = s.activation(sHd[:, j * 512:(j + 1) * 512],
                                            pb[j][:], AF.Gelu,
                                            bias=sSh1b[:, u * 4 + j:u * 4 + j + 1])
                    ss.inc(last)
                sh_gelu[(grp, u)] = ss.n

                for ct in range(NCT):
                    @block.tensor
                    def _(t, grp=grp, u=u, ct=ct, p=p):
                        if ct == 0:
                            t.wait_ge(ss.h, sh_gelu[(grp, u)])
                        ev = sh_evac.get((grp, u, ct - 2),
                                         sh_evac.get((grp, u - 1, NCT - 2 + ct),
                                                     sh_evac.get((grp - 1, NU - 1,
                                                                  NCT - 2 + ct), 0)))
                        if ev:
                            t.wait_ge(vs.h, ev)
                        mm_chain(t, pb[4 + ct % 2][:],
                                 [(sW2[p][j][:, ct * 128:(ct + 1) * 128],
                                   sHd[:, j * 512:(j + 1) * 512])
                                  for j in range(4)], ts)
                    sh_fc2[(grp, u, ct)] = ts.n

                    @block.vector
                    def _(v, grp=grp, u=u, ct=ct):
                        v.wait_ge(ts.h, sh_fc2[(grp, u, ct)])
                        sl = slice(ct * GRP, (ct + 1) * GRP)
                        if u == 0:
                            vs.inc(v.tensor_copy(sacc[:, sl], pb[4 + ct % 2][:]))
                        else:
                            vs.inc(v.tensor_tensor(sacc[:, sl], sacc[:, sl],
                                                   pb[4 + ct % 2][:], ALU.add))
                    sh_evac[(grp, u, ct)] = vs.n

            for ct in range(NCT):
                @block.vector
                def _(v, grp=grp, ct=ct):
                    pv = sh_store.get((grp, ct - 2),
                                      sh_store.get((grp - 1, NCT - 2 + ct)))
                    if pv:
                        v.wait_ge(ys.h, pv)
                    vs.inc(v.tensor_copy(sybuf[ct % 2][:],
                                         sacc[:, ct * GRP:(ct + 1) * GRP]))
                stv = vs.n

                @block.sync
                def _(sy_, grp=grp, ct=ct, stv=stv):
                    sy_.wait_ge(vs.h, stv)
                    ys.inc(sy_.dma_start(
                        out=shOutT[ct * 128:(ct + 1) * 128,
                                   grp * GRP:(grp + 1) * GRP],
                        in_=sybuf[ct % 2][:]), 16)
                sh_store[(grp, ct)] = ys.n
            sh_sdone[grp] = vs.n

        for grp in range(KPRE):
            emit_shared_group(grp)

        if phase < 2:
            @block.sync
            def _(sy_):
                ys.wait(sy_)
            return
        with ExitStack() as stA:
            eA = stA.enter_context
            sMe = eA(nc.sbuf_tensor("sMe", [128, 8 * 256], F32))
            sjunk = eA(nc.sbuf_tensor("sjunk", [128, 256], F32))
            sids = eA(nc.sbuf_tensor("sids", [128, 256], F32))
            szer = eA(nc.sbuf_tensor("szer", [128, 256], F32))
            smask = eA(nc.sbuf_tensor("smask", [128, 256], F32))
            smsel = eA(nc.sbuf_tensor("smsel", [128, 256], F32))
            stmp = eA(nc.sbuf_tensor("stmp", [128, 256], F32))
            spairs = eA(nc.sbuf_tensor("spairs", [128, 512], F32))
            sslot_i = eA(nc.sbuf_tensor("sslot_i", [128, 256], I32))
            sLT = eA(nc.sbuf_tensor("sLT", [128, 128], F32))
            siota2 = eA(nc.sbuf_tensor("siota2", [128, 128], F32))
            sbase = eA(nc.sbuf_tensor("sbase", [128, 1], F32))
            scnt = eA(nc.sbuf_tensor("scnt", [128, 8], F32))
            sth = eA(nc.sbuf_tensor("sth", [1, 8], F32))
            sthk = eA(nc.sbuf_tensor("sthk", [1, 8], F32))
            sgt = eA(nc.sbuf_tensor("sgt", [1, 8], F32))
            sok = eA(nc.sbuf_tensor("sok", [1, 8], F32))
            st1 = eA(nc.sbuf_tensor("st1", [1, 8], F32))
            sthc = eA(nc.sbuf_tensor("sthc", [128, 8], F32))
            st8 = eA(nc.sbuf_tensor("st8", [128, 8], F32))
            sthv = eA(nc.sbuf_tensor("sthv", [128, 1], F32))

            @block.gpsimd
            def _(g):
                g.wait_ge(cc.h, cc_m0t)
                for ei in range(8):
                    for r in range(8):
                        gs.inc(g.dma_start(
                            out=sMe[:, ei * 256:(ei + 1) * 256][
                                r * 16:(r + 1) * 16, :],
                            in_=m0all[r * 8 + ei].rearrange("(p n) -> p n", p=16)),
                            16)
                for hj in range(16):
                    gs.inc(g.dma_start(out=sF1bC[:, hj:hj + 1],
                                       in_=io["myfc1b"][hj]), 16)
                for j in range(8):
                    gs.inc(g.dma_start(out=sSh2b[:, j:j + 1], in_=sh2b[j]), 16)
                gs.inc(g.dma_start(out=sB2row[:], in_=io["myfc2brow"]), 16)
                gs.inc(g.dma_start(out=scm[:], in_=io["cmask"]), 16)
                gc.inc(g.iota(sids[:], [[1, 256]], base=0, channel_multiplier=256,
                              allow_small_or_imprecise_dtypes=True), 1)
                gc.inc(g.iota(siota2[:], [[1, 128]], base=0, channel_multiplier=-1,
                              allow_small_or_imprecise_dtypes=True), 1)
                gc.inc(g.iota(sPcol[:], [[0, 1]], base=0, channel_multiplier=1,
                              allow_small_or_imprecise_dtypes=True), 1)
            ld_gs = gs.n

            @block.gpsimd
            def _(g):
                if premoved:
                    g.wait_ge(ys.h, io["x3tm_done"])
                else:
                    g.wait_ge(ys.h, tp_st[TOK // 128 - 1])
                cc.inc(g.collective_compute(
                    "AllGather", ALU.bypass, replica_groups=[list(range(8))],
                    ins=[x3tmLoc.opt()], outs=[x3tmAll.opt()]), 1)
            ag2 = cc.n

            @block.vector
            def _(v):
                v.wait_ge(gs.h, ld_gs)
                gc.wait(v)
                v.tensor_scalar(sLT[:], siota2[:], 0.0, None, ALU.is_gt)
                v.memset(ONESROWF[:], 1.0)
                v.memset(ONESF[:], 1.0)
                v.memset(ONES1024[:], 1.0 / 1024)
                v.memset(sEPS[:], EPS)
                v.memset(sth[:], 0.5)
                v.memset(sthk[:], 1.1)
                v.memset(szer[:], 0.0)
                vs.inc(v.memset(spairs[:], 0.0))

            if not premoved:
                @block.sync
                def _(sy_):
                    vs.wait(sy_)
                    ys.inc(sy_.dma_start(
                        out=elist[0:LCAP + 128, :].rearrange(
                            "(p q) w -> p (q w)", p=128),
                        in_=spairs[:, 0:2 * (LCAP + 128) // 128]), 16)
            pf = ys.n

            @block.tensor
            def _(t):
                vs.wait(t)
                t.matmul(pb[6][:, 0:8], ONESROWF[:], scm[:], start=True, stop=True)
                ts.inc(t.matmul(pb[7][:], ONESROWF[:], sB2row[0:1, 0:512],
                                start=True, stop=True))

            @block.vector
            def _(v):
                ts.wait(v)
                vs.fence(v, v.tensor_copy(scm128[:], pb[6][:, 0:8]))
                vs.inc(v.tensor_copy(sB2T[:, 0:512], pb[7][:]))

            @block.tensor
            def _(t):
                vs.wait(t)
                ts.inc(t.matmul(pb[6][:], ONESROWF[:], sB2row[0:1, 512:1024],
                                start=True, stop=True))

            @block.vector
            def _(v):
                ts.wait(v)
                vs.inc(v.tensor_copy(sB2T[:, 512:1024], pb[6][:]))

            dk = 0.25
            for it in range(NIT):
                @block.vector
                def _(v, it=it):
                    if it == 0:
                        vs.wait(v)
                    else:
                        ts.wait(v)
                        vs.fence(v, v.tensor_copy(sthc[:], pb[7][:, 0:8]))
                    for ei in range(8):
                        last = v.tensor_scalar(
                            sjunk[:], sMe[:, ei * 256:(ei + 1) * 256],
                            sthc[:, ei:ei + 1] if it > 0 else 0.5,
                            0.0, ALU.is_ge, ALU.add,
                            accum_out=scnt[:, ei:ei + 1])
                    vs.inc(last)

                @block.tensor
                def _(t):
                    vs.wait(t)
                    ts.inc(t.matmul(pb[6][0:1, 0:8], ONESF[:], scnt[:],
                                    start=True, stop=True))

                @block.vector
                def _(v, it=it):
                    ts.wait(v)
                    vs.fence(v, v.tensor_scalar(sgt[:], pb[6][0:1, 0:8], CAP,
                                                None, ALU.is_gt))
                    vs.fence(v, v.tensor_scalar(sok[:], sgt[:], -1.0, 1.0,
                                                ALU.mult, ALU.add))
                    vs.fence(v, v.tensor_tensor(st1[:], sok[:], sth[:], ALU.mult))
                    vs.fence(v, v.tensor_tensor(sthk[:], sgt[:], sthk[:], ALU.mult))
                    vs.fence(v, v.tensor_tensor(sthk[:], sthk[:], st1[:], ALU.add))
                    d = dk / 2
                    vs.fence(v, v.scalar_tensor_tensor(st1[:], sgt[:], 2.0 * d,
                                                       sth[:], ALU.mult, ALU.add))
                    vs.inc(v.tensor_scalar(sth[:], st1[:], -d, None, ALU.add))
                dk = dk / 2

                @block.tensor
                def _(t, it=it):
                    vs.wait(t)
                    src = sthk if it == NIT - 1 else sth
                    ts.inc(t.matmul(pb[7][:, 0:8], ONESROWF[:], src[:],
                                    start=True, stop=True))

            @block.vector
            def _(v):
                ts.wait(v)
                vs.fence(v, v.tensor_copy(sthc[:], pb[7][:, 0:8]))
                vs.fence(v, v.tensor_tensor(st8[:], sthc[:], scm128[:], ALU.mult))
                vs.fence(v, v.reduce_sum(sthv[:], st8[:],
                                         axis=mybir.AxisListType.X))
                for ei in range(8):
                    if ei == 0:
                        vs.fence(v, v.tensor_scalar(
                            smsel[:], sMe[:, 0:256], scm128[:, 0:1], None,
                            ALU.mult))
                    else:
                        vs.fence(v, v.tensor_scalar(
                            stmp[:], sMe[:, ei * 256:(ei + 1) * 256],
                            scm128[:, ei:ei + 1], None, ALU.mult))
                        vs.fence(v, v.tensor_tensor(smsel[:], smsel[:], stmp[:],
                                                    ALU.add))
                vs.fence(v, v.tensor_scalar(smask[:], smsel[:], sthv[:, 0:1],
                                            None, ALU.is_ge))
                vs.fence(v, v.tensor_tensor(smsel[:], smsel[:], smask[:],
                                            ALU.mult))
                vs.inc(v.tensor_tensor_scan(sjunk[:], smask[:], szer[:], 0.0,
                                            ALU.add, ALU.add))

            @block.tensor
            def _(t):
                vs.wait(t)
                ts.inc(t.matmul(pb[6][:, 0:1], sLT[:], sjunk[:, 255:256],
                                start=True, stop=True))

            @block.vector
            def _(v):
                ts.wait(v)
                vs.fence(v, v.tensor_copy(sbase[:], pb[6][:, 0:1]))
                vs.fence(v, v.tensor_tensor(sjunk[:], sjunk[:], smask[:],
                                            ALU.subtract))
                vs.fence(v, v.tensor_scalar(sjunk[:], sjunk[:], sbase[:, 0:1],
                                            1.0, ALU.add, ALU.add))
                vs.fence(v, v.tensor_scalar(stmp[:], sids[:], float(LCAP + 1),
                                            None, ALU.add))
                vs.fence(v, v.tensor_tensor(sjunk[:], sjunk[:], stmp[:],
                                            ALU.subtract))
                vs.fence(v, v.tensor_tensor(sjunk[:], sjunk[:], smask[:],
                                            ALU.mult))
                vs.fence(v, v.tensor_tensor(sjunk[:], sjunk[:], stmp[:],
                                            ALU.add))
                vs.fence(v, v.tensor_copy(sslot_i[:], sjunk[:]))
                v.wait_ge(ys.h, pf)
                pview = spairs[:].rearrange("p (n w) -> p n w", w=2)
                vs.fence(v, v.tensor_copy(
                    pview[:, :, 0:1],
                    sids[:].rearrange("p (n o) -> p n o", o=1)))
                vs.inc(v.tensor_copy(
                    pview[:, :, 1:2],
                    smsel[:].rearrange("p (n o) -> p n o", o=1)))

            @block.gpsimd
            def _(g):
                vs.wait(g)
                for n in range(256):
                    gs.inc(g.indirect_dma_start(
                        out=elist,
                        out_offset=bass.IndirectOffsetOnAxis(sslot_i[:, n:n + 1], 0),
                        in_=spairs[:, 2 * n:2 * n + 2], in_offset=None), 16)
            sc256 = gs.n

            @block.gpsimd
            def _(g):
                g.wait_ge(gs.h, sc256)
                for t in range(NLT):
                    gs.inc(g.dma_start(
                        out=sLstAll[:, 2 * t:2 * t + 2],
                        in_=elist[1 + 128 * t:1 + 128 * (t + 1), :]), 16)
            lst = gs.n

            @block.vector
            def _(v):
                v.wait_ge(gs.h, lst)
                lv = sLstAll[:].rearrange("p (t w) -> p t w", w=2)
                m3 = sM01[:].rearrange("p (t o) -> p t o", o=1)
                i3 = sIdT[:].rearrange("p (t o) -> p t o", o=1)
                vs.fence(v, v.tensor_scalar(m3, lv[:, :, 1:2], 0.0, None,
                                            ALU.is_gt))
                vs.fence(v, v.tensor_scalar(sGp[:], sPcol[:], float(GTOK), None,
                                            ALU.add))
                vs.fence(v, v.tensor_scalar(i3, lv[:, :, 0:1], sGp[:, 0:1], None,
                                            ALU.subtract))
                vs.fence(v, v.tensor_tensor(sIdT[:], sIdT[:], sM01[:], ALU.mult))
                vs.fence(v, v.tensor_scalar(sIdT[:], sIdT[:], sGp[:, 0:1], None,
                                            ALU.add))
                vs.fence(v, v.tensor_copy(sIdxAll_i[:], sIdT[:]))
                vs.inc(v.tensor_copy(
                    sIdG_i[:].rearrange("p (t o) -> p t o", o=1), lv[:, :, 0:1]))
            idxv = vs.n

        if phase < 3:
            @block.sync
            def _(sy_):
                ys.wait(sy_)
            return
        sHe = e(nc.sbuf_tensor("sHe", [128, 16 * 512], BF16))

        gat, xe, xst = {}, {}, {}
        fc1mm, gelu_ss, fc2ts, evac_vs, sc_done = {}, {}, {}, {}, {}

        def emit_expert_blk(blk):
            for pr in range(2):
                @block.gpsimd
                def _(g, blk=blk, pr=pr):
                    if blk == 0 and pr == 0:
                        g.wait_ge(cc.h, ag2)
                        g.wait_ge(vs.h, idxv)
                    for t in range(4 * blk + 2 * pr, 4 * blk + 2 * pr + 2):
                        if t >= 2:
                            g.wait_ge(ys.h, xst[t - 2])
                        gs.inc(g.indirect_dma_start(
                            out=sGat[t % 2][:], out_offset=None, in_=x3tmAll,
                            in_offset=bass.IndirectOffsetOnAxis(
                                sIdG_i[:, t:t + 1], 0)), 16)
                        gat[t] = gs.n

                @block.sync
                def _(sy_, blk=blk, pr=pr):
                    for t in range(4 * blk + 2 * pr, 4 * blk + 2 * pr + 2):
                        sy_.wait_ge(gs.h, gat[t])
                        ys.inc(sy_.dma_start(
                            out=xeDram[t * 128:(t + 1) * 128, :],
                            in_=sGat[t % 2][:]), 16)
                        xst[t] = ys.n

            @block.sync
            def _(sy_, blk=blk):
                sy_.wait_ge(ys.h, xst[4 * blk + 3])
                if blk >= 1:
                    sy_.wait_ge(ts.h, fc1mm[(blk - 1, 15)])
                for ct in range(NCT):
                    wh.inc(sy_.dma_start(
                        out=sXe[ct][:],
                        in_=xeDram[blk * 512:(blk + 1) * 512,
                                   ct * 128:(ct + 1) * 128], transpose=True), 16)
                xe[blk] = wh.n

            for hj in range(16):
                @block.tensor
                def _(t, blk=blk, hj=hj):
                    if hj == 0:
                        t.wait_ge(wh.h, xe[blk])
                        pv = evac_vs.get(4 * blk - 1)
                        if pv:
                            t.wait_ge(vs.h, pv)
                    g2 = gelu_ss.get((blk, hj - 2),
                                     gelu_ss.get((blk - 1, 14 + hj)))
                    if g2:
                        t.wait_ge(ss.h, g2)
                    mm_chain(t, pb[6 + hj % 2][:],
                             [(sF1T[ct][:, hj * 128:(hj + 1) * 128],
                               sXe[ct][:]) for ct in range(NCT)], ts)
                fc1mm[(blk, hj)] = ts.n

                @block.scalar
                def _(s, blk=blk, hj=hj):
                    s.wait_ge(ts.h, fc1mm[(blk, hj)])
                    ss.inc(s.activation(sHe[:, hj * 512:(hj + 1) * 512],
                                        pb[6 + hj % 2][:], AF.Gelu,
                                        bias=sF1bC[:, hj:hj + 1]))
                gelu_ss[(blk, hj)] = ss.n

            for tsl in range(4):
                tg = 4 * blk + tsl

                @block.tensor
                def _(t, blk=blk, tsl=tsl, tg=tg):
                    if tsl == 0:
                        t.wait_ge(ss.h, gelu_ss[(blk, 15)])
                    pv = evac_vs.get(tg - 1)
                    if pv:
                        t.wait_ge(vs.h, pv)
                    for half in range(2):
                        mm_chain(t, pb[6 + half][:],
                                 [(sHe[:, hj * 512 + tsl * 128:
                                       hj * 512 + tsl * 128 + 128],
                                   sF2T[hj][:, half * 512:(half + 1) * 512])
                                  for hj in range(16)],
                                 ts if half == 1 else None)
                fc2ts[tg] = ts.n

                @block.vector
                def _(v, tg=tg):
                    v.wait_ge(ts.h, fc2ts[tg])
                    pg = sc_done.get(tg - 2)
                    if pg:
                        v.wait_ge(gs.h, pg)
                    w_ap = sLstAll[:, 2 * tg + 1:2 * tg + 2]
                    vs.fence(v, v.tensor_tensor(sYtmp[:], pb[6][:],
                                                sB2T[:, 0:512], ALU.add))
                    vs.fence(v, v.tensor_scalar(sYout[tg % 2][:, 0:512],
                                                sYtmp[:], w_ap, None, ALU.mult))
                    vs.fence(v, v.tensor_tensor(sYtmp[:], pb[7][:],
                                                sB2T[:, 512:1024], ALU.add))
                    vs.inc(v.tensor_scalar(sYout[tg % 2][:, 512:1024], sYtmp[:],
                                           w_ap, None, ALU.mult))
                evac_vs[tg] = vs.n

                @block.gpsimd
                def _(g, tg=tg):
                    g.wait_ge(vs.h, evac_vs[tg])
                    gs.inc(g.indirect_dma_start(
                        out=yAll,
                        out_offset=bass.IndirectOffsetOnAxis(
                            sIdxAll_i[:, tg:tg + 1], 0),
                        in_=sYout[tg % 2][:], in_offset=None), 16)
                sc_done[tg] = gs.n

        for blk in range(NLT // 4):
            emit_expert_blk(blk)

        @block.gpsimd
        def _(g):
            g.wait_ge(gs.h, sc_done[NLT - 1])
            cc.inc(g.collective_compute(
                "ReduceScatter", ALU.add, replica_groups=[list(range(8))],
                ins=[yAll[0:GTOK, :].opt()], outs=[rsOut.opt()]), 1)
        rs_done = cc.n
        rs_half = cc.n

        for k in range(KPRE, NGRP):
            emit_shared_group(k)

        if "rsDbg" in io:
            @block.gpsimd
            def _(g):
                g.wait_ge(cc.h, rs_done)
                gs.inc(g.dma_start(out=io["rsDbg"], in_=rsOut), 16)
                gs.inc(g.dma_start(out=io["yDbg"], in_=yAll), 16)

        ld_done, rsadd_vs, stats_ts = {}, {}, {}
        for grp in range(NGRP):
            t0 = grp * GRP
            for ct in range(NCT):
                @block.sync
                def _(sy_, ct=ct, t0=t0, grp=grp):
                    if grp == 0 and ct == 0:
                        sy_.wait_ge(cc.h, rs_half)
                    if grp == 0 and ct == NCT // 2:
                        sy_.wait_ge(cc.h, rs_done)
                    pv = rsadd_vs.get((grp, ct - 2),
                                      rsadd_vs.get((grp - 1, NCT - 2 + ct)))
                    if pv:
                        sy_.wait_ge(vs.h, pv)
                    ys.inc(sy_.dma_start(
                        out=sybuf[ct % 2][:],
                        in_=shOutT[ct * 128:(ct + 1) * 128, t0:t0 + GRP]), 16)
                    ys.inc(sy_.dma_start(
                        out=sxf[ct % 2][:],
                        in_=xT[ct * 128:(ct + 1) * 128, t0:t0 + GRP]), 16)
                    ys.inc(sy_.dma_start(
                        out=srs[ct % 2][:],
                        in_=rsOut[t0:t0 + GRP, ct * 128:(ct + 1) * 128],
                        transpose=True), 16)
                ld_done[(grp, ct)] = ys.n

                @block.vector
                def _(v, ct=ct, grp=grp):
                    v.wait_ge(ys.h, ld_done[(grp, ct)])
                    st_prev = stats_ts.get((grp, ct - 2),
                                           stats_ts.get((grp - 1, NCT - 2 + ct)))
                    if st_prev:
                        v.wait_ge(ts.h, st_prev)
                    sl = slice(ct * GRP, (ct + 1) * GRP)
                    vs.fence(v, v.tensor_tensor(sacc[:, sl], sybuf[ct % 2][:],
                                                srs[ct % 2][:], ALU.add))
                    vs.fence(v, v.tensor_scalar(sxo[:], sacc[:, sl],
                                                sSh2b[:, ct:ct + 1], None,
                                                ALU.add))
                    v.tensor_tensor(sacc[:, sl], sxo[:], sxf[ct % 2][:], ALU.add)
                    vs.inc(v.tensor_copy(sy_bf[ct % 2][:], sacc[:, sl]))
                rsadd_vs[(grp, ct)] = vs.n

                @block.scalar
                def _(s, ct=ct):
                    vs.wait(s)
                    ss.inc(s.square(sy2[ct % 2][:], sy_bf[ct % 2][:]))

                @block.tensor
                def _(t, ct=ct, grp=grp):
                    ss.wait(t)
                    t.matmul(pb[0][0:1, :], ONES1024[:], sy_bf[ct % 2][:],
                             start=(ct == 0), stop=(ct == NCT - 1))
                    ts.inc(t.matmul(pb[1][0:1, :], ONES1024[:], sy2[ct % 2][:],
                                    start=(ct == 0), stop=(ct == NCT - 1)))
                stats_ts[(grp, ct)] = ts.n

            @block.vector
            def _(v):
                ts.wait(v)
                vs.fence(v, v.tensor_copy(s_m[:], pb[0][0:1, :]))
                vs.fence(v, v.tensor_tensor(s_t1[:], s_m[:], s_m[:], ALU.mult))
                vs.inc(v.tensor_tensor(s_var[:], pb[1][0:1, :], s_t1[:],
                                       ALU.subtract))

            @block.scalar
            def _(s):
                vs.wait(s)
                s.activation(sln1[:], s_var[:], AF.Ln, bias=sEPS[0:1, 0:1])
                ss.inc(s.activation(s_r[:], sln1[:], AF.Exp, scale=-0.5))

            @block.vector
            def _(v):
                ss.wait(v)
                vs.inc(v.scalar_tensor_tensor(s_bt[:], s_m[:], -1.0, s_r[:],
                                              ALU.mult, ALU.mult))

            @block.tensor
            def _(t):
                vs.wait(t)
                t.matmul(pb[2][:], ONESROWF[:], s_r[:], start=True, stop=True)
                ts.inc(t.matmul(pb[3][:], ONESROWF[:], s_bt[:], start=True,
                                stop=True))

            for ct in range(NCT):
                @block.vector
                def _(v, ct=ct, grp=grp):
                    if ct == 0:
                        ts.wait(v)
                    if not (grp == 0 and ct < 2):
                        ys.wait(v)
                    v.tensor_tensor(sxo[:], sacc[:, ct * GRP:(ct + 1) * GRP],
                                    pb[2][:], ALU.mult)
                    vs.inc(v.tensor_tensor(sxout[ct % 2][:], sxo[:], pb[3][:],
                                           ALU.add))

                @block.sync
                def _(sy_, ct=ct, t0=t0):
                    vs.wait(sy_)
                    ys.inc(sy_.dma_start(
                        out=x4T[ct * 128:(ct + 1) * 128, t0:t0 + GRP],
                        in_=sxout[ct % 2][:]), 16)

        @block.sync
        def _(sy_):
            ys.wait(sy_)


def build_ffn1g():
    nc = bass.Bass("TRN2", target_bir_lowering=False, detect_race_conditions=False)
    io = {n: nc.dram_tensor(n, sh, F32, kind="ExternalInput").ap()
          for n, sh in FFN1G_IN}
    io["m0all"] = nc.dram_tensor("m0all", [64, TOK], F32, kind="ExternalInput").ap()
    io["x3bfT"] = nc.dram_tensor("x3bfT", [D, TOK], BF16,
                                 kind="ExternalInput").ap()
    io["x4T"] = nc.dram_tensor("x4T", [D, TOK], F32, kind="ExternalOutput").ap()
    with ExitStack() as st0:
        S = make_sems(nc, st0)
        _ffn1g_prepass(nc, S, io)
        _stage_ffn1g(nc, S, io)
    return nc


def _prep_ffn1g_inputs(core, ins):
    return {
        "myfc1": ins["L1_ffn_fc1_w"][core],
        "myfc1b": ins["L1_ffn_fc1_b"][core].reshape(16, 128, 1),
        "myfc2": ins["L1_ffn_fc2_w"][core],
        "myfc2brow": ins["L1_ffn_fc2_b"][core].reshape(1, D),
        "cmask": np.eye(8, dtype=np.float32)[core].reshape(1, 8),
        "sh1w": ins["L1_ffn_sh1_w"],
        "sh1b": ins["L1_ffn_sh1_b"].reshape(32, 128, 1),
        "sh2w": ins["L1_ffn_sh2_w"],
        "sh2b": ins["L1_ffn_sh2_b"].reshape(8, 128, 1),
    }


def _stage_collective(nc, S, m0t_loc, ccout):
    with ExitStack() as st:
        block = st.enter_context(nc.Block())

        @block.gpsimd
        def _(g):
            S["ys"].wait(g)
            S["cc"].inc(g.collective_compute(
                "AllGather", ALU.bypass, replica_groups=[list(range(8))],
                ins=[m0t_loc.opt()], outs=[ccout.opt()]), 1)


def build_fused():
    nc = bass.Bass("TRN2", target_bir_lowering=False, detect_race_conditions=False)
    io = {}
    for n, sh in GLA_IN + FFN0_IN[1:] + MHA_IN[1:] + FFN1G_IN[1:]:
        io[n] = nc.dram_tensor(n, sh, F32, kind="ExternalInput").ap()
    x1T = nc.dram_tensor("x1T", [D, TOK], F32, kind="Internal").ap()
    x2T = nc.dram_tensor("x2T", [D, TOK], F32, kind="Internal").ap()
    x3T = nc.dram_tensor("x3T", [D, TOK], F32, kind="Internal").ap()
    m0 = nc.dram_tensor("m0", [TOK, 8], F32, kind="Internal").ap()
    m0t = nc.dram_tensor("m0t", [8, TOK], F32, kind="Internal").ap()
    ccout = nc.dram_tensor("ccout", [64, TOK], F32, kind="Internal").ap()
    x3bfT = nc.dram_tensor("x3bfT", [D, TOK], BF16, kind="Internal").ap()
    io["x3bfT"] = x3bfT
    io["x3tmLoc"] = nc.dram_tensor("x3tmLoc", [TOK, D], BF16,
                                   kind="Internal").ap()
    io["yAll"] = nc.dram_tensor("yAll", [GTOK + 128, D], BF16,
                                kind="Internal").ap()
    io["elist"] = nc.dram_tensor("elist", [LCAP + 128 + 32768, 2], F32,
                                 kind="Internal").ap()
    _mk_ffn1_bf16(nc, io)
    io["x1T"] = x1T
    io["f0xT"] = x1T
    io["x2T"] = x2T
    io["mxT"] = x2T
    io["x3T"] = x3T
    io["f1xT"] = x3T
    io["m0"] = m0
    io["m0t"] = m0t
    io["m0all"] = ccout
    io["x4T"] = nc.dram_tensor("x4T", [D, TOK], F32, kind="ExternalOutput").ap()
    with ExitStack() as st0:
        S = make_sems(nc, st0)
        _stage_gla(nc, S, io)
        _stage_ffn0(nc, S, io)
        _stage_mha(nc, S, io)
        _ffn1g_prepass(nc, S, io)
        _stage_collective(nc, S, m0t, ccout)
        _stage_ffn1g(nc, S, io)
    return nc


def _prep_fused(x_shard, core, ins):
    d = {}
    d.update(_prep_gla_inputs(x_shard, ins))
    d.update({k: v for k, v in _prep_ffn0_inputs(None, ins).items() if k != "f0xT"})
    d.update({k: v for k, v in _prep_mha_inputs(None, ins).items() if k != "mxT"})
    d.update(_prep_ffn1g_inputs(core, ins))
    return d


def _to_xT(x_shard):
    return np.ascontiguousarray(
        np.concatenate([x_shard[0].T, x_shard[1].T], axis=1)).astype(np.float32)


def _from_xT(xT):
    return np.stack([xT[:, 0:TB].T, xT[:, TB:2 * TB].T])


def kernel(**inputs):
    ins = inputs
    x = np.asarray(ins["L0_attn_x"], dtype=np.float32)

    nc = build_fused()
    in_maps = []
    for c in range(8):
        d = _prep_fused(x[2 * c:2 * c + 2], c, ins)
        in_maps.append(d)
    res = run_bass_kernel_spmd(nc, in_maps, core_ids=list(range(8)))
    out = np.concatenate([_from_xT(np.asarray(res.results[c]["x4T"])) for c in range(8)],
                         axis=0)
    return np.ascontiguousarray(out).astype(np.float32)


def kernel_unfused(**inputs):
    ins = inputs
    x = np.asarray(ins["L0_attn_x"], dtype=np.float32)

    nc = build_gla()
    in_maps = [_prep_gla_inputs(x[2 * c:2 * c + 2], ins) for c in range(8)]
    res = run_bass_kernel_spmd(nc, in_maps, core_ids=list(range(8)))
    x1T = [np.asarray(res.results[c]["x1T"]) for c in range(8)]

    nc = build_ffn0()
    in_maps = [_prep_ffn0_inputs(x1T[c], ins) for c in range(8)]
    res = run_bass_kernel_spmd(nc, in_maps, core_ids=list(range(8)))
    x2T = [np.asarray(res.results[c]["x2T"]) for c in range(8)]

    nc = build_mha()
    in_maps = [_prep_mha_inputs(x2T[c], ins) for c in range(8)]
    res = run_bass_kernel_spmd(nc, in_maps, core_ids=list(range(8)))
    x3T = [np.asarray(res.results[c]["x3T"]) for c in range(8)]
    m0 = [np.asarray(res.results[c]["m0"]) for c in range(8)]
    m0t_all = np.ascontiguousarray(
        np.concatenate([np.asarray(res.results[c]["m0t"]) for c in range(8)], axis=1))

    nc = build_ffn1()
    in_maps = [_prep_ffn1_inputs(x3T[c], m0t_all, m0[c], ins) for c in range(8)]
    res = run_bass_kernel_spmd(nc, in_maps, core_ids=list(range(8)))
    out = np.concatenate([_from_xT(np.asarray(res.results[c]["x4T"])) for c in range(8)],
                         axis=0)
    return np.ascontiguousarray(out).astype(np.float32)
